# version 10 (final)
# baseline (speedup 1.0000x reference)
import numpy as np
from contextlib import ExitStack

import concourse.bass as bass
import concourse.mybir as mybir
from concourse.bass_utils import run_bass_kernel_spmd

F32 = mybir.dt.float32
BF16 = mybir.dt.bfloat16
I8 = mybir.dt.int8
I32 = mybir.dt.int32
AF = mybir.ActivationFunctionType
ALU = mybir.AluOpType
AX = mybir.AxisListType.X

P = 128
T = 4096
D = 1024
NCH = D // P
WIN = 512
NW = T // WIN
NTT = T // P
SEG = 32
NSEG = T // SEG
SPW = WIN // SEG
HEADS = 16
DH = 64
R23 = float(1.5 * 2.0 ** 23)
PI = 3.14159265358979323846
TWO_PI_HI = 6.28125
TWO_PI_LO = 0.0019353071795864769
LN1E10 = 23.025850929940457
LN1E4 = 9.210340371976184
SC_SDPA = DH ** -0.5

DEBUG = set()
STOP_AFTER = None
STAGE_LIST = None
I1_PHASE = 9
I1_WINDOWS = NW
I1_UNITS = 16
F1_QUARTERS = 4
F1_W2 = NCH
F1_BIS = 0
F1_W1 = True
F1_WIN = NW
C0_SER = False
PR_MODE = 0

ROW_XSC, ROW_RS, ROW_MU1, ROW_S1, ROW_MU2, ROW_S2 = range(6)

ENGS = ("sync", "tensor", "vector", "scalar", "gpsimd")


def ap(t, off, dims):
    return bass.AP(t, off, dims)


class Sem:
    __slots__ = ("h", "n", "name")

    def __init__(self, handle, name):
        self.h = handle
        self.n = 0
        self.name = name


class Prog:
    def __init__(self, nc):
        self.nc = nc
        self.q = {e: [] for e in ENGS}

    def wait(self, eng, sem, val=None):
        v = sem.n if val is None else val
        if v < 0:
            return
        self.q[eng].append(lambda e, h=sem.h, v=v: e.wait_ge(h, v))

    def step(self, eng, fn, sem=None, k=1):
        if sem is not None:
            sem.n += k
            self.q[eng].append(lambda e, f=fn, h=sem.h, k=k: f(e).then_inc(h, k))
        else:
            self.q[eng].append(lambda e, f=fn: f(e))

    def fstep(self, eng, fn, fsem):
        fsem.n += 1
        v = fsem.n
        self.q[eng].append(lambda e, f=fn, h=fsem.h, v=v:
                           (f(e).then_inc(h, 1), e.wait_ge(h, v)))
        return v

    def emit(self):
        with self.nc.Block() as blk:
            for eng in ENGS:
                fns = self.q[eng]
                if not fns:
                    continue
                getattr(blk, eng)(lambda e, fns=fns: [f(e) for f in fns])


class K:
    def __init__(self):
        self.nc = bass.Bass("TRN2", target_bir_lowering=False)
        self.ctx = ExitStack()
        self._ns = 0
        self._stage_sems = {}

    def sbuf(self, name, shape, dtype=F32):
        return self.ctx.enter_context(self.nc.sbuf_tensor(name, shape, dtype))

    def psum(self, name, shape, dtype=F32):
        return self.ctx.enter_context(self.nc.psum_tensor(name, shape, dtype))

    def lsbuf(self, stack, name, shape, dtype=F32):
        return stack.enter_context(self.nc.sbuf_tensor(name, shape, dtype))

    def lpsum(self, stack, name, shape, dtype=F32):
        return stack.enter_context(self.nc.psum_tensor(name, shape, dtype))

    def sem(self, name=None):
        self._ns += 1
        nm = name or f"sm{self._ns}"
        return Sem(self.ctx.enter_context(self.nc.semaphore(nm)), nm)

    def lsem(self, stack, name=None):
        self._ns += 1
        nm = name or f"sm{self._ns}"
        s = Sem(stack.enter_context(self.nc.semaphore(nm)), nm)
        self._stage_sems.setdefault(id(stack), []).append(s)
        return s

    def close_stage(self, stack):
        sems = self._stage_sems.pop(id(stack), [])
        if sems:
            with self.nc.Block() as blk:
                @blk.gpsimd
                def _(g, sems=sems):
                    for s in sems:
                        g.wait_ge(s.h, s.n)
                    for s in sems:
                        g.sem_clear(s.h)
        stack.close()

    def prog(self):
        return Prog(self.nc)

    def build(self):
        nc = self.nc
        self.x_in = nc.dram_tensor("x", [T, D], F32, kind="ExternalInput")
        self.out = nc.dram_tensor("out", [T, D], F32, kind="ExternalOutput")
        self.w = {}
        for n, sh, dt in [
            ("L0_attn_Wr", [D, D], F32), ("L0_attn_Wk", [D, D], F32),
            ("L0_attn_Wv", [D, D], F32), ("L0_attn_Wo", [D, D], F32),
            ("L0_attn_decay", [D], F32), ("L0_attn_ln_w", [D], F32),
            ("L0_ffn_weight_q", [D, D], I8), ("L0_ffn_weight_scale", [D], F32),
            ("L0_ffn_bias", [D], F32),
            ("L1_attn_w_qkv", [D, 3 * D], F32), ("L1_attn_w_out", [D, D], F32),
            ("L1_attn_ln_g", [D], F32), ("L1_attn_ln_b", [D], F32),
            ("L1_attn_mem_beta", [HEADS], F32),
            ("L1_ffn_w1", [4 * D, D], F32), ("L1_ffn_b1", [4 * D], F32),
            ("L1_ffn_w2", [D, 4 * D], F32), ("L1_ffn_b2", [D], F32),
        ]:
            self.w[n] = nc.dram_tensor(n, sh, dt, kind="ExternalInput")

        def scratch(name, shape, dtype=F32):
            kind = "ExternalOutput" if name in DEBUG else None
            t = (nc.dram_tensor(name, shape, dtype, kind=kind) if kind
                 else nc.dram_tensor(name, shape, dtype))
            setattr(self, name, t)

        scratch("S_xT0", [D, T]); scratch("S_xq0", [D, T], BF16)
        scratch("S_r", [D, T]); scratch("S_state", [D, T])
        scratch("S_x1a", [D, T]); scratch("S_xT1", [D, T])
        scratch("S_xq1", [D, T], BF16)
        scratch("S_q", [D, T]); scratch("S_k", [D, T]); scratch("S_v", [D, T])
        scratch("S_outsT", [D, T], BF16); scratch("S_x2a", [D, T]); scratch("S_x2b", [D, T])
        scratch("S_rows", [8, T]); scratch("S_cos", [P, T]); scratch("S_sin", [P, T])
        scratch("S_act", [D, T], BF16)

        stages = ["prep", "A0", "B0", "C0", "D0", "E0", "A1", "B1", "INNER1",
                  "OUT1", "F1"]
        if STAGE_LIST is not None:
            stages = list(STAGE_LIST)
        elif STOP_AFTER is not None:
            stages = stages[:stages.index(STOP_AFTER) + 1]
        with self.ctx:
            self.persistent()
            act_open = True
            for st in stages:
                if st in ("INNER1", "OUT1", "F1") and act_open \
                        and "S_act" not in DEBUG:
                    self.act_stk.close()
                    act_open = False
                getattr(self, "stage_" + st)()
            if "S_act" in DEBUG:
                self.dump_act()
            if act_open:
                self.act_stk.close()
        return self.nc

    def persistent(self):
        self.identF = self.sbuf("identF", [P, P], F32)
        self.identB = self.sbuf("identB", [P, P], BF16)
        self.ones_col = self.sbuf("ones_col", [P, 1], F32)
        self.vecT = self.sbuf("vecT", [P, 104], F32)
        (self.C_DEC, self.C_WS, self.C_B0, self.C_G, self.C_BE, self.C_B2Q,
         self.C_LD, self.C_NLD) = 0, 8, 16, 24, 32, 40, 48, 96
        self.C_B1Q = 56
        self.beta_r = self.sbuf("beta_r", [P, HEADS], F32)
        self.beta1m_r = self.sbuf("beta1m_r", [P, HEADS], F32)
        self.xsc_pm = self.sbuf("xsc_pm", [P, NTT], F32)
        self.cpool = self.sbuf("cpool", [P, 5], F32)
        self.maskc = self.sbuf("maskc", [P, P], F32)
        self.mask4 = self.sbuf("mask4", [P, WIN], F32)
        self.beta_q2 = self.sbuf("beta_q2", [P, 4], F32)
        self.CONSTS = {1e-5: 0, 1e-6: 1, 1.0 / 64.0: 2, 1.0 / 16.0: 3,
                       1.0 / 1024.0: 4}
        self.act_stk = ExitStack()
        self.act = self.act_stk.enter_context(
            self.nc.sbuf_tensor("act", [P, NCH * T], BF16))

    def vc(self, c0, c=0, n=1):
        return self.vecT[:, c0 + c:c0 + c + n]

    def cc(self, val):
        col = self.CONSTS[val]
        return self.cpool[:, col:col + 1]

    def stage_prep(self):
        w = self.w
        stk = ExitStack()
        sD = self.lsem(stk)
        sC = self.lsem(stk)
        ioA = self.lsbuf(stk, "ioA", [P, P], F32)
        ioB = self.lsbuf(stk, "ioB", [P, P], F32)
        vload = self.lsbuf(stk, "vload", [80, P], F32)
        pP = self.lpsum(stk, "pp_ps", [P, P], F32)

        p = self.prog()
        p.step("gpsimd", lambda g: g.iota(ioA[:, :], [[1, P]], channel_multiplier=0,
                                          allow_small_or_imprecise_dtypes=True))
        p.step("gpsimd", lambda g: g.iota(ioB[:, :], [[0, P]], channel_multiplier=1,
                                          allow_small_or_imprecise_dtypes=True))
        p.step("vector", lambda v: v.memset(self.ones_col[:, :], 1.0))
        for val, col in self.CONSTS.items():
            p.step("vector", lambda v, val=val, col=col: v.memset(
                self.cpool[:, col:col + 1], val))
        names = ["L0_attn_decay", "L0_ffn_weight_scale", "L0_ffn_bias",
                 "L1_attn_ln_g", "L1_attn_ln_b", "L1_ffn_b2"]
        for vi, name in enumerate(names):
            p.step("sync", lambda sy, vi=vi, name=name: sy.dma_start(
                vload[vi * NCH:(vi + 1) * NCH, :],
                ap(w[name], 0, [[P, NCH], [1, P]])), sD, 16)
        p.step("sync", lambda sy: sy.dma_start(
            vload[48:80, :], ap(w["L1_ffn_b1"], 0, [[P, 32], [1, P]])), sD, 16)
        p.step("sync", lambda sy: sy.dma_start(
            self.beta_r[:, :], ap(w["L1_attn_mem_beta"], 0, [[0, P], [1, HEADS]])),
            sD, 16)
        p.wait("sync", sD)
        p.emit()

        p = self.prog()
        p.step("vector", lambda v: v.tensor_tensor(self.identF[:, :], ioA[:, :],
                                                   ioB[:, :], ALU.is_equal), sC)
        p.wait("vector", sC)
        p.step("vector", lambda v: v.tensor_copy(self.identB[:, :], self.identF[:, :]))
        p.emit()

        p = self.prog()
        p.step("tensor", lambda t: t.transpose(pP[:, 0:80], vload[0:80, :],
                                               self.identF[0:80, 0:80]), sC)
        p.wait("vector", sC)
        p.step("vector", lambda v: v.tensor_copy(self.vecT[:, 0:48], pP[:, 0:48]))
        p.step("vector", lambda v: v.tensor_copy(
            self.vecT[:, self.C_B1Q:self.C_B1Q + 32], pP[:, 48:80]))
        p.emit()

        p = self.prog()
        fP = self.sem()
        p.step("scalar", lambda s: s.activation(self.vc(self.C_DEC, 0, 8),
                                                self.vc(self.C_DEC, 0, 8),
                                                AF.Tanh, scale=0.5), sC)
        p.step("scalar", lambda s: s.activation(self.beta_r[:, :], self.beta_r[:, :],
                                                AF.Tanh, scale=0.5), sC)
        p.wait("vector", sC)
        p.fstep("vector", lambda v: v.tensor_scalar(self.vc(self.C_DEC, 0, 8),
                                                    self.vc(self.C_DEC, 0, 8),
                                                    0.5, 0.5, ALU.mult, ALU.add), fP)
        p.fstep("vector", lambda v: v.tensor_scalar(self.beta_r[:, :], self.beta_r[:, :],
                                                    0.5, 0.5, ALU.mult, ALU.add), fP)
        p.fstep("vector", lambda v: v.tensor_scalar(self.beta1m_r[:, :],
                                                    self.beta_r[:, :],
                                                    -1.0, 1.0, ALU.mult, ALU.add), fP)
        p.fstep("vector", lambda v: v.tensor_scalar(self.vc(self.C_LD, 0, 8),
                                                    self.vc(self.C_DEC, 0, 8),
                                                    1e-7, None, ALU.max), fP)
        for c0, n, post in ((self.C_B2Q, 8, 1.0 / 64.0), (self.C_B1Q, 32, 0.25)):
            p.fstep("vector", lambda v, c0=c0, n=n: v.tensor_scalar(
                self.vc(c0, 0, n), self.vc(c0, 0, n), 64.0, -128.25, ALU.mult,
                ALU.max), fP)
            p.fstep("vector", lambda v, c0=c0, n=n: v.tensor_scalar(
                self.vc(c0, 0, n), self.vc(c0, 0, n), 127.25, R23, ALU.min,
                ALU.add), fP)
            p.fstep("vector", lambda v, c0=c0, n=n, post=post: v.tensor_scalar(
                self.vc(c0, 0, n), self.vc(c0, 0, n), R23, post, ALU.subtract,
                ALU.mult), fP)
        p.emit()

        p = self.prog()
        p.step("scalar", lambda s: s.activation(self.vc(self.C_LD, 0, 8),
                                                self.vc(self.C_LD, 0, 8), AF.Ln), sC)
        p.wait("vector", sC)
        p.fstep("vector", lambda v: v.tensor_scalar(self.vc(self.C_NLD, 0, 8),
                                                    self.vc(self.C_LD, 0, 8),
                                                    -1.0, None, ALU.mult), fP)
        p.emit()
        self._fP = fP

        self.prep_mask()
        self.prep_trig(sD, stk)
        self.close_stage(stk)

    def prep_mask(self):
        stk = ExitStack()
        ioR = self.lsbuf(stk, "pm_ioR", [P, P], I32)
        ioC = self.lsbuf(stk, "pm_ioC", [P, P], I32)
        t1 = self.lsbuf(stk, "pm_t1", [P, P], I32)
        t2 = self.lsbuf(stk, "pm_t2", [P, P], I32)
        fP = self._fP
        p = self.prog()
        p.step("gpsimd", lambda g: g.iota(ioC[:, :], [[1, P]], channel_multiplier=0))
        p.step("gpsimd", lambda g: g.iota(ioR[:, :], [[0, P]], channel_multiplier=1))
        p.emit()
        p = self.prog()
        p.fstep("vector", lambda v: v.tensor_scalar(
            t1[:, :], ioR[:, :], 5, None, ALU.arith_shift_right), fP)
        p.fstep("vector", lambda v: v.tensor_scalar(
            t2[:, :], ioC[:, :], 5, None, ALU.arith_shift_right), fP)
        p.fstep("vector", lambda v: v.tensor_tensor(
            t1[:, :], t1[:, :], t2[:, :], ALU.is_equal), fP)
        p.fstep("vector", lambda v: v.tensor_scalar(
            ioR[:, :], ioR[:, :], 31, None, ALU.bitwise_and), fP)
        p.fstep("vector", lambda v: v.tensor_scalar(
            ioC[:, :], ioC[:, :], 31, None, ALU.bitwise_and), fP)
        p.fstep("vector", lambda v: v.tensor_tensor(
            t2[:, :], ioC[:, :], ioR[:, :], ALU.is_le), fP)
        p.fstep("vector", lambda v: v.tensor_tensor(
            t1[:, :], t1[:, :], t2[:, :], ALU.bitwise_and), fP)
        p.fstep("vector", lambda v: v.tensor_copy(self.maskc[:, :], t1[:, :]), fP)
        p.fstep("vector", lambda v: v.tensor_scalar(
            self.maskc[:, :], self.maskc[:, :], 1e30, -1e30, ALU.mult, ALU.add), fP)
        for j in range(4):
            for m in range(4):
                h = 2 * j + (m % 2) + 8 * (m // 2)
                p.fstep("vector", lambda v, j=j, m=m, h=h: v.tensor_copy(
                    self.beta_q2[32 * j:32 * (j + 1), m:m + 1],
                    self.beta_r[32 * j:32 * (j + 1), h:h + 1]), fP)
        for rep in range(4):
            p.step("vector", lambda v, rep=rep: v.tensor_copy(
                self.mask4[:, rep * P:(rep + 1) * P], self.maskc[:, :]))
        p.emit()
        self.close_stage(stk)

    def prep_trig(self, sD, stk):
        frq = self.lsbuf(stk, "frq", [64, 1], F32)
        sgn = self.lsbuf(stk, "sgn", [64, 1], F32)
        pint = self.lsbuf(stk, "pint", [64, 2], I32)
        angb = self.lsbuf(stk, "angb", [64, 2048], F32)
        redb = self.lsbuf(stk, "redb", [64, 2048], F32)
        red2 = self.lsbuf(stk, "red2", [64, 2048], F32)
        kb = self.lsbuf(stk, "kb", [64, 2048], F32)
        outb = self.lsbuf(stk, "outb", [64, 2048], F32)
        sC = self.lsem(stk)
        fP = self._fP
        INV2PI = 1.0 / (TWO_PI_HI + TWO_PI_LO)

        p = self.prog()
        p.step("gpsimd", lambda g: g.iota(pint[:, 0:1], [[0, 1]], channel_multiplier=1))
        p.emit()
        p = self.prog()
        for i in range(32):
            val = float(np.float32(10000.0 ** (-2.0 * i / 64.0)))
            p.step("vector", lambda v, i=i, val=val: v.memset(
                kb[0:1, i:i + 1], val), sC)
        p.wait("vector", sC)
        p.step("vector", lambda v: v.tensor_copy(kb[0:1, 32:64], kb[0:1, 0:32]), sC)
        p.wait("sync", sC)
        p.step("sync", lambda sy: sy.dma_start(
            ap(self.S_rows, 6 * T, [[32, 1], [1, 32]]), kb[0:1, 0:32]), sD, 16)
        p.wait("sync", sD)
        p.step("sync", lambda sy: sy.dma_start(
            frq[:, 0:1], ap(self.S_rows, 6 * T, [[1, 32], [0, 2], [1, 1]])), sD, 16)
        p.wait("sync", sD)
        p.fstep("vector", lambda v: v.tensor_scalar(pint[:, 1:2], pint[:, 0:1], 1,
                                                    None, ALU.bitwise_and), fP)
        p.fstep("vector", lambda v: v.tensor_copy(sgn[:, :], pint[:, 1:2]), fP)
        p.fstep("vector", lambda v: v.tensor_scalar(sgn[:, :], sgn[:, :],
                                                    2.0, -1.0, ALU.mult, ALU.add), fP)
        p.emit()

        for half in range(2):
            t0 = half * 2048
            p = self.prog()
            p.step("gpsimd", lambda g, t0=t0: g.iota(
                angb[:, :], [[1, 2048]], base=t0, channel_multiplier=0,
                allow_small_or_imprecise_dtypes=True), sC)
            p.wait("vector", sC)
            p.fstep("vector", lambda v: v.tensor_scalar(angb[:, :], angb[:, :],
                                                        frq[:, 0:1], None, ALU.mult),
                    fP)
            p.emit()
            for kind in ("cos", "sin"):
                qoff = 0.25 if kind == "cos" else 0.0
                coff = PI / 2.0 if kind == "cos" else 0.0
                p = self.prog()
                p.fstep("vector", lambda v, qoff=qoff: v.tensor_scalar(
                    kb[:, :], angb[:, :], INV2PI, qoff, ALU.mult, ALU.add), fP)
                p.fstep("vector", lambda v: v.tensor_scalar(
                    kb[:, :], kb[:, :], R23, R23, ALU.add, ALU.subtract), fP)
                p.fstep("vector", lambda v: v.scalar_tensor_tensor(
                    red2[:, :], kb[:, :], -TWO_PI_HI, angb[:, :], ALU.mult, ALU.add),
                    fP)
                p.fstep("vector", lambda v: v.scalar_tensor_tensor(
                    redb[:, :], kb[:, :], -TWO_PI_LO, red2[:, :], ALU.mult, ALU.add),
                    fP)
                p.fstep("vector", lambda v, coff=coff: v.tensor_scalar(
                    redb[:, :], redb[:, :], coff, None, ALU.add), fP)
                p.step("vector", lambda v: v.tensor_scalar(
                    redb[:, :], redb[:, :], 3.1415925, -3.1415925,
                    ALU.min, ALU.max), sC)
                p.wait("scalar", sC)
                p.step("scalar", lambda s: s.activation(outb[:, :], redb[:, :],
                                                        AF.Sin), sC)
                if kind == "sin":
                    p.wait("vector", sC)
                    p.step("vector", lambda v: v.tensor_scalar(
                        outb[:, :], outb[:, :], sgn[:, 0:1], None, ALU.mult), sC)
                p.wait("sync", sC)
                dst = self.S_cos if kind == "cos" else self.S_sin
                for rb in range(2):
                    p.step("sync", lambda sy, dst=dst, rb=rb, t0=t0: sy.dma_start(
                        ap(dst, rb * 64 * T + t0, [[T, 64], [1, 2048]]),
                        outb[:, :]), sD, 16)
                p.wait("sync", sD)
                p.emit()

    def stage_A0(self):
        stk = ExitStack()
        sA = self.lsem(stk); sV = self.lsem(stk)
        sT = self.lsem(stk); sGX = self.lsem(stk); sGN = self.lsem(stk)
        fV = self.lsem(stk)
        sLh = [self.lsem(stk), self.lsem(stk)]
        sOX = [self.lsem(stk), self.lsem(stk)]
        sOQ = [self.lsem(stk), self.lsem(stk)]
        xtm = self.lsbuf(stk, "a0_xtm", [P, 2 * 1024], F32)
        sqb = self.lsbuf(stk, "a0_sqb", [P, 1024], F32)
        xnb = self.lsbuf(stk, "a0_xnb", [P, 2 * 1024], F32)
        xqb = self.lsbuf(stk, "a0_xqb", [P, 2 * 1024], BF16)
        stgX = self.lsbuf(stk, "a0_stgX", [P, 2 * 1024], F32)
        stgQ = self.lsbuf(stk, "a0_stgQ", [P, 2 * 1024], BF16)
        stt = self.lsbuf(stk, "a0_stt", [P, 8], F32)
        rowstg = self.lsbuf(stk, "a0_row", [32, P], F32)
        pA = self.lpsum(stk, "a0_pA", [P, 1024], F32)
        pB = self.lpsum(stk, "a0_pB", [P, 1024], F32)
        pQ = self.lpsum(stk, "a0_pQ", [P, 1024], BF16)
        C_MU, C_S, C_NMS, C_QS, C_T0, C_T1, C_T2, C_T3 = range(8)

        p = self.prog()
        for i in range(NTT):
            o = (i % 2) * 1024
            xt = ap(xtm, o, [[2048, P], [1, 1024]])

            p.wait("sync", sT, 3 * (i - 1))
            p.wait("sync", sV, i - 1)
            p.wait("sync", sA, 2 * (i - 1))
            p.step("sync", lambda sy, i=i, o=o: sy.dma_start(
                ap(xtm, o, [[2048, P], [1, 1024]]),
                ap(self.x_in, i * P * D, [[D, P], [1, D]])), sLh[i % 2], 16)
            if i >= 1:
                j = i - 1
                jo = (j % 2) * 1024
                p.wait("sync", sGX, 2 * j + 1)
                p.step("sync", lambda sy, j=j, jo=jo: sy.dma_start(
                    ap(self.S_xT0, j * P, [[T, P], [P * T, NCH], [1, P]]),
                    ap(stgX, jo, [[2048, P], [P, NCH], [1, P]])), sOX[j % 2], 16)
                p.wait("sync", sGX, 2 * j + 2)
                p.step("sync", lambda sy, j=j, jo=jo: sy.dma_start(
                    ap(self.S_xq0, j * P, [[T, P], [P * T, NCH], [1, P]]),
                    ap(stgQ, jo, [[2048, P], [P, NCH], [1, P]])), sOQ[j % 2], 16)

            p.wait("scalar", sLh[i % 2])
            p.step("scalar", lambda s, xt=xt: s.activation(sqb[:, :], xt, AF.Square),
                   sA)
            act_sqrt_slot = len(p.q["scalar"])
            p.step("scalar", lambda s: s.activation(
                stt[:, C_S:C_S + 1], stt[:, C_T2:C_T2 + 1], AF.Sqrt,
                bias=self.cc(1e-5)), sA)

            p.wait("vector", sA, 2 * i + 1)
            p.fstep("vector", lambda v, xt=xt: v.reduce_sum(
                stt[:, C_T0:C_T0 + 1], xt, AX), fV)
            p.fstep("vector", lambda v: v.reduce_sum(
                stt[:, C_T1:C_T1 + 1], sqb[:, :], AX), fV)
            p.fstep("vector", lambda v: v.tensor_scalar(
                stt[:, C_MU:C_MU + 1], stt[:, C_T0:C_T0 + 1], 1.0 / D, None,
                ALU.mult), fV)
            p.fstep("vector", lambda v: v.tensor_tensor(
                stt[:, C_T2:C_T2 + 1], stt[:, C_MU:C_MU + 1], stt[:, C_MU:C_MU + 1],
                ALU.mult), fV)
            var_tick = p.fstep("vector", lambda v: v.scalar_tensor_tensor(
                stt[:, C_T2:C_T2 + 1], stt[:, C_T1:C_T1 + 1], 1.0 / D,
                stt[:, C_T2:C_T2 + 1], ALU.mult, ALU.subtract), fV)
            p.q["scalar"].insert(act_sqrt_slot,
                                 lambda e, h=fV.h, v=var_tick: e.wait_ge(h, v))
            p.wait("vector", sA, 2 * i + 2)
            p.fstep("vector", lambda v: v.reciprocal(stt[:, C_T3:C_T3 + 1],
                                                     stt[:, C_S:C_S + 1]), fV)
            p.fstep("vector", lambda v: v.tensor_tensor(
                stt[:, C_T0:C_T0 + 1], stt[:, C_S:C_S + 1], stt[:, C_T3:C_T3 + 1],
                ALU.mult), fV)
            p.fstep("vector", lambda v: v.tensor_scalar(
                stt[:, C_T0:C_T0 + 1], stt[:, C_T0:C_T0 + 1], -1.0, 2.0,
                ALU.mult, ALU.add), fV)
            p.fstep("vector", lambda v: v.tensor_tensor(
                stt[:, C_T3:C_T3 + 1], stt[:, C_T3:C_T3 + 1], stt[:, C_T0:C_T0 + 1],
                ALU.mult), fV)
            p.fstep("vector", lambda v: v.tensor_tensor(
                stt[:, C_NMS:C_NMS + 1], stt[:, C_MU:C_MU + 1], stt[:, C_T3:C_T3 + 1],
                ALU.mult), fV)
            p.fstep("vector", lambda v: v.tensor_scalar(
                stt[:, C_NMS:C_NMS + 1], stt[:, C_NMS:C_NMS + 1], -1.0, None,
                ALU.mult), fV)
            if i >= 2:
                p.wait("vector", sT, 3 * (i - 2) + 2)
                p.wait("vector", sT, 3 * (i - 2) + 3)
            p.step("vector", lambda v, xt=xt, o=o: v.tensor_scalar(
                xnb[:, o:o + 1024], xt, stt[:, C_T3:C_T3 + 1],
                stt[:, C_NMS:C_NMS + 1], ALU.mult, ALU.add))
            p.fstep("vector", lambda v, o=o: v.reduce_max(
                stt[:, C_T0:C_T0 + 1], xnb[:, o:o + 1024], AX,
                apply_absolute_value=True), fV)
            p.fstep("vector", lambda v: v.tensor_scalar(
                stt[:, C_T0:C_T0 + 1], stt[:, C_T0:C_T0 + 1], 1e-6, None, ALU.max), fV)
            p.fstep("vector", lambda v: v.reciprocal(stt[:, C_QS:C_QS + 1],
                                                     stt[:, C_T0:C_T0 + 1]), fV)
            p.fstep("vector", lambda v: v.tensor_tensor(
                stt[:, C_T1:C_T1 + 1], stt[:, C_T0:C_T0 + 1], stt[:, C_QS:C_QS + 1],
                ALU.mult), fV)
            p.fstep("vector", lambda v: v.tensor_scalar(
                stt[:, C_T1:C_T1 + 1], stt[:, C_T1:C_T1 + 1], -1.0, 2.0,
                ALU.mult, ALU.add), fV)
            p.fstep("vector", lambda v: v.tensor_tensor(
                stt[:, C_QS:C_QS + 1], stt[:, C_QS:C_QS + 1], stt[:, C_T1:C_T1 + 1],
                ALU.mult), fV)
            p.fstep("vector", lambda v, i=i: v.tensor_scalar(
                self.xsc_pm[:, i:i + 1], stt[:, C_T0:C_T0 + 1], 1.0 / 127.0,
                None, ALU.mult), fV)
            p.fstep("vector", lambda v: v.tensor_scalar(
                stt[:, C_QS:C_QS + 1], stt[:, C_QS:C_QS + 1], 127.0, None,
                ALU.mult), fV)
            p.step("vector", lambda v, o=o: v.tensor_scalar(
                sqb[:, :], xnb[:, o:o + 1024], stt[:, C_QS:C_QS + 1], R23,
                ALU.mult, ALU.add))
            p.step("vector", lambda v, o=o: v.tensor_scalar(
                xqb[:, o:o + 1024], sqb[:, :], R23, None, ALU.subtract), sV)

            p.wait("tensor", sLh[i % 2])
            if i >= 1:
                p.wait("tensor", sGX, 2 * (i - 1) + 1)
            for c in range(NCH):
                last = c == NCH - 1
                p.step("tensor", lambda t, o=o, c=c: t.transpose(
                    pA[:, c * P:(c + 1) * P], ap(xtm, o + c * P, [[2048, P], [1, P]]),
                    self.identF[:, :]), sT if last else None)
            p.wait("tensor", sV, i + 1)
            p.wait("tensor", sGN, i)
            for c in range(NCH):
                last = c == NCH - 1
                p.step("tensor", lambda t, o=o, c=c: t.transpose(
                    pB[:, c * P:(c + 1) * P], ap(xnb, o + c * P, [[2048, P], [1, P]]),
                    self.identF[:, :]), sT if last else None)
            p.wait("tensor", sV, i + 1)
            if i >= 1:
                p.wait("tensor", sGX, 2 * i)
            for c in range(NCH):
                last = c == NCH - 1
                p.step("tensor", lambda t, o=o, c=c: t.transpose(
                    pQ[:, c * P:(c + 1) * P], ap(xqb, o + c * P, [[2048, P], [1, P]]),
                    self.identB[:, :]), sT if last else None)

            p.wait("scalar", sT, 3 * i + 1)
            if i >= 2:
                p.wait("scalar", sOX[i % 2])
            p.step("scalar", lambda sc, o=o: sc.copy(
                ap(stgX, o, [[2048, P], [1, 512]]), pA[:, 0:512]))
            p.step("scalar", lambda sc, o=o: sc.copy(
                ap(stgX, o + 512, [[2048, P], [1, 512]]), pA[:, 512:1024]), sGX)
            p.wait("scalar", sT, 3 * i + 3)
            if i >= 2:
                p.wait("scalar", sOQ[i % 2])
            p.step("scalar", lambda sc, o=o: sc.copy(
                ap(stgQ, o, [[2048, P], [1, 1024]]), pQ[:, :]), sGX)
            p.wait("vector", sT, 3 * i + 2)
            p.step("vector", lambda v, i=i: v.tensor_copy(
                ap(self.act, i * P, [[NCH * T, P], [T, 4], [1, P]]),
                ap(pB, 0, [[1024, P], [P, 4], [1, P]])))
            p.step("vector", lambda v, i=i: v.tensor_copy(
                ap(self.act, 4 * T + i * P, [[NCH * T, P], [T, 4], [1, P]]),
                ap(pB, 512, [[1024, P], [P, 4], [1, P]])), sGN)

        j = NTT - 1
        jo = (j % 2) * 1024
        p.wait("sync", sGX, 2 * j + 1)
        p.step("sync", lambda sy: sy.dma_start(
            ap(self.S_xT0, j * P, [[T, P], [P * T, NCH], [1, P]]),
            ap(stgX, jo, [[2048, P], [P, NCH], [1, P]])), sOX[j % 2], 16)
        p.wait("sync", sGX, 2 * j + 2)
        p.step("sync", lambda sy: sy.dma_start(
            ap(self.S_xq0, j * P, [[T, P], [P * T, NCH], [1, P]]),
            ap(stgQ, jo, [[2048, P], [P, NCH], [1, P]])), sOQ[j % 2], 16)
        for s in sOX + sOQ:
            p.wait("sync", s)
        p.emit()

        p = self.prog()
        sC = self.lsem(stk)
        p.step("tensor", lambda t: t.transpose(pA[0:32, 0:P], self.xsc_pm[:, :],
                                               self.identF[:, :]), sC)
        p.wait("vector", sC)
        p.step("vector", lambda v: v.tensor_copy(rowstg[:, :], pA[0:32, 0:P]), sC)
        p.wait("sync", sC)
        p.step("sync", lambda sy: sy.dma_start(
            ap(self.S_rows, ROW_XSC * T, [[P, 32], [1, P]]), rowstg[:, :]), sC, 16)
        p.wait("sync", sC)
        p.emit()
        self.close_stage(stk)

    def stage_B0(self):
        stk = ExitStack()
        w = self.w
        sWL = self.lsem(stk); sWT = self.lsem(stk); sWE = self.lsem(stk)
        sMM = self.lsem(stk); sEV = self.lsem(stk); sRM = self.lsem(stk)
        sSP = self.lsem(stk); sAC = self.lsem(stk); sEXP = self.lsem(stk); sIO = self.lsem(stk)
        sRSP = [self.lsem(stk), self.lsem(stk)]
        wraw = self.lsbuf(stk, "b0_wraw", [P, 1024], F32)
        lhsW = self.lsbuf(stk, "b0_lhs", [P, 1024], BF16)
        kf = self.lsbuf(stk, "b0_kf", [P, T], F32)
        uf = self.lsbuf(stk, "b0_uf", [P, T], F32)
        wp = self.lsbuf(stk, "b0_wp", [P, 2 * WIN], F32)
        ramp = self.lsbuf(stk, "b0_ramp", [P, 2 * WIN], F32)
        rstg = self.lsbuf(stk, "b0_rstg", [P, 2 * WIN], F32)
        pW = self.lpsum(stk, "b0_pW", [P, 1024], F32)
        pM = self.lpsum(stk, "b0_pM", [P, 2 * WIN], F32)

        WSRC = {"k": w["L0_attn_Wk"], "v": w["L0_attn_Wv"], "r": w["L0_attn_Wr"]}

        p = self.prog()
        for c in range(NCH):
            for kind in ("k", "v", "r"):
                p.wait("sync", sWE)
                p.wait("sync", sMM)
                p.step("sync", lambda sy, src=WSRC[kind], c=c: sy.dma_start(
                    wraw[:, :], ap(src, c * P * D, [[D, P], [1, D]])), sWL, 16)
                p.wait("tensor", sWL)
                p.wait("tensor", sWE)
                for cc in range(NCH):
                    p.step("tensor", lambda t, cc=cc: t.transpose(
                        pW[:, cc * P:(cc + 1) * P], wraw[:, cc * P:(cc + 1) * P],
                        self.identF[:, :]), sWT if cc == NCH - 1 else None)
                p.wait("vector", sWT)
                p.step("vector", lambda v: v.tensor_copy(lhsW[:, 0:512], pW[:, 0:512]))
                p.step("vector", lambda v: v.tensor_copy(lhsW[:, 512:1024],
                                                         pW[:, 512:1024]), sWE)

                if kind == "v":
                    p.wait("vector", sSP)
                    p.wait("vector", sRSP[0])
                    p.wait("vector", sRSP[1])
                for wi in range(NW):
                    pm = (wi % 2)
                    p.wait("tensor", sWE)
                    p.wait("tensor", sEV, sEV.n - 1)
                    for cc in range(NCH):
                        p.step("tensor", lambda t, cc=cc, c=c, wi=wi, pm=pm: t.matmul(
                            pM[:, pm * WIN:(pm + 1) * WIN],
                            lhsW[:, cc * P:(cc + 1) * P],
                            self.act[:, cc * T + wi * WIN:cc * T + (wi + 1) * WIN],
                            start=(cc == 0), stop=(cc == NCH - 1)),
                            sMM if cc == NCH - 1 else None)
                    if kind == "k":
                        p.wait("vector", sMM)
                        p.step("vector", lambda v, wi=wi, pm=pm: v.tensor_copy(
                            kf[:, wi * WIN:(wi + 1) * WIN],
                            pM[:, pm * WIN:(pm + 1) * WIN]), sEV)
                    elif kind == "v":
                        p.wait("gpsimd", sEV, sEV.n - 1)
                        p.step("gpsimd", lambda g, wi=wi, pm=pm: g.iota(
                            ramp[:, pm * WIN:(pm + 1) * WIN], [[1, WIN]],
                            base=wi * WIN, channel_multiplier=0,
                            allow_small_or_imprecise_dtypes=True), sIO)
                        p.wait("vector", sIO)
                        p.step("vector", lambda v, c=c, pm=pm: v.tensor_scalar(
                            wp[:, pm * WIN:(pm + 1) * WIN],
                            ramp[:, pm * WIN:(pm + 1) * WIN],
                            self.vc(self.C_NLD, c), LN1E10, ALU.mult, ALU.min), sRM)
                        p.wait("scalar", sRM)
                        p.step("scalar", lambda s, pm=pm: s.activation(
                            wp[:, pm * WIN:(pm + 1) * WIN],
                            wp[:, pm * WIN:(pm + 1) * WIN], AF.Exp), sEXP)
                        p.wait("vector", sMM)
                        p.step("vector", lambda v, wi=wi, pm=pm: v.tensor_tensor(
                            uf[:, wi * WIN:(wi + 1) * WIN],
                            pM[:, pm * WIN:(pm + 1) * WIN],
                            kf[:, wi * WIN:(wi + 1) * WIN], ALU.mult))
                        p.wait("vector", sEXP)
                        p.step("vector", lambda v, wi=wi, pm=pm: v.tensor_tensor(
                            uf[:, wi * WIN:(wi + 1) * WIN],
                            uf[:, wi * WIN:(wi + 1) * WIN],
                            wp[:, pm * WIN:(pm + 1) * WIN], ALU.mult), sEV)
                    else:
                        p.wait("scalar", sMM)
                        p.wait("scalar", sRSP[pm])
                        p.step("scalar", lambda s, pm=pm: s.activation(
                            rstg[:, pm * WIN:(pm + 1) * WIN],
                            pM[:, pm * WIN:(pm + 1) * WIN], AF.Tanh, scale=0.5), sAC)
                        p.wait("vector", sAC)
                        p.step("vector", lambda v, pm=pm: v.tensor_scalar(
                            rstg[:, pm * WIN:(pm + 1) * WIN],
                            rstg[:, pm * WIN:(pm + 1) * WIN],
                            0.5, 0.5, ALU.mult, ALU.add), sEV)
                        p.wait("sync", sEV)
                        p.step("sync", lambda sy, c=c, wi=wi, pm=pm: sy.dma_start(
                            ap(self.S_r, c * P * T + wi * WIN, [[T, P], [1, WIN]]),
                            rstg[:, pm * WIN:(pm + 1) * WIN]), sRSP[pm], 16)

                if kind == "v":
                    p.wait("vector", sEV)
                    if "B0DBG" in DEBUG:
                        p.wait("sync", sEV)
                        for wi2 in range(NW):
                            p.step("sync", lambda sy, c=c, wi2=wi2: sy.dma_start(
                                ap(self.S_x1a, c * P * T + wi2 * WIN, [[T, P], [1, WIN]]),
                                uf[:, wi2 * WIN:(wi2 + 1) * WIN]), sSP, 16)
                        for wi2 in range(NW):
                            p.step("sync", lambda sy, c=c, wi2=wi2: sy.dma_start(
                                ap(self.S_xT1, c * P * T + wi2 * WIN, [[T, P], [1, WIN]]),
                                kf[:, wi2 * WIN:(wi2 + 1) * WIN]), sSP, 16)
                        p.step("sync", lambda sy, c=c: sy.dma_start(
                            ap(self.S_rows, 7 * T + c * WIN, [[1, 1], [1, WIN]]),
                            wp[0:1, 0:WIN]), sSP, 16)
                    p.step("vector", lambda v: v.tensor_tensor_scan(
                        kf[:, :], uf[:, :], uf[:, :], 0.0, ALU.add, ALU.bypass), sEV)
                    if "B0DBG" in DEBUG:
                        p.wait("sync", sEV)
                        for wi2 in range(NW):
                            p.step("sync", lambda sy, c=c, wi2=wi2: sy.dma_start(
                                ap(self.S_x2a, c * P * T + wi2 * WIN, [[T, P], [1, WIN]]),
                                kf[:, wi2 * WIN:(wi2 + 1) * WIN]), sSP, 16)
                        p.wait("vector", sSP)
                    for wi in range(NW):
                        pm = wi % 2
                        p.wait("gpsimd", sEV)
                        p.wait("gpsimd", sAC, sAC.n - 1)
                        p.step("gpsimd", lambda g, wi=wi, pm=pm: g.iota(
                            ramp[:, pm * WIN:(pm + 1) * WIN], [[1, WIN]],
                            base=wi * WIN, channel_multiplier=0,
                            allow_small_or_imprecise_dtypes=True), sIO)
                        p.wait("vector", sIO)
                        p.step("vector", lambda v, c=c, pm=pm: v.tensor_scalar(
                            wp[:, pm * WIN:(pm + 1) * WIN],
                            ramp[:, pm * WIN:(pm + 1) * WIN],
                            self.vc(self.C_LD, c), -103.0, ALU.mult, ALU.max), sRM)
                        p.wait("scalar", sRM)
                        p.step("scalar", lambda s, pm=pm: s.activation(
                            wp[:, pm * WIN:(pm + 1) * WIN],
                            wp[:, pm * WIN:(pm + 1) * WIN], AF.Exp), sAC)
                        p.wait("vector", sAC)
                        p.wait("vector", sSP)
                        p.step("vector", lambda v, wi=wi, pm=pm: v.tensor_tensor(
                            uf[:, wi * WIN:(wi + 1) * WIN],
                            kf[:, wi * WIN:(wi + 1) * WIN],
                            wp[:, pm * WIN:(pm + 1) * WIN], ALU.mult), sEV)
                        p.wait("sync", sEV)
                        p.step("sync", lambda sy, c=c, wi=wi: sy.dma_start(
                            ap(self.S_state, c * P * T + wi * WIN, [[T, P], [1, WIN]]),
                            uf[:, wi * WIN:(wi + 1) * WIN]), sSP, 16)
        p.wait("sync", sSP)
        p.wait("sync", sRSP[0])
        p.wait("sync", sRSP[1])
        p.wait("sync", sEV)
        p.emit()
        self.close_stage(stk)

    def stage_C0(self):
        stk = ExitStack()
        sY = self.lsem(stk); sSQ = self.lsem(stk); sMM = self.lsem(stk)
        sRW = self.lsem(stk); fV = self.lsem(stk); sRC = self.lsem(stk)
        sLh = [self.lsem(stk), self.lsem(stk)]
        rt = self.lsbuf(stk, "c0_rt", [P, 2 * WIN], F32)
        st2 = self.lsbuf(stk, "c0_st", [P, 2 * WIN], F32)
        sqb = self.lsbuf(stk, "c0_sq", [P, 2 * WIN], F32)
        rowb = self.lsbuf(stk, "c0_row", [1, WIN], F32)
        statb = self.lsbuf(stk, "c0_stat", [P, NTT], F32)
        rs_b = self.lsbuf(stk, "c0_rsb", [P, WIN], F32)
        yt32 = self.lsbuf(stk, "c0_yt", [P, 2 * WIN], F32)
        pS = self.lpsum(stk, "c0_pS", [1, WIN], F32)

        p = self.prog()
        for wi in range(NW):
            for c in range(NCH):
                idx = wi * NCH + c
                pm = idx % 2
                o = pm * WIN
                p.wait("sync", sY, idx if C0_SER else idx - 1)
                p.step("sync", lambda sy, c=c, wi=wi, o=o, pm=pm: sy.dma_start(
                    ap(rt, o, [[2 * WIN, P], [1, WIN]]),
                    ap(self.S_r, c * P * T + wi * WIN, [[T, P], [1, WIN]])),
                    sLh[pm], 16)
                p.step("sync", lambda sy, c=c, wi=wi, o=o, pm=pm: sy.dma_start(
                    ap(st2, o, [[2 * WIN, P], [1, WIN]]),
                    ap(self.S_state, c * P * T + wi * WIN, [[T, P], [1, WIN]])),
                    sLh[pm], 16)
                p.wait("vector", sLh[pm])
                p.wait("vector", sSQ, idx if C0_SER else idx - 1)
                p.step("vector", lambda v, o=o: v.tensor_tensor(
                    ap(yt32, o, [[2 * WIN, P], [1, WIN]]),
                    ap(rt, o, [[2 * WIN, P], [1, WIN]]),
                    ap(st2, o, [[2 * WIN, P], [1, WIN]]), ALU.mult), sY)
                p.step("vector", lambda v, c=c, wi=wi, o=o: v.tensor_copy(
                    self.act[:, c * T + wi * WIN:c * T + (wi + 1) * WIN],
                    ap(yt32, o, [[2 * WIN, P], [1, WIN]])))
                p.wait("scalar", sY)
                p.wait("scalar", sMM, idx if C0_SER else idx - 1)
                p.step("scalar", lambda s, o=o: s.activation(
                    ap(sqb, o, [[2 * WIN, P], [1, WIN]]),
                    ap(yt32, o, [[2 * WIN, P], [1, WIN]]),
                    AF.Square), sSQ)
                p.wait("tensor", sSQ)
                if c == 0:
                    p.wait("tensor", sRC, wi)
                p.step("tensor", lambda t, c=c, o=o: t.matmul(
                    pS[0:1, :], self.ones_col[:, :],
                    ap(sqb, o, [[2 * WIN, P], [1, WIN]]),
                    start=(c == 0), stop=(c == NCH - 1)), sMM)
            p.wait("vector", sMM)
            p.wait("vector", sRW)
            p.step("vector", lambda v: v.tensor_copy(rowb[:, :], pS[0:1, :]), sRC)
            p.wait("sync", sRC)
            p.step("sync", lambda sy, wi=wi: sy.dma_start(
                ap(self.S_rows, ROW_RS * T + wi * WIN, [[WIN, 1], [1, WIN]]),
                rowb[:, :]), sRW, 16)
        p.wait("sync", sRW)
        p.emit()

        p = self.prog()
        sC = self.lsem(stk)
        p.step("sync", lambda sy: sy.dma_start(
            statb[:, :], ap(self.S_rows, ROW_RS * T, [[NTT, P], [1, NTT]])), sC, 16)
        p.wait("sync", sC)
        p.wait("vector", sC)
        p.fstep("vector", lambda v: v.tensor_scalar(
            statb[:, :], statb[:, :], 1.0 / D, None, ALU.mult), fV)
        tick = fV.n
        p.wait("scalar", fV, tick)
        p.step("scalar", lambda s: s.activation(statb[:, :], statb[:, :], AF.Sqrt,
                                                bias=self.cc(1e-6)), sC)
        p.wait("vector", sC)
        p.fstep("vector", lambda v: v.reciprocal(sqb[:, 0:NTT], statb[:, :]), fV)
        p.fstep("vector", lambda v: v.tensor_copy(statb[:, :], sqb[:, 0:NTT]), fV)
        p.wait("sync", fV)
        p.step("sync", lambda sy: sy.dma_start(
            ap(self.S_rows, ROW_RS * T, [[NTT, P], [1, NTT]]), statb[:, :]), sC, 16)
        p.wait("sync", sC)
        p.emit()

        p = self.prog()
        sC2 = self.lsem(stk); sU = self.lsem(stk)
        for wi in range(NW):
            p.wait("sync", sU, NCH * wi)
            p.step("sync", lambda sy, wi=wi: sy.dma_start(
                rs_b[:, :], ap(self.S_rows, ROW_RS * T + wi * WIN,
                               [[0, P], [1, WIN]])), sC2, 16)
            p.wait("vector", sC2)
            for c in range(NCH):
                p.step("vector", lambda v, c=c, wi=wi: v.tensor_tensor(
                    self.act[:, c * T + wi * WIN:c * T + (wi + 1) * WIN],
                    self.act[:, c * T + wi * WIN:c * T + (wi + 1) * WIN],
                    rs_b[:, :], ALU.mult), sU)
        p.emit()
        self.close_stage(stk)

    def stage_D0(self):
        stk = ExitStack()
        w = self.w
        sWL = self.lsem(stk); sWT = self.lsem(stk); sWE = self.lsem(stk)
        sMM = self.lsem(stk); sEV = self.lsem(stk); sXL = self.lsem(stk)
        sWF = self.lsem(stk)
        sSPh = [self.lsem(stk), self.lsem(stk)]
        lnw_b = self.lsbuf(stk, "d0_lnw", [P, D], F32)
        wraw = self.lsbuf(stk, "d0_wraw", [P, 1024], F32)
        lhsW = self.lsbuf(stk, "d0_lhs", [P, 1024], BF16)
        xt0 = self.lsbuf(stk, "d0_xt0", [P, 2 * WIN], F32)
        stg = self.lsbuf(stk, "d0_stg", [P, 2 * WIN], F32)
        pW = self.lpsum(stk, "d0_pW", [P, 1024], F32)
        pM = self.lpsum(stk, "d0_pM", [P, 2 * WIN], F32)

        p = self.prog()
        p.step("sync", lambda sy: sy.dma_start(
            lnw_b[:, :], ap(w["L0_attn_ln_w"], 0, [[0, P], [1, D]])), sWL, 16)
        for c in range(NCH):
            p.wait("sync", sWE)
            p.wait("sync", sMM)
            p.step("sync", lambda sy, c=c: sy.dma_start(
                wraw[:, :], ap(w["L0_attn_Wo"], c * P * D, [[D, P], [1, D]])), sWL, 16)
            p.wait("vector", sWL)
            p.step("vector", lambda v: v.tensor_tensor(
                wraw[:, :], wraw[:, :], lnw_b[:, :], ALU.mult), sWF)
            p.wait("tensor", sWF)
            p.wait("tensor", sWE)
            for cc in range(NCH):
                p.step("tensor", lambda t, cc=cc: t.transpose(
                    pW[:, cc * P:(cc + 1) * P], wraw[:, cc * P:(cc + 1) * P],
                    self.identF[:, :]), sWT if cc == NCH - 1 else None)
            p.wait("vector", sWT)
            p.step("vector", lambda v: v.tensor_copy(lhsW[:, 0:512], pW[:, 0:512]))
            p.step("vector", lambda v: v.tensor_copy(lhsW[:, 512:1024],
                                                     pW[:, 512:1024]), sWE)
            for wi in range(NW):
                pm = wi % 2
                o = pm * WIN
                p.wait("tensor", sWE)
                p.wait("tensor", sEV, sEV.n - 1)
                for cc in range(NCH):
                    p.step("tensor", lambda t, cc=cc, wi=wi, o=o: t.matmul(
                        ap(pM, o, [[2 * WIN, P], [1, WIN]]),
                        lhsW[:, cc * P:(cc + 1) * P],
                        self.act[:, cc * T + wi * WIN:cc * T + (wi + 1) * WIN],
                        start=(cc == 0), stop=(cc == NCH - 1)),
                        sMM if cc == NCH - 1 else None)
                p.wait("sync", sEV, sEV.n - 1)
                p.step("sync", lambda sy, c=c, wi=wi, o=o: sy.dma_start(
                    ap(xt0, o, [[2 * WIN, P], [1, WIN]]),
                    ap(self.S_xT0, c * P * T + wi * WIN, [[T, P], [1, WIN]])),
                    sXL, 16)
                p.wait("vector", sMM)
                p.wait("vector", sXL)
                p.wait("vector", sSPh[pm])
                p.step("vector", lambda v, c=c, o=o: v.scalar_tensor_tensor(
                    ap(stg, o, [[2 * WIN, P], [1, WIN]]),
                    ap(pM, o, [[2 * WIN, P], [1, WIN]]),
                    self.vc(self.C_B0, c),
                    ap(xt0, o, [[2 * WIN, P], [1, WIN]]), ALU.add, ALU.add), sEV)
                p.wait("sync", sEV)
                p.step("sync", lambda sy, c=c, wi=wi, o=o, pm=pm: sy.dma_start(
                    ap(self.S_x1a, c * P * T + wi * WIN, [[T, P], [1, WIN]]),
                    ap(stg, o, [[2 * WIN, P], [1, WIN]])), sSPh[pm], 16)
        p.wait("sync", sSPh[0])
        p.wait("sync", sSPh[1])
        p.emit()
        self.close_stage(stk)

    def stage_E0(self):
        stk = ExitStack()
        w = self.w
        sWL = self.lsem(stk); sWT = self.lsem(stk); sWE = self.lsem(stk)
        sMM = self.lsem(stk); sEV = self.lsem(stk)
        sQL = self.lsem(stk); sWF = self.lsem(stk)
        sXLh = [self.lsem(stk), self.lsem(stk)]
        sSPh = [self.lsem(stk), self.lsem(stk)]
        xqf = self.lsbuf(stk, "e0_xqf", [P, NCH * T], BF16)
        wq8 = self.lsbuf(stk, "e0_wq8", [P, 1024], I8)
        wqf = self.lsbuf(stk, "e0_wqf", [P, 1024], BF16)
        lhsQ = self.lsbuf(stk, "e0_lhsQ", [P, 1024], BF16)
        x1a = self.lsbuf(stk, "e0_x1a", [P, 2 * WIN], F32)
        xs_b = self.lsbuf(stk, "e0_xsb", [P, 2 * WIN], F32)
        stg = self.lsbuf(stk, "e0_stg", [P, 2 * WIN], F32)
        pW = self.lpsum(stk, "e0_pW", [P, 1024], BF16)
        pM = self.lpsum(stk, "e0_pM", [P, 2 * WIN], F32)

        p = self.prog()
        for c in range(NCH):
            p.step("sync", lambda sy, c=c: sy.dma_start(
                ap(xqf, c * T, [[NCH * T, P], [1, T]]),
                ap(self.S_xq0, c * P * T, [[T, P], [1, T]])), sQL, 16)
        p.wait("sync", sQL)
        for c in range(NCH):
            p.wait("sync", sWE)
            p.wait("sync", sMM)
            p.step("sync", lambda sy, c=c: sy.dma_start(
                wq8[:, :], ap(w["L0_ffn_weight_q"], c * P * D, [[D, P], [1, D]])),
                sWL, 16)
            p.wait("vector", sWL)
            p.step("vector", lambda v: v.tensor_copy(wqf[:, :], wq8[:, :]), sWF)
            p.wait("tensor", sWF)
            p.wait("tensor", sWE)
            for cc in range(NCH):
                p.step("tensor", lambda t, cc=cc: t.transpose(
                    pW[:, cc * P:(cc + 1) * P], wqf[:, cc * P:(cc + 1) * P],
                    self.identB[:, :]), sWT if cc == NCH - 1 else None)
            p.wait("vector", sWT)
            p.step("vector", lambda v: v.tensor_copy(lhsQ[:, :], pW[:, :]), sWE)
            for wi in range(NW):
                pm = wi % 2
                o = pm * WIN
                p.wait("tensor", sWE)
                p.wait("tensor", sEV, sEV.n - 1)
                for cc in range(NCH):
                    p.step("tensor", lambda t, cc=cc, wi=wi, o=o: t.matmul(
                        ap(pM, o, [[2 * WIN, P], [1, WIN]]),
                        lhsQ[:, cc * P:(cc + 1) * P],
                        ap(xqf, cc * T + wi * WIN, [[NCH * T, P], [1, WIN]]),
                        start=(cc == 0), stop=(cc == NCH - 1)),
                        sMM if cc == NCH - 1 else None)
                p.wait("sync", sEV, sEV.n - 1)
                p.step("sync", lambda sy, c=c, wi=wi, o=o: sy.dma_start(
                    ap(x1a, o, [[2 * WIN, P], [1, WIN]]),
                    ap(self.S_x1a, c * P * T + wi * WIN, [[T, P], [1, WIN]])),
                    sXLh[pm], 16)
                p.step("sync", lambda sy, wi=wi, o=o: sy.dma_start(
                    ap(xs_b, o, [[2 * WIN, P], [1, WIN]]),
                    ap(self.S_rows, ROW_XSC * T + wi * WIN, [[0, P], [1, WIN]])),
                    sXLh[pm], 16)
                p.wait("vector", sMM)
                p.wait("vector", sXLh[pm])
                p.wait("vector", sSPh[pm])
                p.step("vector", lambda v, c=c, o=o: v.scalar_tensor_tensor(
                    ap(stg, o, [[2 * WIN, P], [1, WIN]]),
                    ap(pM, o, [[2 * WIN, P], [1, WIN]]),
                    self.vc(self.C_WS, c),
                    ap(xs_b, o, [[2 * WIN, P], [1, WIN]]), ALU.mult, ALU.mult))
                p.step("vector", lambda v, o=o: v.tensor_tensor(
                    ap(stg, o, [[2 * WIN, P], [1, WIN]]),
                    ap(stg, o, [[2 * WIN, P], [1, WIN]]),
                    ap(x1a, o, [[2 * WIN, P], [1, WIN]]), ALU.add), sEV)
                p.wait("sync", sEV)
                p.step("sync", lambda sy, c=c, wi=wi, o=o: sy.dma_start(
                    ap(self.S_xT1, c * P * T + wi * WIN, [[T, P], [1, WIN]]),
                    ap(stg, o, [[2 * WIN, P], [1, WIN]])), sSPh[pm], 16)
        p.wait("sync", sSPh[0])
        p.wait("sync", sSPh[1])
        p.emit()
        self.close_stage(stk)

    def stage_A1(self):
        stk = ExitStack()
        R_SX, R_SXX, R_NMU, R_S = 3, 4, 5, 6
        xt = self.lsbuf(stk, "a1_xt", [P, 2 * WIN], F32)
        sqb = self.lsbuf(stk, "a1_sq", [P, 2 * WIN], F32)
        rowb = self.lsbuf(stk, "a1_row", [33, WIN], F32)
        statA = self.lsbuf(stk, "a1_stA", [P, NTT], F32)
        statB = self.lsbuf(stk, "a1_stB", [P, NTT], F32)
        mu_b = self.lsbuf(stk, "a1_mub", [P, WIN], F32)
        s_b = self.lsbuf(stk, "a1_sb", [P, WIN], F32)
        stg = self.lsbuf(stk, "a1_stg", [P, 2 * WIN], BF16)
        xp32 = self.lsbuf(stk, "a1_xp", [P, 2 * WIN], F32)
        pR = self.lpsum(stk, "a1_pR", [1, WIN], F32)
        pR2 = self.lpsum(stk, "a1_pR2", [1, WIN], F32)

        def stats_pass(src_from_act, row_sx, row_sxx):
            p = self.prog()
            sSQ = self.lsem(stk); sMM = self.lsem(stk)
            sRW = self.lsem(stk); sEV = self.lsem(stk); sRC = self.lsem(stk)
            sLh = [self.lsem(stk), self.lsem(stk)]
            for wi in range(NW):
                for c in range(NCH):
                    idx = wi * NCH + c
                    o = (idx % 2) * WIN
                    if src_from_act:
                        srcap = self.act[:, c * T + wi * WIN:c * T + (wi + 1) * WIN]
                    else:
                        p.wait("sync", sMM, idx - 1)
                        p.step("sync", lambda sy, c=c, wi=wi, o=o, idx=idx:
                               sy.dma_start(
                                   ap(xt, o, [[2 * WIN, P], [1, WIN]]),
                                   ap(self.S_xT1, c * P * T + wi * WIN,
                                      [[T, P], [1, WIN]])), sLh[idx % 2], 16)
                        srcap = ap(xt, o, [[2 * WIN, P], [1, WIN]])
                    p.wait("scalar", sLh[idx % 2])
                    p.wait("scalar", sMM, idx - 1)
                    p.step("scalar", lambda s, srcap=srcap, o=o: s.activation(
                        ap(sqb, o, [[2 * WIN, P], [1, WIN]]), srcap, AF.Square), sSQ)
                    p.wait("tensor", sSQ)
                    if c == 0:
                        p.wait("tensor", sRC, wi)
                    p.step("tensor", lambda t, c=c, srcap=srcap: t.matmul(
                        pR[0:1, :], self.ones_col[:, :], srcap,
                        start=(c == 0), stop=(c == NCH - 1)))
                    p.step("tensor", lambda t, c=c, o=o: t.matmul(
                        pR2[0:1, :], self.ones_col[:, :],
                        ap(sqb, o, [[2 * WIN, P], [1, WIN]]),
                        start=(c == 0), stop=(c == NCH - 1)), sMM)
                p.wait("vector", sMM)
                p.wait("vector", sRW)
                p.step("vector", lambda v: v.tensor_copy(rowb[0:1, :], pR[0:1, :]))
                p.step("vector", lambda v: v.tensor_copy(rowb[32:33, :], pR2[0:1, :]),
                       sRC)
                p.wait("sync", sRC)
                p.step("sync", lambda sy, wi=wi, row_sx=row_sx: sy.dma_start(
                    ap(self.S_rows, row_sx * T + wi * WIN, [[WIN, 1], [1, WIN]]),
                    rowb[0:1, :]), sRW, 16)
                p.step("sync", lambda sy, wi=wi, row_sxx=row_sxx: sy.dma_start(
                    ap(self.S_rows, row_sxx * T + wi * WIN, [[WIN, 1], [1, WIN]]),
                    rowb[32:33, :]), sRW, 16)
            p.wait("sync", sRW)
            p.emit()

        def row_math():
            p = self.prog()
            sC = self.lsem(stk); fV = self.lsem(stk)
            p.step("sync", lambda sy: sy.dma_start(
                statA[:, :], ap(self.S_rows, R_SX * T, [[NTT, P], [1, NTT]])), sC, 16)
            p.step("sync", lambda sy: sy.dma_start(
                statB[:, :], ap(self.S_rows, R_SXX * T, [[NTT, P], [1, NTT]])), sC, 16)
            p.wait("sync", sC)
            p.wait("vector", sC)
            p.fstep("vector", lambda v: v.tensor_scalar(
                statA[:, :], statA[:, :], 1.0 / D, None, ALU.mult), fV)
            p.fstep("vector", lambda v: v.tensor_scalar(
                statB[:, :], statB[:, :], 1.0 / D, None, ALU.mult), fV)
            p.fstep("vector", lambda v: v.scalar_tensor_tensor(
                sqb[:, 0:NTT], statA[:, :], 1.0, statA[:, :], ALU.mult, ALU.mult),
                fV)
            tick = p.fstep("vector", lambda v: v.tensor_tensor(
                statB[:, :], statB[:, :], sqb[:, 0:NTT], ALU.subtract), fV)
            p.wait("scalar", fV, tick)
            p.step("scalar", lambda s: s.activation(
                statB[:, :], statB[:, :], AF.Sqrt, bias=self.cc(1e-5)), sC)
            p.wait("vector", sC, sC.n)
            p.fstep("vector", lambda v: v.reciprocal(sqb[:, 0:NTT], statB[:, :]), fV)
            p.fstep("vector", lambda v: v.tensor_tensor(
                sqb[:, NTT:2 * NTT], statB[:, :], sqb[:, 0:NTT], ALU.mult), fV)
            p.fstep("vector", lambda v: v.tensor_scalar(
                sqb[:, NTT:2 * NTT], sqb[:, NTT:2 * NTT], -1.0, 2.0,
                ALU.mult, ALU.add), fV)
            p.fstep("vector", lambda v: v.tensor_tensor(
                sqb[:, 0:NTT], sqb[:, 0:NTT], sqb[:, NTT:2 * NTT], ALU.mult), fV)
            p.fstep("vector", lambda v: v.tensor_copy(statB[:, :], sqb[:, 0:NTT]), fV)
            p.fstep("vector", lambda v: v.tensor_scalar(
                statA[:, :], statA[:, :], -1.0, None, ALU.mult), fV)
            p.wait("sync", fV)
            p.step("sync", lambda sy: sy.dma_start(
                ap(self.S_rows, R_NMU * T, [[NTT, P], [1, NTT]]), statA[:, :]), sC, 16)
            p.step("sync", lambda sy: sy.dma_start(
                ap(self.S_rows, R_S * T, [[NTT, P], [1, NTT]]), statB[:, :]), sC, 16)
            p.wait("sync", sC)
            p.emit()

        stats_pass(False, R_SX, R_SXX)
        row_math()
        p = self.prog()
        sBC = self.lsem(stk); sSQ = self.lsem(stk); sMM = self.lsem(stk)
        sRW = self.lsem(stk); sEV = self.lsem(stk); sQT = self.lsem(stk)
        sRC = self.lsem(stk)
        sQDh = [self.lsem(stk), self.lsem(stk)]
        sLh = [self.lsem(stk), self.lsem(stk)]
        for wi in range(NW):
            p.wait("sync", sEV, sEV.n - 1)
            p.step("sync", lambda sy, wi=wi: sy.dma_start(
                mu_b[:, :], ap(self.S_rows, R_NMU * T + wi * WIN, [[0, P], [1, WIN]])),
                sBC, 16)
            p.step("sync", lambda sy, wi=wi: sy.dma_start(
                s_b[:, :], ap(self.S_rows, R_S * T + wi * WIN, [[0, P], [1, WIN]])),
                sBC, 16)
            for c in range(NCH):
                idx = wi * NCH + c
                o = (idx % 2) * WIN
                p.wait("sync", sMM, idx - 1)
                p.step("sync", lambda sy, c=c, wi=wi, o=o, idx=idx: sy.dma_start(
                    ap(xt, o, [[2 * WIN, P], [1, WIN]]),
                    ap(self.S_xT1, c * P * T + wi * WIN, [[T, P], [1, WIN]])),
                    sLh[idx % 2], 16)
                p.wait("vector", sLh[idx % 2])
                p.wait("vector", sBC)
                p.wait("vector", sSQ, idx - 1)
                p.wait("vector", sMM, idx - 1)
                av = self.act[:, c * T + wi * WIN:c * T + (wi + 1) * WIN]
                p.step("vector", lambda v, o=o: v.tensor_tensor(
                    ap(xp32, o, [[2 * WIN, P], [1, WIN]]),
                    ap(xt, o, [[2 * WIN, P], [1, WIN]]), mu_b[:, :], ALU.add))
                p.step("vector", lambda v, o=o: v.tensor_tensor(
                    ap(xp32, o, [[2 * WIN, P], [1, WIN]]),
                    ap(xp32, o, [[2 * WIN, P], [1, WIN]]), s_b[:, :], ALU.mult), sEV)
                p.step("vector", lambda v, av=av, o=o: v.tensor_copy(
                    av, ap(xp32, o, [[2 * WIN, P], [1, WIN]])))
                p.step("vector", lambda v, o=o: v.tensor_scalar(
                    ap(sqb, o, [[2 * WIN, P], [1, WIN]]),
                    ap(xp32, o, [[2 * WIN, P], [1, WIN]]), 16.0, -128.25,
                    ALU.mult, ALU.max))
                p.step("vector", lambda v, o=o: v.tensor_scalar(
                    ap(sqb, o, [[2 * WIN, P], [1, WIN]]),
                    ap(sqb, o, [[2 * WIN, P], [1, WIN]]), 127.25, R23,
                    ALU.min, ALU.add))
                p.wait("vector", sQDh[idx % 2])
                p.step("vector", lambda v, o=o: v.tensor_scalar(
                    ap(stg, o, [[2 * WIN, P], [1, WIN]]),
                    ap(sqb, o, [[2 * WIN, P], [1, WIN]]), R23, None,
                    ALU.subtract), sQT)
                p.wait("sync", sQT)
                p.step("sync", lambda sy, c=c, wi=wi, o=o, idx=idx: sy.dma_start(
                    ap(self.S_xq1, c * P * T + wi * WIN, [[T, P], [1, WIN]]),
                    ap(stg, o, [[2 * WIN, P], [1, WIN]])), sQDh[idx % 2], 16)
                p.wait("scalar", sEV)
                p.wait("scalar", sMM, idx - 1)
                p.step("scalar", lambda s, o=o: s.activation(
                    ap(xt, o, [[2 * WIN, P], [1, WIN]]),
                    ap(xp32, o, [[2 * WIN, P], [1, WIN]]), AF.Square), sSQ)
                p.wait("tensor", sSQ)
                if c == 0:
                    p.wait("tensor", sRC, wi)
                p.step("tensor", lambda t, c=c, o=o: t.matmul(
                    pR[0:1, :], self.ones_col[:, :],
                    ap(xp32, o, [[2 * WIN, P], [1, WIN]]),
                    start=(c == 0), stop=(c == NCH - 1)))
                p.step("tensor", lambda t, c=c, o=o: t.matmul(
                    pR2[0:1, :], self.ones_col[:, :],
                    ap(xt, o, [[2 * WIN, P], [1, WIN]]),
                    start=(c == 0), stop=(c == NCH - 1)), sMM)
            p.wait("vector", sMM)
            p.wait("vector", sRW)
            p.step("vector", lambda v: v.tensor_copy(rowb[0:1, :], pR[0:1, :]))
            p.step("vector", lambda v: v.tensor_copy(rowb[32:33, :], pR2[0:1, :]), sRC)
            p.wait("sync", sRC)
            p.step("sync", lambda sy, wi=wi: sy.dma_start(
                ap(self.S_rows, R_SX * T + wi * WIN, [[WIN, 1], [1, WIN]]),
                rowb[0:1, :]), sRW, 16)
            p.step("sync", lambda sy, wi=wi: sy.dma_start(
                ap(self.S_rows, R_SXX * T + wi * WIN, [[WIN, 1], [1, WIN]]),
                rowb[32:33, :]), sRW, 16)
        p.wait("sync", sRW)
        p.wait("sync", sQDh[0])
        p.wait("sync", sQDh[1])
        p.emit()
        row_math()
        p = self.prog()
        sBC = self.lsem(stk); sU = self.lsem(stk)
        for wi in range(NW):
            p.wait("sync", sU, NCH * wi)
            p.step("sync", lambda sy, wi=wi: sy.dma_start(
                mu_b[:, :], ap(self.S_rows, R_NMU * T + wi * WIN, [[0, P], [1, WIN]])),
                sBC, 16)
            p.step("sync", lambda sy, wi=wi: sy.dma_start(
                s_b[:, :], ap(self.S_rows, R_S * T + wi * WIN, [[0, P], [1, WIN]])),
                sBC, 16)
            p.wait("vector", sBC)
            for c in range(NCH):
                av = self.act[:, c * T + wi * WIN:c * T + (wi + 1) * WIN]
                p.step("vector", lambda v, av=av: v.tensor_tensor(
                    av, av, mu_b[:, :], ALU.add))
                p.step("vector", lambda v, av=av: v.tensor_tensor(
                    av, av, s_b[:, :], ALU.mult))
                p.step("vector", lambda v, av=av, c=c: v.tensor_scalar(
                    av, av, self.vc(self.C_G, c), self.vc(self.C_BE, c),
                    ALU.mult, ALU.add), sU)
        p.emit()
        self.close_stage(stk)

    def stage_B1(self):
        stk = ExitStack()
        w = self.w
        sMM = self.lsem(stk); sEV = self.lsem(stk)
        sCV = self.lsem(stk)
        sSPh = [self.lsem(stk), self.lsem(stk)]
        sWLh = [self.lsem(stk), self.lsem(stk)]
        lhsB = self.lsbuf(stk, "b1_lhs", [P, 2 * 1024], F32)
        lhsQ16 = self.lsbuf(stk, "b1_lhs16", [P, 2 * 1024], BF16)
        stg = self.lsbuf(stk, "b1_stg", [P, 2 * WIN], F32)
        pM = self.lpsum(stk, "b1_pM", [P, 2 * WIN], F32)
        DSTS = [self.S_q, self.S_k, self.S_v]

        p = self.prog()
        for co in range(24):
            lo = (co % 2) * 1024
            p.wait("sync", sCV, sCV.n - 1)
            p.step("sync", lambda sy, co=co, lo=lo: sy.dma_start(
                ap(lhsB, lo, [[2048, P], [P, NCH], [1, P]]),
                ap(w["L1_attn_w_qkv"], co * P, [[3 * D, P], [P * 3 * D, NCH], [1, P]])),
                sWLh[co % 2], 16)
            p.wait("vector", sWLh[co % 2])
            p.wait("vector", sMM, sMM.n - NW)
            p.step("vector", lambda v, lo=lo: v.tensor_copy(
                ap(lhsQ16, lo, [[2048, P], [1, 1024]]),
                ap(lhsB, lo, [[2048, P], [1, 1024]])), sCV)
            dst = DSTS[co // 8]
            cr = co % 8
            for wi in range(NW):
                o = (wi % 2) * WIN
                p.wait("tensor", sCV)
                p.wait("tensor", sEV, sEV.n - 1)
                for cc in range(NCH):
                    p.step("tensor", lambda t, cc=cc, wi=wi, o=o, lo=lo: t.matmul(
                        ap(pM, o, [[2 * WIN, P], [1, WIN]]),
                        ap(lhsQ16, lo + cc * P, [[2048, P], [1, P]]),
                        self.act[:, cc * T + wi * WIN:cc * T + (wi + 1) * WIN],
                        start=(cc == 0), stop=(cc == NCH - 1)),
                        sMM if cc == NCH - 1 else None)
                p.wait("vector", sMM)
                p.wait("vector", sSPh[wi % 2])
                p.step("vector", lambda v, o=o: v.tensor_copy(
                    ap(stg, o, [[2 * WIN, P], [1, WIN]]),
                    ap(pM, o, [[2 * WIN, P], [1, WIN]])), sEV)
                p.wait("sync", sEV)
                p.step("sync", lambda sy, dst=dst, cr=cr, wi=wi, o=o: sy.dma_start(
                    ap(dst, cr * P * T + wi * WIN, [[T, P], [1, WIN]]),
                    ap(stg, o, [[2 * WIN, P], [1, WIN]])), sSPh[wi % 2], 16)
        p.wait("sync", sSPh[0])
        p.wait("sync", sSPh[1])
        p.emit()
        self.close_stage(stk)

    def stage_INNER1(self):
        stk = ExitStack()
        bufs = {}
        for n, sh, dt in [
            ("qf", [P, 4096], F32), ("kf", [P, 4096], F32), ("vf", [P, 4096], F32),
            ("qr", [P, 4096], F32), ("kr", [P, 4096], F32),
            ("cosr", [P, 4096], F32), ("sinr", [P, 4096], F32),
            ("sktm", [P, 4096], F32), ("vtm", [P, 4096], F32),
            ("Ps", [P, WIN], F32), ("PT", [P, WIN], F32),
            ("wv", [P, 260], F32), ("memz2", [P, 1040], F32),
            ("dent", [P, 16], F32), ("stg", [P, WIN], BF16),
            ("nmx", [P, 8], F32), ("riv", [P, 8], F32),
        ]:
            bufs[n] = self.lsbuf(stk, "i1_" + n, sh, dt)
        for n in ["pb1", "pb2", "pb3"]:
            bufs[n] = self.lpsum(stk, "i1_" + n, [P, WIN], F32)
        for n in ["pb4", "pb5", "pb6", "pb7"]:
            bufs[n] = self.lpsum(stk, "i1_" + n, [P, 260], F32)

        for nm in ("sL", "sRP", "sEL", "sTP", "sTE", "sRA", "sMK", "sSD", "sSD2",
                   "sAE", "sDV", "fI", "sM1", "sM2", "sZ", "sOT"):
            bufs[nm] = self.lsem(stk)

        p = self.prog()
        memz2, wv = bufs["memz2"], bufs["wv"]
        p.step("vector", lambda v: v.memset(memz2[:, :], 0.0))
        p.step("vector", lambda v: v.memset(
            ap(memz2, 64, [[1040, 64], [130, 8], [1, 1]]), 1e-6))
        p.step("vector", lambda v: v.memset(
            ap(memz2, 64 * 1040 + 129, [[1040, 64], [130, 8], [1, 1]]), 1e-6))
        p.step("vector", lambda v: v.memset(wv[:, :], 0.0))
        p.step("vector", lambda v: v.memset(
            ap(wv, 64, [[260, P], [65, 4], [1, 1]]), -1.0))
        p.emit()

        for w in range(min(NW, I1_WINDOWS)):
            self.inner1_window(w, bufs)
        self.close_stage(stk)

    def inner1_window(self, w, B):
        qf, kf, vf = B["qf"], B["kf"], B["vf"]
        qr, kr = B["qr"], B["kr"]
        cosr, sinr = B["cosr"], B["sinr"]
        sktm, vtm = B["sktm"], B["vtm"]
        Ps, PT, wv, memz2 = B["Ps"], B["PT"], B["wv"], B["memz2"]
        dent, stg, nmx, riv = B["dent"], B["stg"], B["nmx"], B["riv"]
        pb1, pb2, pb3 = B["pb1"], B["pb2"], B["pb3"]
        pb4, pb5, pb6, pb7 = B["pb4"], B["pb5"], B["pb6"], B["pb7"]

        p = self.prog()
        (sL, sRP, sEL, sTP, sTE, sRA, sMK, sSD, sSD2, sAE, sDV, fI, sM1, sM2,
         sZ, sOT) = (B[k] for k in ("sL", "sRP", "sEL", "sTP", "sTE", "sRA",
                                    "sMK", "sSD", "sSD2", "sAE", "sDV", "fI",
                                    "sM1", "sM2", "sZ", "sOT"))
        bMK, bSD2, bDV, bZ = sMK.n, sSD2.n, sDV.n, sZ.n

        for c in range(NCH):
            for t, S in ((qf, self.S_q), (kf, self.S_k), (vf, self.S_v)):
                p.step("sync", lambda sy, t=t, S=S, c=c: sy.dma_start(
                    ap(t, c * WIN, [[4096, P], [1, WIN]]),
                    ap(S, c * P * T + w * WIN, [[T, P], [1, WIN]])), sL, 16)
        p.step("sync", lambda sy: sy.dma_start(
            cosr[:, :], ap(self.S_cos, w * WIN, [[T, P], [0, NCH], [1, WIN]])), sL, 16)
        p.step("sync", lambda sy: sy.dma_start(
            sinr[:, :], ap(self.S_sin, w * WIN, [[T, P], [0, NCH], [1, WIN]])), sL, 16)
        p.wait("sync", sL)
        for t, dstr in (((qf, qr), (kf, kr)) if I1_PHASE >= 1 else ()):
            p.step("sync", lambda sy, t=t: sy.dma_start(
                ap(sktm, 0, [[2 * 4096, 64], [1, 4096]]),
                ap(t, 4096, [[2 * 4096, 64], [1, 4096]])), sRP, 16)
            p.step("sync", lambda sy, t=t: sy.dma_start(
                ap(sktm, 4096, [[2 * 4096, 64], [1, 4096]]),
                ap(t, 0, [[2 * 4096, 64], [1, 4096]])), sRP, 16)
            p.wait("vector", sRP)
            p.step("vector", lambda v, t=t, dstr=dstr: v.tensor_tensor(
                dstr[:, :], t[:, :], cosr[:, :], ALU.mult))
            p.step("vector", lambda v: v.tensor_tensor(
                sktm[:, :], sktm[:, :], sinr[:, :], ALU.mult))
            p.step("vector", lambda v, dstr=dstr: v.tensor_tensor(
                dstr[:, :], dstr[:, :], sktm[:, :], ALU.add), sRP)
            p.wait("sync", sRP)
        for t in ((qf, kf) if I1_PHASE >= 2 else ()):
            p.step("vector", lambda v, t=t: v.tensor_scalar(
                vtm[:, :], t[:, :], 0.0, None, ALU.min), sEL)
            p.wait("scalar", sEL)
            p.step("scalar", lambda sc: sc.activation(vtm[:, :], vtm[:, :], AF.Exp),
                   sAE)
            p.wait("vector", sAE)
            p.step("vector", lambda v, t=t: v.scalar_tensor_tensor(
                t[:, :], t[:, :], 0.0, vtm[:, :], ALU.max, ALU.add), sEL)
        p.wait("tensor", sL)
        for src, dst in (((kf, sktm), (vf, vtm)) if I1_PHASE >= 3 else ()):
            for c in range(NCH):
                p.wait("tensor", sEL)
                p.wait("tensor", sTE)
                for tt in range(4):
                    p.step("tensor", lambda t, src=src, c=c, tt=tt: t.transpose(
                        pb1[:, tt * P:(tt + 1) * P],
                        ap(src, c * WIN + tt * P, [[4096, P], [1, P]]),
                        self.identF[:, :]), sTP if tt == 3 else None)
                p.wait("vector", sTP)
                p.step("vector", lambda v, dst=dst, c=c: v.tensor_copy(
                    ap(dst, c * P, [[4096, P], [1024, 4], [1, P]]),
                    ap(pb1, 0, [[WIN, P], [P, 4], [1, P]])), sTE)
        if I1_PHASE >= 3:
            p.step("vector", lambda v: v.tensor_scalar(
                sktm[:, :], sktm[:, :], -1.0, None, ALU.mult), sTE)
        p.wait("gpsimd", sTE)
        for (src, dst) in ((((sktm, sinr), (vtm, cosr))) if I1_PHASE >= 4 else ()):
            for j in range(4):
                for m in range(4):
                    h = 2 * j + (m % 2) + 8 * (m // 2)
                    for g in range(4):
                        p.step("gpsimd",
                               lambda gp, src=src, dst=dst, j=j, m=m, h=h, g=g:
                               gp.dma_start(
                                   ap(dst, j * 32 * 4096 + g * 256 + m * 64,
                                      [[4096, 32], [1024, 4], [1, 64]]),
                                   ap(src, g * 32 * 4096 + h * 64,
                                      [[4096, 32], [1024, 4], [1, 64]])), sRA, 16)
        for g in range(4 if I1_PHASE >= 5 else 0):
            for hq in range(4):
                u = g * 4 + hq
                if u >= I1_UNITS:
                    continue
                p.wait("tensor", sMK, bMK + u)
                ORD = (0, 2, 1, 3)
                for slot, hj in enumerate(ORD):
                    h = hq * 4 + hj
                    c = h // 2
                    base = (h % 2) * 64
                    if slot == 2:
                        p.wait("tensor", sSD)
                    p.step("tensor", lambda t, c=c, base=base, g=g, slot=slot:
                           t.matmul(
                               pb2[:, slot * P:(slot + 1) * P],
                               ap(qr, base * 4096 + c * WIN + g * P,
                                  [[4096, 64], [1, P]]),
                               ap(kr, base * 4096 + c * WIN + g * P,
                                  [[4096, 64], [1, P]]),
                               start=True, stop=True,
                               tile_position=(base, 0)), sSD)
                p.wait("vector", sSD)
                p.wait("vector", sTP)
                p.step("vector", lambda v: v.scalar_tensor_tensor(
                    Ps[:, :], pb2[:, :], SC_SDPA, self.mask4[:, :],
                    ALU.mult, ALU.add), sMK)
                p.fstep("vector", lambda v: v.tensor_reduce(
                    nmx[:, 0:4], ap(Ps, 0, [[WIN, P], [P, 4], [1, P]]),
                    AX, ALU.max), fI)
                p.fstep("vector", lambda v: v.tensor_scalar(
                    nmx[:, 0:4], nmx[:, 0:4], -1.0, None, ALU.mult), fI)
                p.wait("scalar", fI)
                for slot in range(4):
                    p.step("scalar", lambda sc, slot=slot: sc.activation(
                        Ps[:, slot * P:(slot + 1) * P], Ps[:, slot * P:(slot + 1) * P],
                        AF.Exp, bias=nmx[:, slot:slot + 1],
                        accum_out=riv[:, slot:slot + 1]), sAE)
                p.wait("vector", sAE)
                p.fstep("vector", lambda v: v.reciprocal(riv[:, 4:8], riv[:, 0:4]), fI)
                p.fstep("vector", lambda v: v.tensor_tensor(
                    nmx[:, 4:8], riv[:, 0:4], riv[:, 4:8], ALU.mult), fI)
                p.fstep("vector", lambda v: v.tensor_scalar(
                    nmx[:, 4:8], nmx[:, 4:8], -1.0, 2.0, ALU.mult, ALU.add), fI)
                p.fstep("vector", lambda v: v.tensor_tensor(
                    riv[:, 4:8], riv[:, 4:8], nmx[:, 4:8], ALU.mult), fI)
                p.wait("tensor", sAE)
                p.wait("tensor", sTE)
                for slot in range(4):
                    p.step("tensor", lambda t, slot=slot: t.transpose(
                        pb1[:, slot * P:(slot + 1) * P], Ps[:, slot * P:(slot + 1) * P],
                        self.identF[:, :]), sTP if slot == 3 else None)
                p.wait("vector", sTP)
                p.wait("vector", sSD2, bSD2 + 4 * u)
                p.step("vector", lambda v: v.tensor_copy(PT[:, :], pb1[:, :]), sTE)
                if w == 0 and u == 0 and "I1DBG" in DEBUG:
                    p.wait("sync", sTE)
                    p.step("sync", lambda sy: sy.dma_start(
                        ap(self.S_x2a, 0, [[T, P], [1, WIN]]), PT[:, :]), sL, 16)
                    p.step("sync", lambda sy: sy.dma_start(
                        ap(self.S_x2a, WIN, [[T, P], [1, WIN]]), Ps[:, :]), sL, 16)
                    p.step("sync", lambda sy: sy.dma_start(
                        ap(self.S_x2a, 2 * WIN, [[T, P], [1, 1024]]),
                        vtm[:, 0:1024]), sL, 16)
                    p.step("sync", lambda sy: sy.dma_start(
                        ap(self.S_x2a, 3 * WIN + 1024, [[T, P], [1, 512]]),
                        qr[:, 0:512]), sL, 16)
                    p.wait("sync", sL)
                p.wait("tensor", sTE)
                p.wait("tensor", sDV, bDV + 4 * u)
                for slot, hj in enumerate(ORD):
                    h = hq * 4 + hj
                    p.step("tensor", lambda t, h=h, slot=slot, g=g: t.matmul(
                        pb3[:, slot * 64:(slot + 1) * 64], PT[:, slot * P:(slot + 1) * P],
                        ap(vtm, g * 1024 + h * 64, [[4096, P], [1, 64]]),
                        start=True, stop=True), sSD2)
                p.wait("vector", sSD2)
                for slot, hj in enumerate(ORD):
                    h = hq * 4 + hj
                    p.step("vector", lambda v, h=h, slot=slot, g=g: v.tensor_scalar(
                        ap(vf, g * 1024 + h * 64, [[4096, P], [1, 64]]),
                        pb3[:, slot * 64:(slot + 1) * 64],
                        riv[:, 4 + slot:5 + slot], self.beta1m_r[:, h:h + 1],
                        ALU.mult, ALU.mult), sDV)
        if w == 0 and "I1DBG" in DEBUG:
            p.wait("sync", sDV)
            p.step("sync", lambda sy: sy.dma_start(
                ap(self.S_x2b, 0, [[T, P], [1, 4096]]), vf[:, :]), sL, 16)
            p.wait("sync", sL)
        p.wait("tensor", sRA)
        p.wait("vector", sRA)
        for sg in range(SPW if I1_PHASE >= 6 else 0):
            p.wait("tensor", sZ, bZ + sg)
            for pr in range(NCH):
                col = 130 * (pr // 4)
                strip = (pr % 4) * 32
                p.step("tensor", lambda t, pr=pr, sg=sg, col=col, strip=strip:
                       t.matmul(
                           ap(pb5, strip * 260 + col, [[260, 32], [1, 130]]),
                           ap(qf, pr * WIN + sg * 32, [[4096, P], [1, 32]]),
                           ap(memz2, pr * 130, [[1040, P], [1, 130]]),
                           start=True, stop=True, tile_position=(0, strip)), None)
                p.step("tensor", lambda t, pr=pr, sg=sg, col=col, strip=strip:
                       t.matmul(
                           ap(pb4, strip * 260 + col, [[260, 32], [1, 130]]),
                           ap(kf, pr * WIN + sg * 32, [[4096, P], [1, 32]]),
                           ap(memz2, pr * 130, [[1040, P], [1, 130]]),
                           start=True, stop=True, tile_position=(0, strip)),
                       sM1 if pr == NCH - 1 else None)
            p.wait("vector", sM1)
            p.fstep("vector", lambda v: v.reciprocal(
                dent[:, 0:4], ap(pb4, 64, [[260, P], [65, 4], [1, 1]])), fI)
            p.fstep("vector", lambda v: v.tensor_tensor(
                dent[:, 4:8], ap(pb4, 64, [[260, P], [65, 4], [1, 1]]),
                dent[:, 0:4], ALU.mult), fI)
            p.fstep("vector", lambda v: v.tensor_scalar(
                dent[:, 4:8], dent[:, 4:8], -1.0, 2.0, ALU.mult, ALU.add), fI)
            p.fstep("vector", lambda v: v.tensor_tensor(
                dent[:, 0:4], dent[:, 0:4], dent[:, 4:8], ALU.mult), fI)
            p.fstep("vector", lambda v: v.tensor_scalar(
                dent[:, 8:12], ap(pb5, 64, [[260, P], [65, 4], [1, 1]]),
                1e-6, None, ALU.add), fI)
            p.fstep("vector", lambda v: v.reciprocal(dent[:, 12:16], dent[:, 8:12]),
                    fI)
            p.fstep("vector", lambda v: v.tensor_tensor(
                dent[:, 4:8], dent[:, 8:12], dent[:, 12:16], ALU.mult), fI)
            p.fstep("vector", lambda v: v.tensor_scalar(
                dent[:, 4:8], dent[:, 4:8], -1.0, 2.0, ALU.mult, ALU.add), fI)
            p.fstep("vector", lambda v: v.tensor_tensor(
                dent[:, 12:16], dent[:, 12:16], dent[:, 4:8], ALU.mult), fI)
            p.fstep("vector", lambda v: v.tensor_tensor(
                dent[:, 12:16], dent[:, 12:16], self.beta_q2[:, :], ALU.mult), fI)
            for m in range(4):
                last = m == 3
                p.step("vector", lambda v, m=m, sg=sg: v.scalar_tensor_tensor(
                    wv[:, m * 65:m * 65 + 64],
                    ap(pb4, m * 65, [[260, P], [1, 64]]),
                    dent[:, m:m + 1],
                    ap(cosr, sg * 256 + m * 64, [[4096, P], [1, 64]]),
                    ALU.mult, ALU.subtract), fI)
            wv_tick = fI.n
            for m in range(4):
                p.step("vector", lambda v, m=m, sg=sg: v.tensor_scalar(
                    ap(kr, sg * 256 + m * 64, [[4096, P], [1, 64]]),
                    ap(pb5, m * 65, [[260, P], [1, 64]]),
                    dent[:, 12 + m:13 + m], None, ALU.mult), sDV)
            p.wait("tensor", fI, wv_tick)
            for j in range(4):
                if j > 0:
                    p.wait("tensor", sM2)
                for pr in (j, j + 4):
                    for e in range(2):
                        h = 2 * pr + e
                        m = 2 * (pr // 4) + e
                        pb = pb6 if pr < 4 else pb7
                        pcol = 65 * (pr % 4)
                        p.step("tensor", lambda t, pr=pr, e=e, j=j, m=m, sg=sg,
                               pb=pb, pcol=pcol: t.matmul(
                                   ap(pb, e * 64 * 260 + pcol, [[260, 64], [1, 65]]),
                                   ap(sinr, j * 32 * 4096 + sg * 256 + m * 64,
                                      [[4096, 32], [1, 64]]),
                                   ap(wv, j * 32 * 260 + m * 65, [[260, 32], [1, 65]]),
                                   start=True, stop=True,
                                   tile_position=(j * 32, e * 64)), sM2)
            p.wait("vector", sM2)
            for pb, base in ((pb6, 0), (pb7, 4)):
                for e in range(2):
                    p.step("vector", lambda v, pb=pb, base=base, e=e: v.tensor_tensor(
                        ap(memz2, e * 64 * 1040 + base * 130 + e * 65,
                           [[1040, 64], [130, 4], [1, 65]]),
                        ap(memz2, e * 64 * 1040 + base * 130 + e * 65,
                           [[1040, 64], [130, 4], [1, 65]]),
                        ap(pb, e * 64 * 260, [[260, 64], [65, 4], [1, 65]]),
                        ALU.add), sZ if (pb is pb7 and e == 1) else None)
        p.wait("gpsimd", sZ)
        for j in range(4 if I1_PHASE >= 7 else 0):
            for m in range(4):
                h = 2 * j + (m % 2) + 8 * (m // 2)
                for g in range(4):
                    p.step("gpsimd",
                           lambda gp, j=j, m=m, h=h, g=g: gp.dma_start(
                               ap(sktm, g * 32 * 4096 + h * 64,
                                  [[4096, 32], [1024, 4], [1, 64]]),
                               ap(kr, j * 32 * 4096 + g * 256 + m * 64,
                                  [[4096, 32], [1024, 4], [1, 64]])), sRA, 16)
        p.wait("vector", sRA)
        p.wait("vector", sDV)
        if I1_PHASE >= 7:
            p.step("vector", lambda v: v.tensor_tensor(
                vf[:, :], vf[:, :], sktm[:, :], ALU.add), sDV)
        p.wait("tensor", sDV)
        for c in range(NCH if I1_PHASE >= 5.5 else 0):
            p.wait("tensor", sTE)
            for tt in range(4):
                p.step("tensor", lambda t, c=c, tt=tt: t.transpose(
                    pb1[:, tt * P:(tt + 1) * P],
                    ap(vf, tt * 1024 + c * P, [[4096, P], [1, P]]),
                    self.identF[:, :]), sTP if tt == 3 else None)
            p.wait("vector", sTP)
            p.wait("vector", sOT)
            p.step("vector", lambda v: v.tensor_copy(stg[:, :], pb1[:, :]), sTE)
            p.wait("sync", sTE)
            p.step("sync", lambda sy, c=c: sy.dma_start(
                ap(self.S_outsT, c * P * T + w * WIN, [[T, P], [1, WIN]]),
                stg[:, :]), sOT, 16)
        p.wait("sync", sOT)
        p.wait("sync", sRA)
        p.emit()

    def stage_OUT1(self):
        stk = ExitStack()
        w = self.w
        sMM = self.lsem(stk); sEV = self.lsem(stk)
        sXL = self.lsem(stk); sAL = self.lsem(stk); sCV = self.lsem(stk)
        sSPh = [self.lsem(stk), self.lsem(stk)]
        sWLh = [self.lsem(stk), self.lsem(stk)]
        af = self.lsbuf(stk, "o1_af", [P, NCH * T], BF16)
        lhsO = self.lsbuf(stk, "o1_lhs", [P, 2 * 1024], F32)
        lhsO16 = self.lsbuf(stk, "o1_lhs16", [P, 2 * 1024], BF16)
        x1t = self.lsbuf(stk, "o1_x1t", [P, 2 * WIN], F32)
        stg = self.lsbuf(stk, "o1_stg", [P, 2 * WIN], F32)
        pM = self.lpsum(stk, "o1_pM", [P, 2 * WIN], F32)

        p = self.prog()
        for c in range(NCH):
            p.step("sync", lambda sy, c=c: sy.dma_start(
                ap(af, c * T, [[NCH * T, P], [1, T]]),
                ap(self.S_outsT, c * P * T, [[T, P], [1, T]])), sAL, 16)
        p.wait("sync", sAL)
        for co in range(NCH):
            lo = (co % 2) * 1024
            p.wait("sync", sCV, sCV.n - 1)
            p.step("sync", lambda sy, co=co, lo=lo: sy.dma_start(
                ap(lhsO, lo, [[2048, P], [P, NCH], [1, P]]),
                ap(w["L1_attn_w_out"], co * P, [[D, P], [P * D, NCH], [1, P]])),
                sWLh[co % 2], 16)
            p.wait("vector", sWLh[co % 2])
            p.wait("vector", sMM, sMM.n - NW)
            p.step("vector", lambda v, lo=lo: v.tensor_copy(
                ap(lhsO16, lo, [[2048, P], [1, 1024]]),
                ap(lhsO, lo, [[2048, P], [1, 1024]])), sCV)
            for wi in range(NW):
                o = (wi % 2) * WIN
                p.wait("tensor", sCV)
                p.wait("tensor", sEV, sEV.n - 1)
                for cc in range(NCH):
                    p.step("tensor", lambda t, cc=cc, wi=wi, o=o, lo=lo: t.matmul(
                        ap(pM, o, [[2 * WIN, P], [1, WIN]]),
                        ap(lhsO16, lo + cc * P, [[2048, P], [1, P]]),
                        af[:, cc * T + wi * WIN:cc * T + (wi + 1) * WIN],
                        start=(cc == 0), stop=(cc == NCH - 1)),
                        sMM if cc == NCH - 1 else None)
                p.wait("sync", sEV, sEV.n - 1)
                p.step("sync", lambda sy, co=co, wi=wi, o=o: sy.dma_start(
                    ap(x1t, o, [[2 * WIN, P], [1, WIN]]),
                    ap(self.S_xT1, co * P * T + wi * WIN, [[T, P], [1, WIN]])),
                    sXL, 16)
                p.wait("vector", sMM)
                p.wait("vector", sXL)
                p.wait("vector", sSPh[wi % 2])
                p.step("vector", lambda v, co=co, o=o: v.scalar_tensor_tensor(
                    ap(stg, o, [[2 * WIN, P], [1, WIN]]),
                    ap(pM, o, [[2 * WIN, P], [1, WIN]]),
                    self.vc(self.C_B2Q, co),
                    ap(x1t, o, [[2 * WIN, P], [1, WIN]]), ALU.add, ALU.add), sEV)
                p.wait("sync", sEV)
                p.step("sync", lambda sy, co=co, wi=wi, o=o: sy.dma_start(
                    ap(self.S_x2a, co * P * T + wi * WIN, [[T, P], [1, WIN]]),
                    ap(stg, o, [[2 * WIN, P], [1, WIN]])), sSPh[wi % 2], 16)
        p.wait("sync", sSPh[0])
        p.wait("sync", sSPh[1])
        p.emit()
        self.close_stage(stk)

    def stage_F1(self):
        stk = ExitStack()
        w = self.w
        xqf = self.lsbuf(stk, "f1_xqf", [P, NCH * T], BF16)
        hh = self.lsbuf(stk, "f1_hh", [P, NCH * T], BF16)
        wraw = self.lsbuf(stk, "f1_wraw", [P, 1024], F32)
        lhsQ = self.lsbuf(stk, "f1_lhs", [P, 1024], BF16)
        xpt = self.lsbuf(stk, "f1_xpt", [P, 2 * WIN], F32)
        et = self.lsbuf(stk, "f1_et", [P, 2 * WIN], F32)
        evq = self.lsbuf(stk, "f1_evq", [P, 2 * WIN], F32)
        stg = self.lsbuf(stk, "f1_stg", [P, 2 * WIN], F32)
        pM = self.lpsum(stk, "f1_pM", [P, 2 * WIN], F32)
        pW = self.lpsum(stk, "f1_pW", [P, 1024], F32)
        pWB = self.lpsum(stk, "f1_pWB", [P, 1024], BF16)
        pT = self.lpsum(stk, "f1_pT", [P, WIN], F32)
        sQL = self.lsem(stk); sWL = self.lsem(stk); sWF = self.lsem(stk); sWT = self.lsem(stk)
        sWE = self.lsem(stk); sMM = self.lsem(stk); sAE = self.lsem(stk); sHW = self.lsem(stk)
        sXL = self.lsem(stk); sEV = self.lsem(stk); sTP = self.lsem(stk)
        sFC = self.lsem(stk); sAQ = self.lsem(stk)
        sSPh = [self.lsem(stk), self.lsem(stk)]
        sODh = [self.lsem(stk), self.lsem(stk)]

        p = self.prog()
        for c in range(NCH):
            p.step("sync", lambda sy, c=c: sy.dma_start(
                ap(xqf, c * T, [[NCH * T, P], [1, T]]),
                ap(self.S_xq1, c * P * T, [[T, P], [1, T]])), sQL, 16)
        p.wait("sync", sQL)
        for q in range(F1_QUARTERS):
            src_prev = self.S_x2a if q % 2 == 0 else self.S_x2b
            dst_part = self.S_x2b if q % 2 == 0 else self.S_x2a
            final = q == 3
            for ho in range(NCH if F1_W1 else 0):
                hc = q * NCH + ho
                p.wait("sync", sWE)
                p.wait("sync", sMM)
                p.step("sync", lambda sy, hc=hc: sy.dma_start(
                    wraw[:, :], ap(w["L1_ffn_w1"], hc * P * D, [[D, P], [1, D]])),
                    sWL, 16)
                p.wait("vector", sWL)
                p.step("vector", lambda v: v.tensor_scalar(
                    wraw[:, :], wraw[:, :], 64.0, -128.25, ALU.mult, ALU.max))
                p.step("vector", lambda v: v.tensor_scalar(
                    wraw[:, :], wraw[:, :], 127.25, R23, ALU.min, ALU.add))
                p.step("vector", lambda v: v.tensor_scalar(
                    wraw[:, :], wraw[:, :], R23, None, ALU.subtract), sWF)
                p.wait("tensor", sWF)
                p.wait("tensor", sWE)
                for cc in range(NCH):
                    p.step("tensor", lambda t, cc=cc: t.transpose(
                        pW[:, cc * P:(cc + 1) * P], wraw[:, cc * P:(cc + 1) * P],
                        self.identF[:, :]), sWT if cc == NCH - 1 else None)
                p.wait("vector", sWT)
                p.step("vector", lambda v: v.tensor_copy(lhsQ[:, 0:512], pW[:, 0:512]))
                p.step("vector", lambda v: v.tensor_copy(
                    lhsQ[:, 512:1024], pW[:, 512:1024]), sWE)
                for wi in range(NW):
                    o = (wi % 2) * WIN
                    p.wait("tensor", sWE)
                    p.wait("tensor", sHW, sHW.n - 1)
                    for cc in range(NCH):
                        p.step("tensor", lambda t, cc=cc, wi=wi, o=o: t.matmul(
                            ap(pM, o, [[2 * WIN, P], [1, WIN]]),
                            lhsQ[:, cc * P:(cc + 1) * P],
                            ap(xqf, cc * T + wi * WIN, [[NCH * T, P], [1, WIN]]),
                            start=(cc == 0), stop=(cc == NCH - 1)),
                            sMM if cc == NCH - 1 else None)
                    p.wait("scalar", sMM)
                    p.step("scalar", lambda sc, hc=hc, o=o: sc.activation(
                        ap(et, o, [[2 * WIN, P], [1, WIN]]),
                        ap(pM, o, [[2 * WIN, P], [1, WIN]]), AF.Relu,
                        bias=self.vc(self.C_B1Q, hc), scale=self.cc(1.0 / 64.0)),
                        sAE)
                    p.wait("vector", sAE)
                    p.step("vector", lambda v, o=o: v.tensor_scalar(
                        ap(et, o, [[2 * WIN, P], [1, WIN]]),
                        ap(et, o, [[2 * WIN, P], [1, WIN]]), 127.25, R23,
                        ALU.min, ALU.add))
                    p.step("vector", lambda v, ho=ho, wi=wi, o=o: v.tensor_scalar(
                        ap(hh, ho * T + wi * WIN, [[NCH * T, P], [1, WIN]]),
                        ap(et, o, [[2 * WIN, P], [1, WIN]]), R23, None,
                        ALU.subtract), sHW)
            for co in range(F1_W2 if q == 0 else NCH):
                p.wait("sync", sWE)
                p.wait("sync", sMM)
                p.step("sync", lambda sy, co=co, q=q: sy.dma_start(
                    wraw[:, :], ap(w["L1_ffn_w2"], co * P * 4 * D + q * 1024,
                                   [[4 * D, P], [1, 1024]])), sWL, 16)
                p.wait("vector", sWL)
                p.step("vector", lambda v: v.tensor_scalar(
                    wraw[:, :], wraw[:, :], 64.0, -128.25, ALU.mult, ALU.max))
                p.step("vector", lambda v: v.tensor_scalar(
                    wraw[:, :], wraw[:, :], 127.25, R23, ALU.min, ALU.add))
                p.step("vector", lambda v: v.tensor_scalar(
                    wraw[:, :], wraw[:, :], R23, None, ALU.subtract), sWF)
                p.wait("tensor", sWF)
                p.wait("tensor", sWE)
                for cc in range(NCH):
                    p.step("tensor", lambda t, cc=cc: t.transpose(
                        pW[:, cc * P:(cc + 1) * P], wraw[:, cc * P:(cc + 1) * P],
                        self.identF[:, :]), sWT if cc == NCH - 1 else None)
                p.wait("vector", sWT)
                p.step("vector", lambda v: v.tensor_copy(lhsQ[:, 0:512], pW[:, 0:512]))
                p.step("vector", lambda v: v.tensor_copy(
                    lhsQ[:, 512:1024], pW[:, 512:1024]), sWE)
                for wi in range(F1_WIN):
                    o = 0 if F1_BIS & 8 else (wi % 2) * WIN
                    opm = 0 if F1_BIS & 16 else o
                    p.wait("tensor", sWE)
                    p.wait("tensor", sAQ, sAQ.n - 1)
                    for hc in range(NCH):
                        p.step("tensor", lambda t, hc=hc, wi=wi, opm=opm: t.matmul(
                            ap(pM, opm, [[2 * WIN, P], [1, WIN]]),
                            lhsQ[:, hc * P:(hc + 1) * P],
                            ap(hh, hc * T + wi * WIN, [[NCH * T, P], [1, WIN]]),
                            start=(hc == 0), stop=(hc == NCH - 1)),
                            sMM if hc == NCH - 1 else None)
                    p.wait("sync", sEV, sEV.n - 1)
                    p.step("sync", lambda sy, co=co, wi=wi, o=o, src=src_prev:
                           sy.dma_start(
                               ap(xpt, o, [[2 * WIN, P], [1, WIN]]),
                               ap(src, co * P * T + wi * WIN, [[T, P], [1, WIN]])),
                           sXL, 16)
                    p.wait("scalar", sMM)
                    p.wait("scalar", sEV, sEV.n - 1)
                    p.step("scalar", lambda sc, o=o, opm=opm: sc.activation(
                        ap(evq, o, [[2 * WIN, P], [1, WIN]]),
                        ap(pM, opm, [[2 * WIN, P], [1, WIN]]), AF.Copy,
                        bias=0.0, scale=self.cc(1.0 / 1024.0)), sAQ)
                    p.wait("vector", sAQ)
                    p.wait("vector", sXL)
                    if final:
                        p.wait("vector", sTP)
                    else:
                        p.wait("vector", sSPh[wi % 2])
                    p.step("vector", lambda v, o=o: v.tensor_tensor(
                        ap(stg, o, [[2 * WIN, P], [1, WIN]]),
                        ap(evq, o, [[2 * WIN, P], [1, WIN]]),
                        ap(xpt, o, [[2 * WIN, P], [1, WIN]]), ALU.add), sEV)
                    if not final:
                        p.wait("sync", sEV)
                        p.step("sync", lambda sy, co=co, wi=wi, o=o, dst=dst_part:
                               sy.dma_start(
                                   ap(dst, co * P * T + wi * WIN, [[T, P], [1, WIN]]),
                                   ap(stg, o, [[2 * WIN, P], [1, WIN]])), sSPh[wi % 2], 16)
                    else:
                        p.wait("tensor", sEV)
                        p.wait("tensor", sFC, sFC.n - 1)
                        for tt in range(4):
                            p.step("tensor", lambda t, tt=tt, o=o: t.transpose(
                                pT[:, tt * P:(tt + 1) * P],
                                ap(stg, o + tt * P, [[2 * WIN, P], [1, P]]),
                                self.identF[:, :]), sTP if tt == 3 else None)
                        p.wait("vector", sTP)
                        p.wait("vector", sODh[wi % 2])
                        p.step("vector", lambda v, o=o: v.tensor_copy(
                            ap(et, o, [[2 * WIN, P], [1, WIN]]), pT[:, :]), sFC)
                        p.wait("sync", sFC)
                        p.step("sync", lambda sy, co=co, wi=wi, o=o: sy.dma_start(
                            ap(self.out, wi * WIN * D + co * P,
                               [[D, P], [P * D, 4], [1, P]]),
                            ap(et, o, [[2 * WIN, P], [P, 4], [1, P]])), sODh[wi % 2], 16)
        p.wait("sync", sODh[0])
        p.wait("sync", sODh[1])
        p.wait("sync", sSPh[0])
        p.wait("sync", sSPh[1])
        p.emit()
        self.close_stage(stk)

    def stage_PROBE(self):
        stk = ExitStack()
        w = self.w
        if PR_MODE & 4:
            xqf = self.lsbuf(stk, "pr_xqf", [P, NCH * T], BF16)
        hh = self.lsbuf(stk, "pr_hh", [P, NCH * T], BF16)
        wraw = self.lsbuf(stk, "pr_wraw", [P, 1024], F32)
        lhsQ = self.lsbuf(stk, "pr_lhsQ", [P, 1024], BF16)
        xpt = self.lsbuf(stk, "pr_xpt", [P, 2 * WIN], F32)
        evq = self.lsbuf(stk, "pr_evq", [P, 2 * WIN], F32)
        stg = self.lsbuf(stk, "pr_stg", [P, 2 * WIN], F32)
        if PR_MODE & 8:
            pM = self.lpsum(stk, "pr_pM", [P, 2 * WIN], F32)
            pW = self.lpsum(stk, "pr_pW", [P, 1024], F32)
            pWB = self.lpsum(stk, "pr_pWB", [P, 1024], BF16)
            pT = self.lpsum(stk, "pr_pT", [P, WIN], F32)
        else:
            pW = self.lpsum(stk, "pr_pW", [P, 1024], F32)
            pM = self.lpsum(stk, "pr_pM", [P, 2 * WIN], F32)
        sWT = self.lsem(stk); sWE = self.lsem(stk); sMM = self.lsem(stk)
        sAQ = self.lsem(stk); sEV = self.lsem(stk); sSP = self.lsem(stk)
        sIN = self.lsem(stk); sWL = self.lsem(stk); sWF = self.lsem(stk)
        sXL = self.lsem(stk); sQL = self.lsem(stk)
        p = self.prog()
        if PR_MODE & 4:
            for c in range(NCH):
                p.step("sync", lambda sy, c=c: sy.dma_start(
                    ap(xqf, c * T, [[NCH * T, P], [1, T]]),
                    ap(self.S_xq1, c * P * T, [[T, P], [1, T]])), sQL, 16)
            p.wait("sync", sQL)
        if PR_MODE & 2:
            p.step("sync", lambda sy: sy.dma_start(
                wraw[:, :], ap(w["L1_ffn_w2"], 0, [[4 * D, P], [1, 1024]])),
                sWL, 16)
            p.wait("vector", sWL)
            p.step("vector", lambda v: v.tensor_scalar(
                wraw[:, :], wraw[:, :], 64.0, -128.25, ALU.mult, ALU.max))
            p.step("vector", lambda v: v.tensor_scalar(
                wraw[:, :], wraw[:, :], 127.25, R23, ALU.min, ALU.add))
            p.step("vector", lambda v: v.tensor_scalar(
                wraw[:, :], wraw[:, :], R23, None, ALU.subtract), sWF)
            p.wait("tensor", sWF)
        else:
            p.step("vector", lambda v: v.memset(wraw[:, :], 0.125))
        p.step("vector", lambda v: v.memset(xpt[:, :], 0.25))
        for c in range(NCH):
            p.step("vector", lambda v, c=c: v.memset(
                hh[:, c * T:(c + 1) * T], 0.5), sIN if c == NCH - 1 else None)
        p.wait("tensor", sIN)
        for cc in range(NCH):
            p.step("tensor", lambda t, cc=cc: t.transpose(
                pW[:, cc * P:(cc + 1) * P], wraw[:, cc * P:(cc + 1) * P],
                self.identF[:, :]), sWT if cc == NCH - 1 else None)
        p.wait("vector", sWT)
        p.step("vector", lambda v: v.tensor_copy(lhsQ[:, 0:512], pW[:, 0:512]))
        p.step("vector", lambda v: v.tensor_copy(
            lhsQ[:, 512:1024], pW[:, 512:1024]), sWE)
        for wi in range(NW):
            o = (wi % 2) * WIN
            p.wait("tensor", sWE)
            p.wait("tensor", sAQ, sAQ.n - 1)
            for cc in range(NCH):
                p.step("tensor", lambda t, cc=cc, wi=wi, o=o: t.matmul(
                    ap(pM, o, [[2 * WIN, P], [1, WIN]]),
                    lhsQ[:, cc * P:(cc + 1) * P],
                    ap(hh, cc * T + wi * WIN, [[NCH * T, P], [1, WIN]]),
                    start=(cc == 0), stop=(cc == NCH - 1)),
                    sMM if cc == NCH - 1 else None)
            if PR_MODE & 1:
                p.wait("sync", sEV, sEV.n - 1)
                p.step("sync", lambda sy, wi=wi, o=o: sy.dma_start(
                    ap(xpt, o, [[2 * WIN, P], [1, WIN]]),
                    ap(self.S_x2a, wi * WIN, [[T, P], [1, WIN]])), sXL, 16)
            p.wait("scalar", sMM)
            p.wait("scalar", sEV, sEV.n - 1)
            p.step("scalar", lambda sc, o=o: sc.activation(
                ap(evq, o, [[2 * WIN, P], [1, WIN]]),
                ap(pM, o, [[2 * WIN, P], [1, WIN]]), AF.Copy,
                bias=0.0, scale=self.cc(1.0 / 1024.0)), sAQ)
            p.wait("vector", sAQ)
            if PR_MODE & 1:
                p.wait("vector", sXL)
            p.wait("vector", sSP, sSP.n - 16)
            p.step("vector", lambda v, o=o: v.tensor_tensor(
                ap(stg, o, [[2 * WIN, P], [1, WIN]]),
                ap(evq, o, [[2 * WIN, P], [1, WIN]]),
                ap(xpt, o, [[2 * WIN, P], [1, WIN]]), ALU.add), sEV)
            p.wait("sync", sEV)
            p.step("sync", lambda sy, wi=wi, o=o: sy.dma_start(
                ap(self.S_x2b, wi * WIN, [[T, P], [1, WIN]]),
                ap(stg, o, [[2 * WIN, P], [1, WIN]])), sSP, 16)
        p.wait("sync", sSP)
        p.emit()
        self.close_stage(stk)

    def dump_act(self):
        s = self.sem()
        p = self.prog()
        for c in range(NCH):
            p.step("sync", lambda sy, c=c: sy.dma_start(
                ap(self.S_act, c * P * T, [[T, P], [1, T]]),
                self.act[:, c * T:(c + 1) * T]), s, 16)
        p.wait("sync", s)
        p.emit()


def build_program():
    return K().build()


def kernel(**inputs):
    nc = build_program()
    x = np.ascontiguousarray(np.asarray(inputs["L0_attn_x"], dtype=np.float32))
    base = {}
    for kk, v in inputs.items():
        if kk == "L0_attn_x":
            continue
        a = np.asarray(v)
        if kk == "L1_attn_mem_beta":
            a = a.reshape(HEADS)
        base[kk] = np.ascontiguousarray(a)
    in_maps = []
    for bb in range(8):
        m = dict(base)
        m["x"] = x[bb]
        in_maps.append(m)
    res = run_bass_kernel_spmd(nc, in_maps, core_ids=list(range(8)))
    return np.stack([r["out"] for r in res.results], axis=0)


# version 11
# speedup vs baseline: 1.0041x; 1.0041x over previous
import numpy as np
from contextlib import ExitStack

import concourse.bass as bass
import concourse.mybir as mybir
from concourse.bass_utils import run_bass_kernel_spmd

F32 = mybir.dt.float32
BF16 = mybir.dt.bfloat16
I8 = mybir.dt.int8
I32 = mybir.dt.int32
AF = mybir.ActivationFunctionType
ALU = mybir.AluOpType
AX = mybir.AxisListType.X

P = 128
T = 4096
D = 1024
NCH = D // P
WIN = 512
NW = T // WIN
NTT = T // P
SEG = 32
NSEG = T // SEG
SPW = WIN // SEG
HEADS = 16
DH = 64
R23 = float(1.5 * 2.0 ** 23)
PI = 3.14159265358979323846
TWO_PI_HI = 6.28125
TWO_PI_LO = 0.0019353071795864769
LN1E10 = 23.025850929940457
LN1E4 = 9.210340371976184
SC_SDPA = DH ** -0.5

DEBUG = set()
STOP_AFTER = None
STAGE_LIST = None
I1_PHASE = 9
I1_WINDOWS = NW
I1_UNITS = 16
F1_QUARTERS = 4
F1_W2 = NCH
F1_BIS = 0
F1_W1 = True
F1_WIN = NW
C0_SER = False
PR_MODE = 0

ROW_XSC, ROW_RS, ROW_MU1, ROW_S1, ROW_MU2, ROW_S2 = range(6)

ENGS = ("sync", "tensor", "vector", "scalar", "gpsimd")


def ap(t, off, dims):
    return bass.AP(t, off, dims)


class Sem:
    __slots__ = ("h", "n", "name")

    def __init__(self, handle, name):
        self.h = handle
        self.n = 0
        self.name = name


class Prog:
    def __init__(self, nc):
        self.nc = nc
        self.q = {e: [] for e in ENGS}

    def wait(self, eng, sem, val=None):
        v = sem.n if val is None else val
        if v < 0:
            return
        self.q[eng].append(lambda e, h=sem.h, v=v: e.wait_ge(h, v))

    def step(self, eng, fn, sem=None, k=1):
        if sem is not None:
            sem.n += k
            self.q[eng].append(lambda e, f=fn, h=sem.h, k=k: f(e).then_inc(h, k))
        else:
            self.q[eng].append(lambda e, f=fn: f(e))

    def fstep(self, eng, fn, fsem):
        fsem.n += 1
        v = fsem.n
        self.q[eng].append(lambda e, f=fn, h=fsem.h, v=v:
                           (f(e).then_inc(h, 1), e.wait_ge(h, v)))
        return v

    def emit(self):
        with self.nc.Block() as blk:
            for eng in ENGS:
                fns = self.q[eng]
                if not fns:
                    continue
                getattr(blk, eng)(lambda e, fns=fns: [f(e) for f in fns])


class K:
    def __init__(self):
        self.nc = bass.Bass("TRN2", target_bir_lowering=False)
        self.ctx = ExitStack()
        self._ns = 0
        self._stage_sems = {}

    def sbuf(self, name, shape, dtype=F32):
        return self.ctx.enter_context(self.nc.sbuf_tensor(name, shape, dtype))

    def psum(self, name, shape, dtype=F32):
        return self.ctx.enter_context(self.nc.psum_tensor(name, shape, dtype))

    def lsbuf(self, stack, name, shape, dtype=F32):
        return stack.enter_context(self.nc.sbuf_tensor(name, shape, dtype))

    def lpsum(self, stack, name, shape, dtype=F32):
        return stack.enter_context(self.nc.psum_tensor(name, shape, dtype))

    def sem(self, name=None):
        self._ns += 1
        nm = name or f"sm{self._ns}"
        return Sem(self.ctx.enter_context(self.nc.semaphore(nm)), nm)

    def lsem(self, stack, name=None):
        self._ns += 1
        nm = name or f"sm{self._ns}"
        s = Sem(stack.enter_context(self.nc.semaphore(nm)), nm)
        self._stage_sems.setdefault(id(stack), []).append(s)
        return s

    def close_stage(self, stack):
        sems = self._stage_sems.pop(id(stack), [])
        if sems:
            with self.nc.Block() as blk:
                @blk.gpsimd
                def _(g, sems=sems):
                    for s in sems:
                        g.wait_ge(s.h, s.n)
                    for s in sems:
                        g.sem_clear(s.h)
        stack.close()

    def prog(self):
        return Prog(self.nc)

    def build(self):
        nc = self.nc
        self.x_in = nc.dram_tensor("x", [T, D], F32, kind="ExternalInput")
        self.out = nc.dram_tensor("out", [T, D], F32, kind="ExternalOutput")
        self.w = {}
        for n, sh, dt in [
            ("L0_attn_Wr", [D, D], F32), ("L0_attn_Wk", [D, D], F32),
            ("L0_attn_Wv", [D, D], F32), ("L0_attn_Wo", [D, D], F32),
            ("L0_attn_decay", [D], F32), ("L0_attn_ln_w", [D], F32),
            ("L0_ffn_weight_q", [D, D], I8), ("L0_ffn_weight_scale", [D], F32),
            ("L0_ffn_bias", [D], F32),
            ("L1_attn_w_qkv", [D, 3 * D], F32), ("L1_attn_w_out", [D, D], F32),
            ("L1_attn_ln_g", [D], F32), ("L1_attn_ln_b", [D], F32),
            ("L1_attn_mem_beta", [HEADS], F32),
            ("L1_ffn_w1", [4 * D, D], F32), ("L1_ffn_b1", [4 * D], F32),
            ("L1_ffn_w2", [D, 4 * D], F32), ("L1_ffn_b2", [D], F32),
        ]:
            self.w[n] = nc.dram_tensor(n, sh, dt, kind="ExternalInput")

        def scratch(name, shape, dtype=F32):
            kind = "ExternalOutput" if name in DEBUG else None
            t = (nc.dram_tensor(name, shape, dtype, kind=kind) if kind
                 else nc.dram_tensor(name, shape, dtype))
            setattr(self, name, t)

        scratch("S_xT0", [D, T]); scratch("S_xq0", [D, T], BF16)
        scratch("S_r", [D, T]); scratch("S_state", [D, T])
        scratch("S_x1a", [D, T]); scratch("S_xT1", [D, T])
        scratch("S_xq1", [D, T], BF16)
        scratch("S_q", [D, T]); scratch("S_k", [D, T]); scratch("S_v", [D, T])
        scratch("S_outsT", [D, T], BF16); scratch("S_x2a", [D, T]); scratch("S_x2b", [D, T])
        scratch("S_rows", [8, T]); scratch("S_cos", [P, T]); scratch("S_sin", [P, T])
        scratch("S_act", [D, T], BF16)

        stages = ["prep", "A0", "B0", "C0", "D0", "E0", "A1", "B1", "INNER1",
                  "OUT1", "F1"]
        if STAGE_LIST is not None:
            stages = list(STAGE_LIST)
        elif STOP_AFTER is not None:
            stages = stages[:stages.index(STOP_AFTER) + 1]
        with self.ctx:
            self.persistent()
            act_open = True
            for st in stages:
                if st in ("INNER1", "OUT1", "F1") and act_open \
                        and "S_act" not in DEBUG:
                    self.act_stk.close()
                    act_open = False
                getattr(self, "stage_" + st)()
            if "S_act" in DEBUG:
                self.dump_act()
            if act_open:
                self.act_stk.close()
        return self.nc

    def persistent(self):
        self.identF = self.sbuf("identF", [P, P], F32)
        self.identB = self.sbuf("identB", [P, P], BF16)
        self.ones_col = self.sbuf("ones_col", [P, 1], F32)
        self.vecT = self.sbuf("vecT", [P, 104], F32)
        (self.C_DEC, self.C_WS, self.C_B0, self.C_G, self.C_BE, self.C_B2Q,
         self.C_LD, self.C_NLD) = 0, 8, 16, 24, 32, 40, 48, 96
        self.C_B1Q = 56
        self.beta_r = self.sbuf("beta_r", [P, HEADS], F32)
        self.beta1m_r = self.sbuf("beta1m_r", [P, HEADS], F32)
        self.xsc_pm = self.sbuf("xsc_pm", [P, NTT], F32)
        self.cpool = self.sbuf("cpool", [P, 5], F32)
        self.maskc = self.sbuf("maskc", [P, P], F32)
        self.mask4 = self.sbuf("mask4", [P, WIN], F32)
        self.beta_q2 = self.sbuf("beta_q2", [P, 4], F32)
        self.CONSTS = {1e-5: 0, 1e-6: 1, 1.0 / 64.0: 2, 1.0 / 16.0: 3,
                       1.0 / 1024.0: 4}
        self.act_stk = ExitStack()
        self.act = self.act_stk.enter_context(
            self.nc.sbuf_tensor("act", [P, NCH * T], BF16))

    def vc(self, c0, c=0, n=1):
        return self.vecT[:, c0 + c:c0 + c + n]

    def cc(self, val):
        col = self.CONSTS[val]
        return self.cpool[:, col:col + 1]

    def stage_prep(self):
        w = self.w
        stk = ExitStack()
        sD = self.lsem(stk)
        sC = self.lsem(stk)
        ioA = self.lsbuf(stk, "ioA", [P, P], F32)
        ioB = self.lsbuf(stk, "ioB", [P, P], F32)
        vload = self.lsbuf(stk, "vload", [80, P], F32)
        pP = self.lpsum(stk, "pp_ps", [P, P], F32)

        p = self.prog()
        p.step("gpsimd", lambda g: g.iota(ioA[:, :], [[1, P]], channel_multiplier=0,
                                          allow_small_or_imprecise_dtypes=True))
        p.step("gpsimd", lambda g: g.iota(ioB[:, :], [[0, P]], channel_multiplier=1,
                                          allow_small_or_imprecise_dtypes=True))
        p.step("vector", lambda v: v.memset(self.ones_col[:, :], 1.0))
        for val, col in self.CONSTS.items():
            p.step("vector", lambda v, val=val, col=col: v.memset(
                self.cpool[:, col:col + 1], val))
        names = ["L0_attn_decay", "L0_ffn_weight_scale", "L0_ffn_bias",
                 "L1_attn_ln_g", "L1_attn_ln_b", "L1_ffn_b2"]
        for vi, name in enumerate(names):
            p.step("sync", lambda sy, vi=vi, name=name: sy.dma_start(
                vload[vi * NCH:(vi + 1) * NCH, :],
                ap(w[name], 0, [[P, NCH], [1, P]])), sD, 16)
        p.step("sync", lambda sy: sy.dma_start(
            vload[48:80, :], ap(w["L1_ffn_b1"], 0, [[P, 32], [1, P]])), sD, 16)
        p.step("sync", lambda sy: sy.dma_start(
            self.beta_r[:, :], ap(w["L1_attn_mem_beta"], 0, [[0, P], [1, HEADS]])),
            sD, 16)
        p.wait("sync", sD)
        p.emit()

        p = self.prog()
        p.step("vector", lambda v: v.tensor_tensor(self.identF[:, :], ioA[:, :],
                                                   ioB[:, :], ALU.is_equal), sC)
        p.wait("vector", sC)
        p.step("vector", lambda v: v.tensor_copy(self.identB[:, :], self.identF[:, :]))
        p.emit()

        p = self.prog()
        p.step("tensor", lambda t: t.transpose(pP[:, 0:80], vload[0:80, :],
                                               self.identF[0:80, 0:80]), sC)
        p.wait("vector", sC)
        p.step("vector", lambda v: v.tensor_copy(self.vecT[:, 0:48], pP[:, 0:48]))
        p.step("vector", lambda v: v.tensor_copy(
            self.vecT[:, self.C_B1Q:self.C_B1Q + 32], pP[:, 48:80]))
        p.emit()

        p = self.prog()
        fP = self.sem()
        p.step("scalar", lambda s: s.activation(self.vc(self.C_DEC, 0, 8),
                                                self.vc(self.C_DEC, 0, 8),
                                                AF.Tanh, scale=0.5), sC)
        p.step("scalar", lambda s: s.activation(self.beta_r[:, :], self.beta_r[:, :],
                                                AF.Tanh, scale=0.5), sC)
        p.wait("vector", sC)
        p.fstep("vector", lambda v: v.tensor_scalar(self.vc(self.C_DEC, 0, 8),
                                                    self.vc(self.C_DEC, 0, 8),
                                                    0.5, 0.5, ALU.mult, ALU.add), fP)
        p.fstep("vector", lambda v: v.tensor_scalar(self.beta_r[:, :], self.beta_r[:, :],
                                                    0.5, 0.5, ALU.mult, ALU.add), fP)
        p.fstep("vector", lambda v: v.tensor_scalar(self.beta1m_r[:, :],
                                                    self.beta_r[:, :],
                                                    -1.0, 1.0, ALU.mult, ALU.add), fP)
        p.fstep("vector", lambda v: v.tensor_scalar(self.vc(self.C_LD, 0, 8),
                                                    self.vc(self.C_DEC, 0, 8),
                                                    1e-7, None, ALU.max), fP)
        for c0, n, post in ((self.C_B2Q, 8, 1.0 / 64.0), (self.C_B1Q, 32, 0.25)):
            p.fstep("vector", lambda v, c0=c0, n=n: v.tensor_scalar(
                self.vc(c0, 0, n), self.vc(c0, 0, n), 64.0, -128.25, ALU.mult,
                ALU.max), fP)
            p.fstep("vector", lambda v, c0=c0, n=n: v.tensor_scalar(
                self.vc(c0, 0, n), self.vc(c0, 0, n), 127.25, R23, ALU.min,
                ALU.add), fP)
            p.fstep("vector", lambda v, c0=c0, n=n, post=post: v.tensor_scalar(
                self.vc(c0, 0, n), self.vc(c0, 0, n), R23, post, ALU.subtract,
                ALU.mult), fP)
        p.emit()

        p = self.prog()
        p.step("scalar", lambda s: s.activation(self.vc(self.C_LD, 0, 8),
                                                self.vc(self.C_LD, 0, 8), AF.Ln), sC)
        p.wait("vector", sC)
        p.fstep("vector", lambda v: v.tensor_scalar(self.vc(self.C_NLD, 0, 8),
                                                    self.vc(self.C_LD, 0, 8),
                                                    -1.0, None, ALU.mult), fP)
        p.emit()
        self._fP = fP

        self.prep_mask()
        self.prep_trig(sD, stk)
        self.close_stage(stk)

    def prep_mask(self):
        stk = ExitStack()
        ioR = self.lsbuf(stk, "pm_ioR", [P, P], I32)
        ioC = self.lsbuf(stk, "pm_ioC", [P, P], I32)
        t1 = self.lsbuf(stk, "pm_t1", [P, P], I32)
        t2 = self.lsbuf(stk, "pm_t2", [P, P], I32)
        fP = self._fP
        p = self.prog()
        p.step("gpsimd", lambda g: g.iota(ioC[:, :], [[1, P]], channel_multiplier=0))
        p.step("gpsimd", lambda g: g.iota(ioR[:, :], [[0, P]], channel_multiplier=1))
        p.emit()
        p = self.prog()
        p.fstep("vector", lambda v: v.tensor_scalar(
            t1[:, :], ioR[:, :], 5, None, ALU.arith_shift_right), fP)
        p.fstep("vector", lambda v: v.tensor_scalar(
            t2[:, :], ioC[:, :], 5, None, ALU.arith_shift_right), fP)
        p.fstep("vector", lambda v: v.tensor_tensor(
            t1[:, :], t1[:, :], t2[:, :], ALU.is_equal), fP)
        p.fstep("vector", lambda v: v.tensor_scalar(
            ioR[:, :], ioR[:, :], 31, None, ALU.bitwise_and), fP)
        p.fstep("vector", lambda v: v.tensor_scalar(
            ioC[:, :], ioC[:, :], 31, None, ALU.bitwise_and), fP)
        p.fstep("vector", lambda v: v.tensor_tensor(
            t2[:, :], ioC[:, :], ioR[:, :], ALU.is_le), fP)
        p.fstep("vector", lambda v: v.tensor_tensor(
            t1[:, :], t1[:, :], t2[:, :], ALU.bitwise_and), fP)
        p.fstep("vector", lambda v: v.tensor_copy(self.maskc[:, :], t1[:, :]), fP)
        p.fstep("vector", lambda v: v.tensor_scalar(
            self.maskc[:, :], self.maskc[:, :], 1e30, -1e30, ALU.mult, ALU.add), fP)
        for j in range(4):
            for m in range(4):
                h = 2 * j + (m % 2) + 8 * (m // 2)
                p.fstep("vector", lambda v, j=j, m=m, h=h: v.tensor_copy(
                    self.beta_q2[32 * j:32 * (j + 1), m:m + 1],
                    self.beta_r[32 * j:32 * (j + 1), h:h + 1]), fP)
        for rep in range(4):
            p.step("vector", lambda v, rep=rep: v.tensor_copy(
                self.mask4[:, rep * P:(rep + 1) * P], self.maskc[:, :]))
        p.emit()
        self.close_stage(stk)

    def prep_trig(self, sD, stk):
        frq = self.lsbuf(stk, "frq", [64, 1], F32)
        sgn = self.lsbuf(stk, "sgn", [64, 1], F32)
        pint = self.lsbuf(stk, "pint", [64, 2], I32)
        angb = self.lsbuf(stk, "angb", [64, 2048], F32)
        redb = self.lsbuf(stk, "redb", [64, 2048], F32)
        red2 = self.lsbuf(stk, "red2", [64, 2048], F32)
        kb = self.lsbuf(stk, "kb", [64, 2048], F32)
        outb = self.lsbuf(stk, "outb", [64, 2048], F32)
        sC = self.lsem(stk)
        fP = self._fP
        INV2PI = 1.0 / (TWO_PI_HI + TWO_PI_LO)

        p = self.prog()
        p.step("gpsimd", lambda g: g.iota(pint[:, 0:1], [[0, 1]], channel_multiplier=1))
        p.emit()
        p = self.prog()
        for i in range(32):
            val = float(np.float32(10000.0 ** (-2.0 * i / 64.0)))
            p.step("vector", lambda v, i=i, val=val: v.memset(
                kb[0:1, i:i + 1], val), sC)
        p.wait("vector", sC)
        p.step("vector", lambda v: v.tensor_copy(kb[0:1, 32:64], kb[0:1, 0:32]), sC)
        p.wait("sync", sC)
        p.step("sync", lambda sy: sy.dma_start(
            ap(self.S_rows, 6 * T, [[32, 1], [1, 32]]), kb[0:1, 0:32]), sD, 16)
        p.wait("sync", sD)
        p.step("sync", lambda sy: sy.dma_start(
            frq[:, 0:1], ap(self.S_rows, 6 * T, [[1, 32], [0, 2], [1, 1]])), sD, 16)
        p.wait("sync", sD)
        p.fstep("vector", lambda v: v.tensor_scalar(pint[:, 1:2], pint[:, 0:1], 1,
                                                    None, ALU.bitwise_and), fP)
        p.fstep("vector", lambda v: v.tensor_copy(sgn[:, :], pint[:, 1:2]), fP)
        p.fstep("vector", lambda v: v.tensor_scalar(sgn[:, :], sgn[:, :],
                                                    2.0, -1.0, ALU.mult, ALU.add), fP)
        p.emit()

        for half in range(2):
            t0 = half * 2048
            p = self.prog()
            p.step("gpsimd", lambda g, t0=t0: g.iota(
                angb[:, :], [[1, 2048]], base=t0, channel_multiplier=0,
                allow_small_or_imprecise_dtypes=True), sC)
            p.wait("vector", sC)
            p.fstep("vector", lambda v: v.tensor_scalar(angb[:, :], angb[:, :],
                                                        frq[:, 0:1], None, ALU.mult),
                    fP)
            p.emit()
            for kind in ("cos", "sin"):
                qoff = 0.25 if kind == "cos" else 0.0
                coff = PI / 2.0 if kind == "cos" else 0.0
                p = self.prog()
                p.fstep("vector", lambda v, qoff=qoff: v.tensor_scalar(
                    kb[:, :], angb[:, :], INV2PI, qoff, ALU.mult, ALU.add), fP)
                p.fstep("vector", lambda v: v.tensor_scalar(
                    kb[:, :], kb[:, :], R23, R23, ALU.add, ALU.subtract), fP)
                p.fstep("vector", lambda v: v.scalar_tensor_tensor(
                    red2[:, :], kb[:, :], -TWO_PI_HI, angb[:, :], ALU.mult, ALU.add),
                    fP)
                p.fstep("vector", lambda v: v.scalar_tensor_tensor(
                    redb[:, :], kb[:, :], -TWO_PI_LO, red2[:, :], ALU.mult, ALU.add),
                    fP)
                p.fstep("vector", lambda v, coff=coff: v.tensor_scalar(
                    redb[:, :], redb[:, :], coff, None, ALU.add), fP)
                p.step("vector", lambda v: v.tensor_scalar(
                    redb[:, :], redb[:, :], 3.1415925, -3.1415925,
                    ALU.min, ALU.max), sC)
                p.wait("scalar", sC)
                p.step("scalar", lambda s: s.activation(outb[:, :], redb[:, :],
                                                        AF.Sin), sC)
                if kind == "sin":
                    p.wait("vector", sC)
                    p.step("vector", lambda v: v.tensor_scalar(
                        outb[:, :], outb[:, :], sgn[:, 0:1], None, ALU.mult), sC)
                p.wait("sync", sC)
                dst = self.S_cos if kind == "cos" else self.S_sin
                for rb in range(2):
                    p.step("sync", lambda sy, dst=dst, rb=rb, t0=t0: sy.dma_start(
                        ap(dst, rb * 64 * T + t0, [[T, 64], [1, 2048]]),
                        outb[:, :]), sD, 16)
                p.wait("sync", sD)
                p.emit()

    def stage_A0(self):
        stk = ExitStack()
        sA = self.lsem(stk); sV = self.lsem(stk)
        sT = self.lsem(stk); sGX = self.lsem(stk); sGN = self.lsem(stk)
        fV = self.lsem(stk)
        sLh = [self.lsem(stk), self.lsem(stk)]
        sOX = [self.lsem(stk), self.lsem(stk)]
        sOQ = [self.lsem(stk), self.lsem(stk)]
        xtm = self.lsbuf(stk, "a0_xtm", [P, 2 * 1024], F32)
        sqb = self.lsbuf(stk, "a0_sqb", [P, 1024], F32)
        xnb = self.lsbuf(stk, "a0_xnb", [P, 2 * 1024], F32)
        xqb = self.lsbuf(stk, "a0_xqb", [P, 2 * 1024], BF16)
        stgX = self.lsbuf(stk, "a0_stgX", [P, 2 * 1024], F32)
        stgQ = self.lsbuf(stk, "a0_stgQ", [P, 2 * 1024], BF16)
        stt = self.lsbuf(stk, "a0_stt", [P, 8], F32)
        rowstg = self.lsbuf(stk, "a0_row", [32, P], F32)
        pA = self.lpsum(stk, "a0_pA", [P, 1024], F32)
        pB = self.lpsum(stk, "a0_pB", [P, 1024], F32)
        pQ = self.lpsum(stk, "a0_pQ", [P, 1024], BF16)
        C_MU, C_S, C_NMS, C_QS, C_T0, C_T1, C_T2, C_T3 = range(8)

        p = self.prog()
        for i in range(NTT):
            o = (i % 2) * 1024
            xt = ap(xtm, o, [[2048, P], [1, 1024]])

            p.wait("sync", sT, 3 * (i - 1))
            p.wait("sync", sV, i - 1)
            p.wait("sync", sA, 2 * (i - 1))
            p.step("sync", lambda sy, i=i, o=o: sy.dma_start(
                ap(xtm, o, [[2048, P], [1, 1024]]),
                ap(self.x_in, i * P * D, [[D, P], [1, D]])), sLh[i % 2], 16)
            if i >= 1:
                j = i - 1
                jo = (j % 2) * 1024
                p.wait("sync", sGX, 2 * j + 1)
                p.step("sync", lambda sy, j=j, jo=jo: sy.dma_start(
                    ap(self.S_xT0, j * P, [[T, P], [P * T, NCH], [1, P]]),
                    ap(stgX, jo, [[2048, P], [P, NCH], [1, P]])), sOX[j % 2], 16)
                p.wait("sync", sGX, 2 * j + 2)
                p.step("sync", lambda sy, j=j, jo=jo: sy.dma_start(
                    ap(self.S_xq0, j * P, [[T, P], [P * T, NCH], [1, P]]),
                    ap(stgQ, jo, [[2048, P], [P, NCH], [1, P]])), sOQ[j % 2], 16)

            p.wait("scalar", sLh[i % 2])
            p.step("scalar", lambda s, xt=xt: s.activation(sqb[:, :], xt, AF.Square),
                   sA)
            act_sqrt_slot = len(p.q["scalar"])
            p.step("scalar", lambda s: s.activation(
                stt[:, C_S:C_S + 1], stt[:, C_T2:C_T2 + 1], AF.Sqrt,
                bias=self.cc(1e-5)), sA)

            p.wait("vector", sA, 2 * i + 1)
            p.fstep("vector", lambda v, xt=xt: v.reduce_sum(
                stt[:, C_T0:C_T0 + 1], xt, AX), fV)
            p.fstep("vector", lambda v: v.reduce_sum(
                stt[:, C_T1:C_T1 + 1], sqb[:, :], AX), fV)
            p.fstep("vector", lambda v: v.tensor_scalar(
                stt[:, C_MU:C_MU + 1], stt[:, C_T0:C_T0 + 1], 1.0 / D, None,
                ALU.mult), fV)
            p.fstep("vector", lambda v: v.tensor_tensor(
                stt[:, C_T2:C_T2 + 1], stt[:, C_MU:C_MU + 1], stt[:, C_MU:C_MU + 1],
                ALU.mult), fV)
            var_tick = p.fstep("vector", lambda v: v.scalar_tensor_tensor(
                stt[:, C_T2:C_T2 + 1], stt[:, C_T1:C_T1 + 1], 1.0 / D,
                stt[:, C_T2:C_T2 + 1], ALU.mult, ALU.subtract), fV)
            p.q["scalar"].insert(act_sqrt_slot,
                                 lambda e, h=fV.h, v=var_tick: e.wait_ge(h, v))
            p.wait("vector", sA, 2 * i + 2)
            p.fstep("vector", lambda v: v.reciprocal(stt[:, C_T3:C_T3 + 1],
                                                     stt[:, C_S:C_S + 1]), fV)
            p.fstep("vector", lambda v: v.tensor_tensor(
                stt[:, C_T0:C_T0 + 1], stt[:, C_S:C_S + 1], stt[:, C_T3:C_T3 + 1],
                ALU.mult), fV)
            p.fstep("vector", lambda v: v.tensor_scalar(
                stt[:, C_T0:C_T0 + 1], stt[:, C_T0:C_T0 + 1], -1.0, 2.0,
                ALU.mult, ALU.add), fV)
            p.fstep("vector", lambda v: v.tensor_tensor(
                stt[:, C_T3:C_T3 + 1], stt[:, C_T3:C_T3 + 1], stt[:, C_T0:C_T0 + 1],
                ALU.mult), fV)
            p.fstep("vector", lambda v: v.tensor_tensor(
                stt[:, C_NMS:C_NMS + 1], stt[:, C_MU:C_MU + 1], stt[:, C_T3:C_T3 + 1],
                ALU.mult), fV)
            p.fstep("vector", lambda v: v.tensor_scalar(
                stt[:, C_NMS:C_NMS + 1], stt[:, C_NMS:C_NMS + 1], -1.0, None,
                ALU.mult), fV)
            if i >= 2:
                p.wait("vector", sT, 3 * (i - 2) + 2)
                p.wait("vector", sT, 3 * (i - 2) + 3)
            p.step("vector", lambda v, xt=xt, o=o: v.tensor_scalar(
                xnb[:, o:o + 1024], xt, stt[:, C_T3:C_T3 + 1],
                stt[:, C_NMS:C_NMS + 1], ALU.mult, ALU.add))
            p.fstep("vector", lambda v, o=o: v.reduce_max(
                stt[:, C_T0:C_T0 + 1], xnb[:, o:o + 1024], AX,
                apply_absolute_value=True), fV)
            p.fstep("vector", lambda v: v.tensor_scalar(
                stt[:, C_T0:C_T0 + 1], stt[:, C_T0:C_T0 + 1], 1e-6, None, ALU.max), fV)
            p.fstep("vector", lambda v: v.reciprocal(stt[:, C_QS:C_QS + 1],
                                                     stt[:, C_T0:C_T0 + 1]), fV)
            p.fstep("vector", lambda v: v.tensor_tensor(
                stt[:, C_T1:C_T1 + 1], stt[:, C_T0:C_T0 + 1], stt[:, C_QS:C_QS + 1],
                ALU.mult), fV)
            p.fstep("vector", lambda v: v.tensor_scalar(
                stt[:, C_T1:C_T1 + 1], stt[:, C_T1:C_T1 + 1], -1.0, 2.0,
                ALU.mult, ALU.add), fV)
            p.fstep("vector", lambda v: v.tensor_tensor(
                stt[:, C_QS:C_QS + 1], stt[:, C_QS:C_QS + 1], stt[:, C_T1:C_T1 + 1],
                ALU.mult), fV)
            p.fstep("vector", lambda v, i=i: v.tensor_scalar(
                self.xsc_pm[:, i:i + 1], stt[:, C_T0:C_T0 + 1], 1.0 / 127.0,
                None, ALU.mult), fV)
            p.fstep("vector", lambda v: v.tensor_scalar(
                stt[:, C_QS:C_QS + 1], stt[:, C_QS:C_QS + 1], 127.0, None,
                ALU.mult), fV)
            p.step("vector", lambda v, o=o: v.tensor_scalar(
                sqb[:, :], xnb[:, o:o + 1024], stt[:, C_QS:C_QS + 1], R23,
                ALU.mult, ALU.add))
            p.step("vector", lambda v, o=o: v.tensor_scalar(
                xqb[:, o:o + 1024], sqb[:, :], R23, None, ALU.subtract), sV)

            p.wait("tensor", sLh[i % 2])
            if i >= 1:
                p.wait("tensor", sGX, 2 * (i - 1) + 1)
            for c in range(NCH):
                last = c == NCH - 1
                p.step("tensor", lambda t, o=o, c=c: t.transpose(
                    pA[:, c * P:(c + 1) * P], ap(xtm, o + c * P, [[2048, P], [1, P]]),
                    self.identF[:, :]), sT if last else None)
            p.wait("tensor", sV, i + 1)
            p.wait("tensor", sGN, i)
            for c in range(NCH):
                last = c == NCH - 1
                p.step("tensor", lambda t, o=o, c=c: t.transpose(
                    pB[:, c * P:(c + 1) * P], ap(xnb, o + c * P, [[2048, P], [1, P]]),
                    self.identF[:, :]), sT if last else None)
            p.wait("tensor", sV, i + 1)
            if i >= 1:
                p.wait("tensor", sGX, 2 * i)
            for c in range(NCH):
                last = c == NCH - 1
                p.step("tensor", lambda t, o=o, c=c: t.transpose(
                    pQ[:, c * P:(c + 1) * P], ap(xqb, o + c * P, [[2048, P], [1, P]]),
                    self.identB[:, :]), sT if last else None)

            p.wait("scalar", sT, 3 * i + 1)
            if i >= 2:
                p.wait("scalar", sOX[i % 2])
            p.step("scalar", lambda sc, o=o: sc.copy(
                ap(stgX, o, [[2048, P], [1, 512]]), pA[:, 0:512]))
            p.step("scalar", lambda sc, o=o: sc.copy(
                ap(stgX, o + 512, [[2048, P], [1, 512]]), pA[:, 512:1024]), sGX)
            p.wait("scalar", sT, 3 * i + 3)
            if i >= 2:
                p.wait("scalar", sOQ[i % 2])
            p.step("scalar", lambda sc, o=o: sc.copy(
                ap(stgQ, o, [[2048, P], [1, 1024]]), pQ[:, :]), sGX)
            p.wait("vector", sT, 3 * i + 2)
            p.step("vector", lambda v, i=i: v.tensor_copy(
                ap(self.act, i * P, [[NCH * T, P], [T, 4], [1, P]]),
                ap(pB, 0, [[1024, P], [P, 4], [1, P]])))
            p.step("vector", lambda v, i=i: v.tensor_copy(
                ap(self.act, 4 * T + i * P, [[NCH * T, P], [T, 4], [1, P]]),
                ap(pB, 512, [[1024, P], [P, 4], [1, P]])), sGN)

        j = NTT - 1
        jo = (j % 2) * 1024
        p.wait("sync", sGX, 2 * j + 1)
        p.step("sync", lambda sy: sy.dma_start(
            ap(self.S_xT0, j * P, [[T, P], [P * T, NCH], [1, P]]),
            ap(stgX, jo, [[2048, P], [P, NCH], [1, P]])), sOX[j % 2], 16)
        p.wait("sync", sGX, 2 * j + 2)
        p.step("sync", lambda sy: sy.dma_start(
            ap(self.S_xq0, j * P, [[T, P], [P * T, NCH], [1, P]]),
            ap(stgQ, jo, [[2048, P], [P, NCH], [1, P]])), sOQ[j % 2], 16)
        for s in sOX + sOQ:
            p.wait("sync", s)
        p.emit()

        p = self.prog()
        sC = self.lsem(stk)
        p.step("tensor", lambda t: t.transpose(pA[0:32, 0:P], self.xsc_pm[:, :],
                                               self.identF[:, :]), sC)
        p.wait("vector", sC)
        p.step("vector", lambda v: v.tensor_copy(rowstg[:, :], pA[0:32, 0:P]), sC)
        p.wait("sync", sC)
        p.step("sync", lambda sy: sy.dma_start(
            ap(self.S_rows, ROW_XSC * T, [[P, 32], [1, P]]), rowstg[:, :]), sC, 16)
        p.wait("sync", sC)
        p.emit()
        self.close_stage(stk)

    def stage_B0(self):
        stk = ExitStack()
        w = self.w
        sWL = self.lsem(stk); sWT = self.lsem(stk); sWE = self.lsem(stk)
        sMM = self.lsem(stk); sEV = self.lsem(stk); sRM = self.lsem(stk)
        sSP = self.lsem(stk); sAC = self.lsem(stk); sEXP = self.lsem(stk); sIO = self.lsem(stk)
        sRSP = [self.lsem(stk), self.lsem(stk)]
        wraw = self.lsbuf(stk, "b0_wraw", [P, 1024], F32)
        lhsW = self.lsbuf(stk, "b0_lhs", [P, 1024], BF16)
        kf = self.lsbuf(stk, "b0_kf", [P, T], F32)
        uf = self.lsbuf(stk, "b0_uf", [P, T], F32)
        wp = self.lsbuf(stk, "b0_wp", [P, 2 * WIN], F32)
        ramp = self.lsbuf(stk, "b0_ramp", [P, 2 * WIN], F32)
        rstg = self.lsbuf(stk, "b0_rstg", [P, 2 * WIN], F32)
        pW = self.lpsum(stk, "b0_pW", [P, 1024], F32)
        pM = self.lpsum(stk, "b0_pM", [P, 2 * WIN], F32)

        WSRC = {"k": w["L0_attn_Wk"], "v": w["L0_attn_Wv"], "r": w["L0_attn_Wr"]}

        p = self.prog()
        for c in range(NCH):
            for kind in ("k", "v", "r"):
                p.wait("sync", sWE)
                p.wait("sync", sMM)
                p.step("sync", lambda sy, src=WSRC[kind], c=c: sy.dma_start(
                    wraw[:, :], ap(src, c * P * D, [[D, P], [1, D]])), sWL, 16)
                p.wait("tensor", sWL)
                p.wait("tensor", sWE)
                for cc in range(NCH):
                    p.step("tensor", lambda t, cc=cc: t.transpose(
                        pW[:, cc * P:(cc + 1) * P], wraw[:, cc * P:(cc + 1) * P],
                        self.identF[:, :]), sWT if cc == NCH - 1 else None)
                p.wait("vector", sWT)
                p.step("vector", lambda v: v.tensor_copy(lhsW[:, 0:512], pW[:, 0:512]))
                p.step("vector", lambda v: v.tensor_copy(lhsW[:, 512:1024],
                                                         pW[:, 512:1024]), sWE)

                if kind == "v":
                    p.wait("vector", sSP)
                    p.wait("vector", sRSP[0])
                    p.wait("vector", sRSP[1])
                for wi in range(NW):
                    pm = (wi % 2)
                    p.wait("tensor", sWE)
                    p.wait("tensor", sEV, sEV.n - 1)
                    for cc in range(NCH):
                        p.step("tensor", lambda t, cc=cc, c=c, wi=wi, pm=pm: t.matmul(
                            pM[:, pm * WIN:(pm + 1) * WIN],
                            lhsW[:, cc * P:(cc + 1) * P],
                            self.act[:, cc * T + wi * WIN:cc * T + (wi + 1) * WIN],
                            start=(cc == 0), stop=(cc == NCH - 1)),
                            sMM if cc == NCH - 1 else None)
                    if kind == "k":
                        p.wait("vector", sMM)
                        p.step("vector", lambda v, wi=wi, pm=pm: v.tensor_copy(
                            kf[:, wi * WIN:(wi + 1) * WIN],
                            pM[:, pm * WIN:(pm + 1) * WIN]), sEV)
                    elif kind == "v":
                        p.wait("gpsimd", sEV, sEV.n - 1)
                        p.step("gpsimd", lambda g, wi=wi, pm=pm: g.iota(
                            ramp[:, pm * WIN:(pm + 1) * WIN], [[1, WIN]],
                            base=wi * WIN, channel_multiplier=0,
                            allow_small_or_imprecise_dtypes=True), sIO)
                        p.wait("vector", sIO)
                        p.step("vector", lambda v, c=c, pm=pm: v.tensor_scalar(
                            wp[:, pm * WIN:(pm + 1) * WIN],
                            ramp[:, pm * WIN:(pm + 1) * WIN],
                            self.vc(self.C_NLD, c), LN1E10, ALU.mult, ALU.min), sRM)
                        p.wait("scalar", sRM)
                        p.step("scalar", lambda s, pm=pm: s.activation(
                            wp[:, pm * WIN:(pm + 1) * WIN],
                            wp[:, pm * WIN:(pm + 1) * WIN], AF.Exp), sEXP)
                        p.wait("vector", sMM)
                        p.step("vector", lambda v, wi=wi, pm=pm: v.tensor_tensor(
                            uf[:, wi * WIN:(wi + 1) * WIN],
                            pM[:, pm * WIN:(pm + 1) * WIN],
                            kf[:, wi * WIN:(wi + 1) * WIN], ALU.mult))
                        p.wait("vector", sEXP)
                        p.step("vector", lambda v, wi=wi, pm=pm: v.tensor_tensor(
                            uf[:, wi * WIN:(wi + 1) * WIN],
                            uf[:, wi * WIN:(wi + 1) * WIN],
                            wp[:, pm * WIN:(pm + 1) * WIN], ALU.mult), sEV)
                    else:
                        p.wait("scalar", sMM)
                        p.wait("scalar", sRSP[pm])
                        p.step("scalar", lambda s, pm=pm: s.activation(
                            rstg[:, pm * WIN:(pm + 1) * WIN],
                            pM[:, pm * WIN:(pm + 1) * WIN], AF.Tanh, scale=0.5), sAC)
                        p.wait("vector", sAC)
                        p.step("vector", lambda v, pm=pm: v.tensor_scalar(
                            rstg[:, pm * WIN:(pm + 1) * WIN],
                            rstg[:, pm * WIN:(pm + 1) * WIN],
                            0.5, 0.5, ALU.mult, ALU.add), sEV)
                        p.wait("sync", sEV)
                        p.step("sync", lambda sy, c=c, wi=wi, pm=pm: sy.dma_start(
                            ap(self.S_r, c * P * T + wi * WIN, [[T, P], [1, WIN]]),
                            rstg[:, pm * WIN:(pm + 1) * WIN]), sRSP[pm], 16)

                if kind == "v":
                    p.wait("vector", sEV)
                    if "B0DBG" in DEBUG:
                        p.wait("sync", sEV)
                        for wi2 in range(NW):
                            p.step("sync", lambda sy, c=c, wi2=wi2: sy.dma_start(
                                ap(self.S_x1a, c * P * T + wi2 * WIN, [[T, P], [1, WIN]]),
                                uf[:, wi2 * WIN:(wi2 + 1) * WIN]), sSP, 16)
                        for wi2 in range(NW):
                            p.step("sync", lambda sy, c=c, wi2=wi2: sy.dma_start(
                                ap(self.S_xT1, c * P * T + wi2 * WIN, [[T, P], [1, WIN]]),
                                kf[:, wi2 * WIN:(wi2 + 1) * WIN]), sSP, 16)
                        p.step("sync", lambda sy, c=c: sy.dma_start(
                            ap(self.S_rows, 7 * T + c * WIN, [[1, 1], [1, WIN]]),
                            wp[0:1, 0:WIN]), sSP, 16)
                    p.step("vector", lambda v: v.tensor_tensor_scan(
                        kf[:, :], uf[:, :], uf[:, :], 0.0, ALU.add, ALU.bypass), sEV)
                    if "B0DBG" in DEBUG:
                        p.wait("sync", sEV)
                        for wi2 in range(NW):
                            p.step("sync", lambda sy, c=c, wi2=wi2: sy.dma_start(
                                ap(self.S_x2a, c * P * T + wi2 * WIN, [[T, P], [1, WIN]]),
                                kf[:, wi2 * WIN:(wi2 + 1) * WIN]), sSP, 16)
                        p.wait("vector", sSP)
                    for wi in range(NW):
                        pm = wi % 2
                        p.wait("gpsimd", sEV)
                        p.wait("gpsimd", sAC, sAC.n - 1)
                        p.step("gpsimd", lambda g, wi=wi, pm=pm: g.iota(
                            ramp[:, pm * WIN:(pm + 1) * WIN], [[1, WIN]],
                            base=wi * WIN, channel_multiplier=0,
                            allow_small_or_imprecise_dtypes=True), sIO)
                        p.wait("vector", sIO)
                        p.step("vector", lambda v, c=c, pm=pm: v.tensor_scalar(
                            wp[:, pm * WIN:(pm + 1) * WIN],
                            ramp[:, pm * WIN:(pm + 1) * WIN],
                            self.vc(self.C_LD, c), -103.0, ALU.mult, ALU.max), sRM)
                        p.wait("scalar", sRM)
                        p.step("scalar", lambda s, pm=pm: s.activation(
                            wp[:, pm * WIN:(pm + 1) * WIN],
                            wp[:, pm * WIN:(pm + 1) * WIN], AF.Exp), sAC)
                        p.wait("vector", sAC)
                        p.step("vector", lambda v, wi=wi, pm=pm: v.tensor_tensor(
                            uf[:, wi * WIN:(wi + 1) * WIN],
                            kf[:, wi * WIN:(wi + 1) * WIN],
                            wp[:, pm * WIN:(pm + 1) * WIN], ALU.mult), sEV)
                        p.wait("sync", sEV)
                        p.step("sync", lambda sy, c=c, wi=wi: sy.dma_start(
                            ap(self.S_state, c * P * T + wi * WIN, [[T, P], [1, WIN]]),
                            uf[:, wi * WIN:(wi + 1) * WIN]), sSP, 16)
        p.wait("sync", sSP)
        p.wait("sync", sRSP[0])
        p.wait("sync", sRSP[1])
        p.wait("sync", sEV)
        p.emit()
        self.close_stage(stk)

    def stage_C0(self):
        stk = ExitStack()
        sY = self.lsem(stk); sSQ = self.lsem(stk); sMM = self.lsem(stk)
        sRW = self.lsem(stk); fV = self.lsem(stk); sRC = self.lsem(stk)
        sLh = [self.lsem(stk), self.lsem(stk)]
        rt = self.lsbuf(stk, "c0_rt", [P, 2 * WIN], F32)
        st2 = self.lsbuf(stk, "c0_st", [P, 2 * WIN], F32)
        sqb = self.lsbuf(stk, "c0_sq", [P, 2 * WIN], F32)
        rowb = self.lsbuf(stk, "c0_row", [1, WIN], F32)
        statb = self.lsbuf(stk, "c0_stat", [P, NTT], F32)
        rs_b = self.lsbuf(stk, "c0_rsb", [P, WIN], F32)
        yt32 = self.lsbuf(stk, "c0_yt", [P, 2 * WIN], F32)
        pS = self.lpsum(stk, "c0_pS", [1, WIN], F32)

        p = self.prog()
        for wi in range(NW):
            for c in range(NCH):
                idx = wi * NCH + c
                pm = idx % 2
                o = pm * WIN
                p.wait("sync", sY, idx if C0_SER else idx - 1)
                p.step("sync", lambda sy, c=c, wi=wi, o=o, pm=pm: sy.dma_start(
                    ap(rt, o, [[2 * WIN, P], [1, WIN]]),
                    ap(self.S_r, c * P * T + wi * WIN, [[T, P], [1, WIN]])),
                    sLh[pm], 16)
                p.step("sync", lambda sy, c=c, wi=wi, o=o, pm=pm: sy.dma_start(
                    ap(st2, o, [[2 * WIN, P], [1, WIN]]),
                    ap(self.S_state, c * P * T + wi * WIN, [[T, P], [1, WIN]])),
                    sLh[pm], 16)
                p.wait("vector", sLh[pm])
                p.wait("vector", sSQ, idx if C0_SER else idx - 1)
                p.step("vector", lambda v, o=o: v.tensor_tensor(
                    ap(yt32, o, [[2 * WIN, P], [1, WIN]]),
                    ap(rt, o, [[2 * WIN, P], [1, WIN]]),
                    ap(st2, o, [[2 * WIN, P], [1, WIN]]), ALU.mult), sY)
                p.step("vector", lambda v, c=c, wi=wi, o=o: v.tensor_copy(
                    self.act[:, c * T + wi * WIN:c * T + (wi + 1) * WIN],
                    ap(yt32, o, [[2 * WIN, P], [1, WIN]])))
                p.wait("scalar", sY)
                p.wait("scalar", sMM, idx if C0_SER else idx - 1)
                p.step("scalar", lambda s, o=o: s.activation(
                    ap(sqb, o, [[2 * WIN, P], [1, WIN]]),
                    ap(yt32, o, [[2 * WIN, P], [1, WIN]]),
                    AF.Square), sSQ)
                p.wait("tensor", sSQ)
                if c == 0:
                    p.wait("tensor", sRC, wi)
                p.step("tensor", lambda t, c=c, o=o: t.matmul(
                    pS[0:1, :], self.ones_col[:, :],
                    ap(sqb, o, [[2 * WIN, P], [1, WIN]]),
                    start=(c == 0), stop=(c == NCH - 1)), sMM)
            p.wait("vector", sMM)
            p.wait("vector", sRW)
            p.step("vector", lambda v: v.tensor_copy(rowb[:, :], pS[0:1, :]), sRC)
            p.wait("sync", sRC)
            p.step("sync", lambda sy, wi=wi: sy.dma_start(
                ap(self.S_rows, ROW_RS * T + wi * WIN, [[WIN, 1], [1, WIN]]),
                rowb[:, :]), sRW, 16)
        p.wait("sync", sRW)
        p.emit()

        p = self.prog()
        sC = self.lsem(stk)
        p.step("sync", lambda sy: sy.dma_start(
            statb[:, :], ap(self.S_rows, ROW_RS * T, [[NTT, P], [1, NTT]])), sC, 16)
        p.wait("sync", sC)
        p.wait("vector", sC)
        p.fstep("vector", lambda v: v.tensor_scalar(
            statb[:, :], statb[:, :], 1.0 / D, None, ALU.mult), fV)
        tick = fV.n
        p.wait("scalar", fV, tick)
        p.step("scalar", lambda s: s.activation(statb[:, :], statb[:, :], AF.Sqrt,
                                                bias=self.cc(1e-6)), sC)
        p.wait("vector", sC)
        p.fstep("vector", lambda v: v.reciprocal(sqb[:, 0:NTT], statb[:, :]), fV)
        p.fstep("vector", lambda v: v.tensor_copy(statb[:, :], sqb[:, 0:NTT]), fV)
        p.wait("sync", fV)
        p.step("sync", lambda sy: sy.dma_start(
            ap(self.S_rows, ROW_RS * T, [[NTT, P], [1, NTT]]), statb[:, :]), sC, 16)
        p.wait("sync", sC)
        p.emit()

        p = self.prog()
        sC2 = self.lsem(stk); sU = self.lsem(stk)
        for wi in range(NW):
            p.wait("sync", sU, NCH * wi)
            p.step("sync", lambda sy, wi=wi: sy.dma_start(
                rs_b[:, :], ap(self.S_rows, ROW_RS * T + wi * WIN,
                               [[0, P], [1, WIN]])), sC2, 16)
            p.wait("vector", sC2)
            for c in range(NCH):
                p.step("vector", lambda v, c=c, wi=wi: v.tensor_tensor(
                    self.act[:, c * T + wi * WIN:c * T + (wi + 1) * WIN],
                    self.act[:, c * T + wi * WIN:c * T + (wi + 1) * WIN],
                    rs_b[:, :], ALU.mult), sU)
        p.emit()
        self.close_stage(stk)

    def stage_D0(self):
        stk = ExitStack()
        w = self.w
        sWL = self.lsem(stk); sWT = self.lsem(stk); sWE = self.lsem(stk)
        sMM = self.lsem(stk); sEV = self.lsem(stk); sXL = self.lsem(stk)
        sWF = self.lsem(stk)
        sSPh = [self.lsem(stk), self.lsem(stk)]
        lnw_b = self.lsbuf(stk, "d0_lnw", [P, D], F32)
        wraw = self.lsbuf(stk, "d0_wraw", [P, 1024], F32)
        lhsW = self.lsbuf(stk, "d0_lhs", [P, 1024], BF16)
        xt0 = self.lsbuf(stk, "d0_xt0", [P, 2 * WIN], F32)
        stg = self.lsbuf(stk, "d0_stg", [P, 2 * WIN], F32)
        pW = self.lpsum(stk, "d0_pW", [P, 1024], F32)
        pM = self.lpsum(stk, "d0_pM", [P, 2 * WIN], F32)

        p = self.prog()
        p.step("sync", lambda sy: sy.dma_start(
            lnw_b[:, :], ap(w["L0_attn_ln_w"], 0, [[0, P], [1, D]])), sWL, 16)
        for c in range(NCH):
            p.wait("sync", sWE)
            p.wait("sync", sMM)
            p.step("sync", lambda sy, c=c: sy.dma_start(
                wraw[:, :], ap(w["L0_attn_Wo"], c * P * D, [[D, P], [1, D]])), sWL, 16)
            p.wait("vector", sWL)
            p.step("vector", lambda v: v.tensor_tensor(
                wraw[:, :], wraw[:, :], lnw_b[:, :], ALU.mult), sWF)
            p.wait("tensor", sWF)
            p.wait("tensor", sWE)
            for cc in range(NCH):
                p.step("tensor", lambda t, cc=cc: t.transpose(
                    pW[:, cc * P:(cc + 1) * P], wraw[:, cc * P:(cc + 1) * P],
                    self.identF[:, :]), sWT if cc == NCH - 1 else None)
            p.wait("vector", sWT)
            p.step("vector", lambda v: v.tensor_copy(lhsW[:, 0:512], pW[:, 0:512]))
            p.step("vector", lambda v: v.tensor_copy(lhsW[:, 512:1024],
                                                     pW[:, 512:1024]), sWE)
            for wi in range(NW):
                pm = wi % 2
                o = pm * WIN
                p.wait("tensor", sWE)
                p.wait("tensor", sEV, sEV.n - 1)
                for cc in range(NCH):
                    p.step("tensor", lambda t, cc=cc, wi=wi, o=o: t.matmul(
                        ap(pM, o, [[2 * WIN, P], [1, WIN]]),
                        lhsW[:, cc * P:(cc + 1) * P],
                        self.act[:, cc * T + wi * WIN:cc * T + (wi + 1) * WIN],
                        start=(cc == 0), stop=(cc == NCH - 1)),
                        sMM if cc == NCH - 1 else None)
                p.wait("sync", sEV, sEV.n - 1)
                p.step("sync", lambda sy, c=c, wi=wi, o=o: sy.dma_start(
                    ap(xt0, o, [[2 * WIN, P], [1, WIN]]),
                    ap(self.S_xT0, c * P * T + wi * WIN, [[T, P], [1, WIN]])),
                    sXL, 16)
                p.wait("vector", sMM)
                p.wait("vector", sXL)
                p.wait("vector", sSPh[pm])
                p.step("vector", lambda v, c=c, o=o: v.scalar_tensor_tensor(
                    ap(stg, o, [[2 * WIN, P], [1, WIN]]),
                    ap(pM, o, [[2 * WIN, P], [1, WIN]]),
                    self.vc(self.C_B0, c),
                    ap(xt0, o, [[2 * WIN, P], [1, WIN]]), ALU.add, ALU.add), sEV)
                p.wait("sync", sEV)
                p.step("sync", lambda sy, c=c, wi=wi, o=o, pm=pm: sy.dma_start(
                    ap(self.S_x1a, c * P * T + wi * WIN, [[T, P], [1, WIN]]),
                    ap(stg, o, [[2 * WIN, P], [1, WIN]])), sSPh[pm], 16)
        p.wait("sync", sSPh[0])
        p.wait("sync", sSPh[1])
        p.emit()
        self.close_stage(stk)

    def stage_E0(self):
        stk = ExitStack()
        w = self.w
        sWL = self.lsem(stk); sWT = self.lsem(stk); sWE = self.lsem(stk)
        sMM = self.lsem(stk); sEV = self.lsem(stk)
        sQL = self.lsem(stk); sWF = self.lsem(stk)
        sXLh = [self.lsem(stk), self.lsem(stk)]
        sSPh = [self.lsem(stk), self.lsem(stk)]
        xqf = self.lsbuf(stk, "e0_xqf", [P, NCH * T], BF16)
        wq8 = self.lsbuf(stk, "e0_wq8", [P, 1024], I8)
        wqf = self.lsbuf(stk, "e0_wqf", [P, 1024], BF16)
        lhsQ = self.lsbuf(stk, "e0_lhsQ", [P, 1024], BF16)
        x1a = self.lsbuf(stk, "e0_x1a", [P, 2 * WIN], F32)
        xs_b = self.lsbuf(stk, "e0_xsb", [P, 2 * WIN], F32)
        stg = self.lsbuf(stk, "e0_stg", [P, 2 * WIN], F32)
        pW = self.lpsum(stk, "e0_pW", [P, 1024], BF16)
        pM = self.lpsum(stk, "e0_pM", [P, 2 * WIN], F32)

        p = self.prog()
        for c in range(NCH):
            p.step("sync", lambda sy, c=c: sy.dma_start(
                ap(xqf, c * T, [[NCH * T, P], [1, T]]),
                ap(self.S_xq0, c * P * T, [[T, P], [1, T]])), sQL, 16)
        p.wait("sync", sQL)
        for c in range(NCH):
            p.wait("sync", sWE)
            p.wait("sync", sMM)
            p.step("sync", lambda sy, c=c: sy.dma_start(
                wq8[:, :], ap(w["L0_ffn_weight_q"], c * P * D, [[D, P], [1, D]])),
                sWL, 16)
            p.wait("vector", sWL)
            p.step("vector", lambda v: v.tensor_copy(wqf[:, :], wq8[:, :]), sWF)
            p.wait("tensor", sWF)
            p.wait("tensor", sWE)
            for cc in range(NCH):
                p.step("tensor", lambda t, cc=cc: t.transpose(
                    pW[:, cc * P:(cc + 1) * P], wqf[:, cc * P:(cc + 1) * P],
                    self.identB[:, :]), sWT if cc == NCH - 1 else None)
            p.wait("vector", sWT)
            p.step("vector", lambda v: v.tensor_copy(lhsQ[:, :], pW[:, :]), sWE)
            for wi in range(NW):
                pm = wi % 2
                o = pm * WIN
                p.wait("tensor", sWE)
                p.wait("tensor", sEV, sEV.n - 1)
                for cc in range(NCH):
                    p.step("tensor", lambda t, cc=cc, wi=wi, o=o: t.matmul(
                        ap(pM, o, [[2 * WIN, P], [1, WIN]]),
                        lhsQ[:, cc * P:(cc + 1) * P],
                        ap(xqf, cc * T + wi * WIN, [[NCH * T, P], [1, WIN]]),
                        start=(cc == 0), stop=(cc == NCH - 1)),
                        sMM if cc == NCH - 1 else None)
                p.wait("sync", sEV, sEV.n - 1)
                p.step("sync", lambda sy, c=c, wi=wi, o=o: sy.dma_start(
                    ap(x1a, o, [[2 * WIN, P], [1, WIN]]),
                    ap(self.S_x1a, c * P * T + wi * WIN, [[T, P], [1, WIN]])),
                    sXLh[pm], 16)
                p.step("sync", lambda sy, wi=wi, o=o: sy.dma_start(
                    ap(xs_b, o, [[2 * WIN, P], [1, WIN]]),
                    ap(self.S_rows, ROW_XSC * T + wi * WIN, [[0, P], [1, WIN]])),
                    sXLh[pm], 16)
                p.wait("vector", sMM)
                p.wait("vector", sXLh[pm])
                p.wait("vector", sSPh[pm])
                p.step("vector", lambda v, c=c, o=o: v.scalar_tensor_tensor(
                    ap(stg, o, [[2 * WIN, P], [1, WIN]]),
                    ap(pM, o, [[2 * WIN, P], [1, WIN]]),
                    self.vc(self.C_WS, c),
                    ap(xs_b, o, [[2 * WIN, P], [1, WIN]]), ALU.mult, ALU.mult))
                p.step("vector", lambda v, o=o: v.tensor_tensor(
                    ap(stg, o, [[2 * WIN, P], [1, WIN]]),
                    ap(stg, o, [[2 * WIN, P], [1, WIN]]),
                    ap(x1a, o, [[2 * WIN, P], [1, WIN]]), ALU.add), sEV)
                p.wait("sync", sEV)
                p.step("sync", lambda sy, c=c, wi=wi, o=o: sy.dma_start(
                    ap(self.S_xT1, c * P * T + wi * WIN, [[T, P], [1, WIN]]),
                    ap(stg, o, [[2 * WIN, P], [1, WIN]])), sSPh[pm], 16)
        p.wait("sync", sSPh[0])
        p.wait("sync", sSPh[1])
        p.emit()
        self.close_stage(stk)

    def stage_A1(self):
        stk = ExitStack()
        R_SX, R_SXX, R_NMU, R_S = 3, 4, 5, 6
        xt = self.lsbuf(stk, "a1_xt", [P, 2 * WIN], F32)
        sqb = self.lsbuf(stk, "a1_sq", [P, 2 * WIN], F32)
        rowb = self.lsbuf(stk, "a1_row", [33, WIN], F32)
        statA = self.lsbuf(stk, "a1_stA", [P, NTT], F32)
        statB = self.lsbuf(stk, "a1_stB", [P, NTT], F32)
        mu_b = self.lsbuf(stk, "a1_mub", [P, WIN], F32)
        s_b = self.lsbuf(stk, "a1_sb", [P, WIN], F32)
        stg = self.lsbuf(stk, "a1_stg", [P, 2 * WIN], BF16)
        xp32 = self.lsbuf(stk, "a1_xp", [P, 2 * WIN], F32)
        pR = self.lpsum(stk, "a1_pR", [1, WIN], F32)
        pR2 = self.lpsum(stk, "a1_pR2", [1, WIN], F32)

        def stats_pass(src_from_act, row_sx, row_sxx):
            p = self.prog()
            sSQ = self.lsem(stk); sMM = self.lsem(stk)
            sRW = self.lsem(stk); sEV = self.lsem(stk); sRC = self.lsem(stk)
            sLh = [self.lsem(stk), self.lsem(stk)]
            for wi in range(NW):
                for c in range(NCH):
                    idx = wi * NCH + c
                    o = (idx % 2) * WIN
                    if src_from_act:
                        srcap = self.act[:, c * T + wi * WIN:c * T + (wi + 1) * WIN]
                    else:
                        p.wait("sync", sMM, idx - 1)
                        p.step("sync", lambda sy, c=c, wi=wi, o=o, idx=idx:
                               sy.dma_start(
                                   ap(xt, o, [[2 * WIN, P], [1, WIN]]),
                                   ap(self.S_xT1, c * P * T + wi * WIN,
                                      [[T, P], [1, WIN]])), sLh[idx % 2], 16)
                        srcap = ap(xt, o, [[2 * WIN, P], [1, WIN]])
                    p.wait("scalar", sLh[idx % 2])
                    p.wait("scalar", sMM, idx - 1)
                    p.step("scalar", lambda s, srcap=srcap, o=o: s.activation(
                        ap(sqb, o, [[2 * WIN, P], [1, WIN]]), srcap, AF.Square), sSQ)
                    p.wait("tensor", sSQ)
                    if c == 0:
                        p.wait("tensor", sRC, wi)
                    p.step("tensor", lambda t, c=c, srcap=srcap: t.matmul(
                        pR[0:1, :], self.ones_col[:, :], srcap,
                        start=(c == 0), stop=(c == NCH - 1)))
                    p.step("tensor", lambda t, c=c, o=o: t.matmul(
                        pR2[0:1, :], self.ones_col[:, :],
                        ap(sqb, o, [[2 * WIN, P], [1, WIN]]),
                        start=(c == 0), stop=(c == NCH - 1)), sMM)
                p.wait("vector", sMM)
                p.wait("vector", sRW)
                p.step("vector", lambda v: v.tensor_copy(rowb[0:1, :], pR[0:1, :]))
                p.step("vector", lambda v: v.tensor_copy(rowb[32:33, :], pR2[0:1, :]),
                       sRC)
                p.wait("sync", sRC)
                p.step("sync", lambda sy, wi=wi, row_sx=row_sx: sy.dma_start(
                    ap(self.S_rows, row_sx * T + wi * WIN, [[WIN, 1], [1, WIN]]),
                    rowb[0:1, :]), sRW, 16)
                p.step("sync", lambda sy, wi=wi, row_sxx=row_sxx: sy.dma_start(
                    ap(self.S_rows, row_sxx * T + wi * WIN, [[WIN, 1], [1, WIN]]),
                    rowb[32:33, :]), sRW, 16)
            p.wait("sync", sRW)
            p.emit()

        def row_math():
            p = self.prog()
            sC = self.lsem(stk); fV = self.lsem(stk)
            p.step("sync", lambda sy: sy.dma_start(
                statA[:, :], ap(self.S_rows, R_SX * T, [[NTT, P], [1, NTT]])), sC, 16)
            p.step("sync", lambda sy: sy.dma_start(
                statB[:, :], ap(self.S_rows, R_SXX * T, [[NTT, P], [1, NTT]])), sC, 16)
            p.wait("sync", sC)
            p.wait("vector", sC)
            p.fstep("vector", lambda v: v.tensor_scalar(
                statA[:, :], statA[:, :], 1.0 / D, None, ALU.mult), fV)
            p.fstep("vector", lambda v: v.tensor_scalar(
                statB[:, :], statB[:, :], 1.0 / D, None, ALU.mult), fV)
            p.fstep("vector", lambda v: v.scalar_tensor_tensor(
                sqb[:, 0:NTT], statA[:, :], 1.0, statA[:, :], ALU.mult, ALU.mult),
                fV)
            tick = p.fstep("vector", lambda v: v.tensor_tensor(
                statB[:, :], statB[:, :], sqb[:, 0:NTT], ALU.subtract), fV)
            p.wait("scalar", fV, tick)
            p.step("scalar", lambda s: s.activation(
                statB[:, :], statB[:, :], AF.Sqrt, bias=self.cc(1e-5)), sC)
            p.wait("vector", sC, sC.n)
            p.fstep("vector", lambda v: v.reciprocal(sqb[:, 0:NTT], statB[:, :]), fV)
            p.fstep("vector", lambda v: v.tensor_tensor(
                sqb[:, NTT:2 * NTT], statB[:, :], sqb[:, 0:NTT], ALU.mult), fV)
            p.fstep("vector", lambda v: v.tensor_scalar(
                sqb[:, NTT:2 * NTT], sqb[:, NTT:2 * NTT], -1.0, 2.0,
                ALU.mult, ALU.add), fV)
            p.fstep("vector", lambda v: v.tensor_tensor(
                sqb[:, 0:NTT], sqb[:, 0:NTT], sqb[:, NTT:2 * NTT], ALU.mult), fV)
            p.fstep("vector", lambda v: v.tensor_copy(statB[:, :], sqb[:, 0:NTT]), fV)
            p.fstep("vector", lambda v: v.tensor_scalar(
                statA[:, :], statA[:, :], -1.0, None, ALU.mult), fV)
            p.wait("sync", fV)
            p.step("sync", lambda sy: sy.dma_start(
                ap(self.S_rows, R_NMU * T, [[NTT, P], [1, NTT]]), statA[:, :]), sC, 16)
            p.step("sync", lambda sy: sy.dma_start(
                ap(self.S_rows, R_S * T, [[NTT, P], [1, NTT]]), statB[:, :]), sC, 16)
            p.wait("sync", sC)
            p.emit()

        stats_pass(False, R_SX, R_SXX)
        row_math()
        p = self.prog()
        sBC = self.lsem(stk); sSQ = self.lsem(stk); sMM = self.lsem(stk)
        sRW = self.lsem(stk); sEV = self.lsem(stk); sQT = self.lsem(stk)
        sRC = self.lsem(stk)
        sQDh = [self.lsem(stk), self.lsem(stk)]
        sLh = [self.lsem(stk), self.lsem(stk)]
        for wi in range(NW):
            p.wait("sync", sEV, sEV.n - 1)
            p.step("sync", lambda sy, wi=wi: sy.dma_start(
                mu_b[:, :], ap(self.S_rows, R_NMU * T + wi * WIN, [[0, P], [1, WIN]])),
                sBC, 16)
            p.step("sync", lambda sy, wi=wi: sy.dma_start(
                s_b[:, :], ap(self.S_rows, R_S * T + wi * WIN, [[0, P], [1, WIN]])),
                sBC, 16)
            for c in range(NCH):
                idx = wi * NCH + c
                o = (idx % 2) * WIN
                p.wait("sync", sMM, idx - 1)
                p.step("sync", lambda sy, c=c, wi=wi, o=o, idx=idx: sy.dma_start(
                    ap(xt, o, [[2 * WIN, P], [1, WIN]]),
                    ap(self.S_xT1, c * P * T + wi * WIN, [[T, P], [1, WIN]])),
                    sLh[idx % 2], 16)
                p.wait("vector", sLh[idx % 2])
                p.wait("vector", sBC)
                p.wait("vector", sSQ, idx - 1)
                p.wait("vector", sMM, idx - 1)
                av = self.act[:, c * T + wi * WIN:c * T + (wi + 1) * WIN]
                p.step("vector", lambda v, o=o: v.tensor_tensor(
                    ap(xp32, o, [[2 * WIN, P], [1, WIN]]),
                    ap(xt, o, [[2 * WIN, P], [1, WIN]]), mu_b[:, :], ALU.add))
                p.step("vector", lambda v, o=o: v.tensor_tensor(
                    ap(xp32, o, [[2 * WIN, P], [1, WIN]]),
                    ap(xp32, o, [[2 * WIN, P], [1, WIN]]), s_b[:, :], ALU.mult), sEV)
                p.step("vector", lambda v, av=av, o=o: v.tensor_copy(
                    av, ap(xp32, o, [[2 * WIN, P], [1, WIN]])))
                p.step("vector", lambda v, o=o: v.tensor_scalar(
                    ap(sqb, o, [[2 * WIN, P], [1, WIN]]),
                    ap(xp32, o, [[2 * WIN, P], [1, WIN]]), 16.0, -128.25,
                    ALU.mult, ALU.max))
                p.step("vector", lambda v, o=o: v.tensor_scalar(
                    ap(sqb, o, [[2 * WIN, P], [1, WIN]]),
                    ap(sqb, o, [[2 * WIN, P], [1, WIN]]), 127.25, R23,
                    ALU.min, ALU.add))
                p.wait("vector", sQDh[idx % 2])
                p.step("vector", lambda v, o=o: v.tensor_scalar(
                    ap(stg, o, [[2 * WIN, P], [1, WIN]]),
                    ap(sqb, o, [[2 * WIN, P], [1, WIN]]), R23, None,
                    ALU.subtract), sQT)
                p.wait("sync", sQT)
                p.step("sync", lambda sy, c=c, wi=wi, o=o, idx=idx: sy.dma_start(
                    ap(self.S_xq1, c * P * T + wi * WIN, [[T, P], [1, WIN]]),
                    ap(stg, o, [[2 * WIN, P], [1, WIN]])), sQDh[idx % 2], 16)
                p.wait("scalar", sEV)
                p.wait("scalar", sMM, idx - 1)
                p.step("scalar", lambda s, o=o: s.activation(
                    ap(xt, o, [[2 * WIN, P], [1, WIN]]),
                    ap(xp32, o, [[2 * WIN, P], [1, WIN]]), AF.Square), sSQ)
                p.wait("tensor", sSQ)
                if c == 0:
                    p.wait("tensor", sRC, wi)
                p.step("tensor", lambda t, c=c, o=o: t.matmul(
                    pR[0:1, :], self.ones_col[:, :],
                    ap(xp32, o, [[2 * WIN, P], [1, WIN]]),
                    start=(c == 0), stop=(c == NCH - 1)))
                p.step("tensor", lambda t, c=c, o=o: t.matmul(
                    pR2[0:1, :], self.ones_col[:, :],
                    ap(xt, o, [[2 * WIN, P], [1, WIN]]),
                    start=(c == 0), stop=(c == NCH - 1)), sMM)
            p.wait("vector", sMM)
            p.wait("vector", sRW)
            p.step("vector", lambda v: v.tensor_copy(rowb[0:1, :], pR[0:1, :]))
            p.step("vector", lambda v: v.tensor_copy(rowb[32:33, :], pR2[0:1, :]), sRC)
            p.wait("sync", sRC)
            p.step("sync", lambda sy, wi=wi: sy.dma_start(
                ap(self.S_rows, R_SX * T + wi * WIN, [[WIN, 1], [1, WIN]]),
                rowb[0:1, :]), sRW, 16)
            p.step("sync", lambda sy, wi=wi: sy.dma_start(
                ap(self.S_rows, R_SXX * T + wi * WIN, [[WIN, 1], [1, WIN]]),
                rowb[32:33, :]), sRW, 16)
        p.wait("sync", sRW)
        p.wait("sync", sQDh[0])
        p.wait("sync", sQDh[1])
        p.emit()
        row_math()
        p = self.prog()
        sBC = self.lsem(stk); sU = self.lsem(stk)
        for wi in range(NW):
            p.wait("sync", sU, NCH * wi)
            p.step("sync", lambda sy, wi=wi: sy.dma_start(
                mu_b[:, :], ap(self.S_rows, R_NMU * T + wi * WIN, [[0, P], [1, WIN]])),
                sBC, 16)
            p.step("sync", lambda sy, wi=wi: sy.dma_start(
                s_b[:, :], ap(self.S_rows, R_S * T + wi * WIN, [[0, P], [1, WIN]])),
                sBC, 16)
            p.wait("vector", sBC)
            for c in range(NCH):
                av = self.act[:, c * T + wi * WIN:c * T + (wi + 1) * WIN]
                p.step("vector", lambda v, av=av: v.tensor_tensor(
                    av, av, mu_b[:, :], ALU.add))
                p.step("vector", lambda v, av=av: v.tensor_tensor(
                    av, av, s_b[:, :], ALU.mult))
                p.step("vector", lambda v, av=av, c=c: v.tensor_scalar(
                    av, av, self.vc(self.C_G, c), self.vc(self.C_BE, c),
                    ALU.mult, ALU.add), sU)
        p.emit()
        self.close_stage(stk)

    def stage_B1(self):
        stk = ExitStack()
        w = self.w
        sMM = self.lsem(stk); sEV = self.lsem(stk)
        sCV = self.lsem(stk)
        sSPh = [self.lsem(stk), self.lsem(stk)]
        sWLh = [self.lsem(stk), self.lsem(stk)]
        lhsB = self.lsbuf(stk, "b1_lhs", [P, 2 * 1024], F32)
        lhsQ16 = self.lsbuf(stk, "b1_lhs16", [P, 2 * 1024], BF16)
        stg = self.lsbuf(stk, "b1_stg", [P, 2 * WIN], F32)
        pM = self.lpsum(stk, "b1_pM", [P, 2 * WIN], F32)
        DSTS = [self.S_q, self.S_k, self.S_v]

        p = self.prog()
        for co in range(24):
            lo = (co % 2) * 1024
            p.wait("sync", sCV, sCV.n - 1)
            p.step("sync", lambda sy, co=co, lo=lo: sy.dma_start(
                ap(lhsB, lo, [[2048, P], [P, NCH], [1, P]]),
                ap(w["L1_attn_w_qkv"], co * P, [[3 * D, P], [P * 3 * D, NCH], [1, P]])),
                sWLh[co % 2], 16)
            p.wait("vector", sWLh[co % 2])
            p.wait("vector", sMM, sMM.n - NW)
            p.step("vector", lambda v, lo=lo: v.tensor_copy(
                ap(lhsQ16, lo, [[2048, P], [1, 1024]]),
                ap(lhsB, lo, [[2048, P], [1, 1024]])), sCV)
            dst = DSTS[co // 8]
            cr = co % 8
            for wi in range(NW):
                o = (wi % 2) * WIN
                p.wait("tensor", sCV)
                p.wait("tensor", sEV, sEV.n - 1)
                for cc in range(NCH):
                    p.step("tensor", lambda t, cc=cc, wi=wi, o=o, lo=lo: t.matmul(
                        ap(pM, o, [[2 * WIN, P], [1, WIN]]),
                        ap(lhsQ16, lo + cc * P, [[2048, P], [1, P]]),
                        self.act[:, cc * T + wi * WIN:cc * T + (wi + 1) * WIN],
                        start=(cc == 0), stop=(cc == NCH - 1)),
                        sMM if cc == NCH - 1 else None)
                p.wait("vector", sMM)
                p.wait("vector", sSPh[wi % 2])
                p.step("vector", lambda v, o=o: v.tensor_copy(
                    ap(stg, o, [[2 * WIN, P], [1, WIN]]),
                    ap(pM, o, [[2 * WIN, P], [1, WIN]])), sEV)
                p.wait("sync", sEV)
                p.step("sync", lambda sy, dst=dst, cr=cr, wi=wi, o=o: sy.dma_start(
                    ap(dst, cr * P * T + wi * WIN, [[T, P], [1, WIN]]),
                    ap(stg, o, [[2 * WIN, P], [1, WIN]])), sSPh[wi % 2], 16)
        p.wait("sync", sSPh[0])
        p.wait("sync", sSPh[1])
        p.emit()
        self.close_stage(stk)

    def stage_INNER1(self):
        stk = ExitStack()
        bufs = {}
        for n, sh, dt in [
            ("qf", [P, 4096], F32), ("kf", [P, 4096], F32), ("vf", [P, 4096], F32),
            ("qr", [P, 4096], F32), ("kr", [P, 4096], F32),
            ("cosr", [P, 4096], F32), ("sinr", [P, 4096], F32),
            ("sktm", [P, 4096], F32), ("vtm", [P, 4096], F32),
            ("Ps", [P, WIN], F32), ("PT", [P, WIN], F32),
            ("wv", [P, 260], F32), ("memz2", [P, 1040], F32),
            ("dent", [P, 16], F32), ("stg", [P, WIN], BF16),
            ("nmx", [P, 8], F32), ("riv", [P, 8], F32),
        ]:
            bufs[n] = self.lsbuf(stk, "i1_" + n, sh, dt)
        for n in ["pb1", "pb2", "pb3"]:
            bufs[n] = self.lpsum(stk, "i1_" + n, [P, WIN], F32)
        for n in ["pb4", "pb5", "pb6", "pb7"]:
            bufs[n] = self.lpsum(stk, "i1_" + n, [P, 260], F32)

        for nm in ("sL", "sRP", "sEL", "sTP", "sTE", "sRA", "sMK", "sSD", "sSD2",
                   "sAE", "sDV", "fI", "sM1", "sM2", "sZ", "sOT"):
            bufs[nm] = self.lsem(stk)

        p = self.prog()
        memz2, wv = bufs["memz2"], bufs["wv"]
        p.step("vector", lambda v: v.memset(memz2[:, :], 0.0))
        p.step("vector", lambda v: v.memset(
            ap(memz2, 64, [[1040, 64], [130, 8], [1, 1]]), 1e-6))
        p.step("vector", lambda v: v.memset(
            ap(memz2, 64 * 1040 + 129, [[1040, 64], [130, 8], [1, 1]]), 1e-6))
        p.step("vector", lambda v: v.memset(wv[:, :], 0.0))
        p.step("vector", lambda v: v.memset(
            ap(wv, 64, [[260, P], [65, 4], [1, 1]]), -1.0))
        p.emit()

        for w in range(min(NW, I1_WINDOWS)):
            self.inner1_window(w, bufs)
        self.close_stage(stk)

    def inner1_window(self, w, B):
        qf, kf, vf = B["qf"], B["kf"], B["vf"]
        qr, kr = B["qr"], B["kr"]
        cosr, sinr = B["cosr"], B["sinr"]
        sktm, vtm = B["sktm"], B["vtm"]
        Ps, PT, wv, memz2 = B["Ps"], B["PT"], B["wv"], B["memz2"]
        dent, stg, nmx, riv = B["dent"], B["stg"], B["nmx"], B["riv"]
        pb1, pb2, pb3 = B["pb1"], B["pb2"], B["pb3"]
        pb4, pb5, pb6, pb7 = B["pb4"], B["pb5"], B["pb6"], B["pb7"]

        p = self.prog()
        (sL, sRP, sEL, sTP, sTE, sRA, sMK, sSD, sSD2, sAE, sDV, fI, sM1, sM2,
         sZ, sOT) = (B[k] for k in ("sL", "sRP", "sEL", "sTP", "sTE", "sRA",
                                    "sMK", "sSD", "sSD2", "sAE", "sDV", "fI",
                                    "sM1", "sM2", "sZ", "sOT"))
        bMK, bSD2, bDV, bZ = sMK.n, sSD2.n, sDV.n, sZ.n

        for c in range(NCH):
            for t, S in ((qf, self.S_q), (kf, self.S_k), (vf, self.S_v)):
                p.step("sync", lambda sy, t=t, S=S, c=c: sy.dma_start(
                    ap(t, c * WIN, [[4096, P], [1, WIN]]),
                    ap(S, c * P * T + w * WIN, [[T, P], [1, WIN]])), sL, 16)
        p.step("sync", lambda sy: sy.dma_start(
            cosr[:, :], ap(self.S_cos, w * WIN, [[T, P], [0, NCH], [1, WIN]])), sL, 16)
        p.step("sync", lambda sy: sy.dma_start(
            sinr[:, :], ap(self.S_sin, w * WIN, [[T, P], [0, NCH], [1, WIN]])), sL, 16)
        p.wait("sync", sL)
        for t, dstr in (((qf, qr), (kf, kr)) if I1_PHASE >= 1 else ()):
            p.step("sync", lambda sy, t=t: sy.dma_start(
                ap(sktm, 0, [[2 * 4096, 64], [1, 4096]]),
                ap(t, 4096, [[2 * 4096, 64], [1, 4096]])), sRP, 16)
            p.step("sync", lambda sy, t=t: sy.dma_start(
                ap(sktm, 4096, [[2 * 4096, 64], [1, 4096]]),
                ap(t, 0, [[2 * 4096, 64], [1, 4096]])), sRP, 16)
            p.wait("vector", sRP)
            p.step("vector", lambda v, t=t, dstr=dstr: v.tensor_tensor(
                dstr[:, :], t[:, :], cosr[:, :], ALU.mult))
            p.step("vector", lambda v: v.tensor_tensor(
                sktm[:, :], sktm[:, :], sinr[:, :], ALU.mult))
            p.step("vector", lambda v, dstr=dstr: v.tensor_tensor(
                dstr[:, :], dstr[:, :], sktm[:, :], ALU.add), sRP)
            p.wait("sync", sRP)
        for t in ((qf, kf) if I1_PHASE >= 2 else ()):
            p.step("vector", lambda v, t=t: v.tensor_scalar(
                vtm[:, :], t[:, :], 0.0, None, ALU.min), sEL)
            p.wait("scalar", sEL)
            p.step("scalar", lambda sc: sc.activation(vtm[:, :], vtm[:, :], AF.Exp),
                   sAE)
            p.wait("vector", sAE)
            p.step("vector", lambda v, t=t: v.scalar_tensor_tensor(
                t[:, :], t[:, :], 0.0, vtm[:, :], ALU.max, ALU.add), sEL)
        p.wait("tensor", sL)
        for src, dst in (((kf, sktm), (vf, vtm)) if I1_PHASE >= 3 else ()):
            for c in range(NCH):
                p.wait("tensor", sEL)
                p.wait("tensor", sTE)
                for tt in range(4):
                    p.step("tensor", lambda t, src=src, c=c, tt=tt: t.transpose(
                        pb1[:, tt * P:(tt + 1) * P],
                        ap(src, c * WIN + tt * P, [[4096, P], [1, P]]),
                        self.identF[:, :]), sTP if tt == 3 else None)
                p.wait("vector", sTP)
                p.step("vector", lambda v, dst=dst, c=c: v.tensor_copy(
                    ap(dst, c * P, [[4096, P], [1024, 4], [1, P]]),
                    ap(pb1, 0, [[WIN, P], [P, 4], [1, P]])), sTE)
        if I1_PHASE >= 3:
            p.step("vector", lambda v: v.tensor_scalar(
                sktm[:, :], sktm[:, :], -1.0, None, ALU.mult), sTE)
        p.wait("gpsimd", sTE)
        for (src, dst) in ((((sktm, sinr), (vtm, cosr))) if I1_PHASE >= 4 else ()):
            for j in range(4):
                for m in range(4):
                    h = 2 * j + (m % 2) + 8 * (m // 2)
                    for g in range(4):
                        p.step("gpsimd",
                               lambda gp, src=src, dst=dst, j=j, m=m, h=h, g=g:
                               gp.dma_start(
                                   ap(dst, j * 32 * 4096 + g * 256 + m * 64,
                                      [[4096, 32], [1024, 4], [1, 64]]),
                                   ap(src, g * 32 * 4096 + h * 64,
                                      [[4096, 32], [1024, 4], [1, 64]])), sRA, 16)
        for g in range(4 if I1_PHASE >= 5 else 0):
            for hq in range(4):
                u = g * 4 + hq
                if u >= I1_UNITS:
                    continue
                p.wait("tensor", sMK, bMK + u)
                ORD = (0, 2, 1, 3)
                for slot, hj in enumerate(ORD):
                    h = hq * 4 + hj
                    c = h // 2
                    base = (h % 2) * 64
                    if slot == 2:
                        p.wait("tensor", sSD)
                    p.step("tensor", lambda t, c=c, base=base, g=g, slot=slot:
                           t.matmul(
                               pb2[:, slot * P:(slot + 1) * P],
                               ap(qr, base * 4096 + c * WIN + g * P,
                                  [[4096, 64], [1, P]]),
                               ap(kr, base * 4096 + c * WIN + g * P,
                                  [[4096, 64], [1, P]]),
                               start=True, stop=True,
                               tile_position=(base, 0)), sSD)
                p.wait("vector", sSD)
                p.wait("vector", sTP)
                p.step("vector", lambda v: v.scalar_tensor_tensor(
                    Ps[:, :], pb2[:, :], SC_SDPA, self.mask4[:, :],
                    ALU.mult, ALU.add), sMK)
                p.fstep("vector", lambda v: v.tensor_reduce(
                    nmx[:, 0:4], ap(Ps, 0, [[WIN, P], [P, 4], [1, P]]),
                    AX, ALU.max), fI)
                p.fstep("vector", lambda v: v.tensor_scalar(
                    nmx[:, 0:4], nmx[:, 0:4], -1.0, None, ALU.mult), fI)
                p.wait("scalar", fI)
                for slot in range(4):
                    p.step("scalar", lambda sc, slot=slot: sc.activation(
                        Ps[:, slot * P:(slot + 1) * P], Ps[:, slot * P:(slot + 1) * P],
                        AF.Exp, bias=nmx[:, slot:slot + 1],
                        accum_out=riv[:, slot:slot + 1]), sAE)
                p.wait("vector", sAE)
                p.fstep("vector", lambda v: v.reciprocal(riv[:, 4:8], riv[:, 0:4]), fI)
                p.fstep("vector", lambda v: v.tensor_tensor(
                    nmx[:, 4:8], riv[:, 0:4], riv[:, 4:8], ALU.mult), fI)
                p.fstep("vector", lambda v: v.tensor_scalar(
                    nmx[:, 4:8], nmx[:, 4:8], -1.0, 2.0, ALU.mult, ALU.add), fI)
                p.fstep("vector", lambda v: v.tensor_tensor(
                    riv[:, 4:8], riv[:, 4:8], nmx[:, 4:8], ALU.mult), fI)
                p.wait("tensor", sAE)
                p.wait("tensor", sTE)
                for slot in range(4):
                    p.step("tensor", lambda t, slot=slot: t.transpose(
                        pb1[:, slot * P:(slot + 1) * P], Ps[:, slot * P:(slot + 1) * P],
                        self.identF[:, :]), sTP if slot == 3 else None)
                p.wait("vector", sTP)
                p.wait("vector", sSD2, bSD2 + 4 * u)
                p.step("vector", lambda v: v.tensor_copy(PT[:, :], pb1[:, :]), sTE)
                if w == 0 and u == 0 and "I1DBG" in DEBUG:
                    p.wait("sync", sTE)
                    p.step("sync", lambda sy: sy.dma_start(
                        ap(self.S_x2a, 0, [[T, P], [1, WIN]]), PT[:, :]), sL, 16)
                    p.step("sync", lambda sy: sy.dma_start(
                        ap(self.S_x2a, WIN, [[T, P], [1, WIN]]), Ps[:, :]), sL, 16)
                    p.step("sync", lambda sy: sy.dma_start(
                        ap(self.S_x2a, 2 * WIN, [[T, P], [1, 1024]]),
                        vtm[:, 0:1024]), sL, 16)
                    p.step("sync", lambda sy: sy.dma_start(
                        ap(self.S_x2a, 3 * WIN + 1024, [[T, P], [1, 512]]),
                        qr[:, 0:512]), sL, 16)
                    p.wait("sync", sL)
                p.wait("tensor", sTE)
                p.wait("tensor", sDV, bDV + 4 * u)
                for slot, hj in enumerate(ORD):
                    h = hq * 4 + hj
                    p.step("tensor", lambda t, h=h, slot=slot, g=g: t.matmul(
                        pb3[:, slot * 64:(slot + 1) * 64], PT[:, slot * P:(slot + 1) * P],
                        ap(vtm, g * 1024 + h * 64, [[4096, P], [1, 64]]),
                        start=True, stop=True), sSD2)
                p.wait("vector", sSD2)
                for slot, hj in enumerate(ORD):
                    h = hq * 4 + hj
                    p.step("vector", lambda v, h=h, slot=slot, g=g: v.tensor_scalar(
                        ap(vf, g * 1024 + h * 64, [[4096, P], [1, 64]]),
                        pb3[:, slot * 64:(slot + 1) * 64],
                        riv[:, 4 + slot:5 + slot], self.beta1m_r[:, h:h + 1],
                        ALU.mult, ALU.mult), sDV)
        if w == 0 and "I1DBG" in DEBUG:
            p.wait("sync", sDV)
            p.step("sync", lambda sy: sy.dma_start(
                ap(self.S_x2b, 0, [[T, P], [1, 4096]]), vf[:, :]), sL, 16)
            p.wait("sync", sL)
        p.wait("tensor", sRA)
        p.wait("vector", sRA)
        for sg in range(SPW if I1_PHASE >= 6 else 0):
            p.wait("tensor", sZ, bZ + sg)
            for pr in range(NCH):
                col = 130 * (pr // 4)
                strip = (pr % 4) * 32
                p.step("tensor", lambda t, pr=pr, sg=sg, col=col, strip=strip:
                       t.matmul(
                           ap(pb5, strip * 260 + col, [[260, 32], [1, 130]]),
                           ap(qf, pr * WIN + sg * 32, [[4096, P], [1, 32]]),
                           ap(memz2, pr * 130, [[1040, P], [1, 130]]),
                           start=True, stop=True, tile_position=(0, strip)), None)
                p.step("tensor", lambda t, pr=pr, sg=sg, col=col, strip=strip:
                       t.matmul(
                           ap(pb4, strip * 260 + col, [[260, 32], [1, 130]]),
                           ap(kf, pr * WIN + sg * 32, [[4096, P], [1, 32]]),
                           ap(memz2, pr * 130, [[1040, P], [1, 130]]),
                           start=True, stop=True, tile_position=(0, strip)),
                       sM1 if pr == NCH - 1 else None)
            p.wait("vector", sM1)
            p.fstep("vector", lambda v: v.reciprocal(
                dent[:, 0:4], ap(pb4, 64, [[260, P], [65, 4], [1, 1]])), fI)
            p.fstep("vector", lambda v: v.tensor_tensor(
                dent[:, 4:8], ap(pb4, 64, [[260, P], [65, 4], [1, 1]]),
                dent[:, 0:4], ALU.mult), fI)
            p.fstep("vector", lambda v: v.tensor_scalar(
                dent[:, 4:8], dent[:, 4:8], -1.0, 2.0, ALU.mult, ALU.add), fI)
            p.fstep("vector", lambda v: v.tensor_tensor(
                dent[:, 0:4], dent[:, 0:4], dent[:, 4:8], ALU.mult), fI)
            p.fstep("vector", lambda v: v.tensor_scalar(
                dent[:, 8:12], ap(pb5, 64, [[260, P], [65, 4], [1, 1]]),
                1e-6, None, ALU.add), fI)
            p.fstep("vector", lambda v: v.reciprocal(dent[:, 12:16], dent[:, 8:12]),
                    fI)
            p.fstep("vector", lambda v: v.tensor_tensor(
                dent[:, 4:8], dent[:, 8:12], dent[:, 12:16], ALU.mult), fI)
            p.fstep("vector", lambda v: v.tensor_scalar(
                dent[:, 4:8], dent[:, 4:8], -1.0, 2.0, ALU.mult, ALU.add), fI)
            p.fstep("vector", lambda v: v.tensor_tensor(
                dent[:, 12:16], dent[:, 12:16], dent[:, 4:8], ALU.mult), fI)
            p.fstep("vector", lambda v: v.tensor_tensor(
                dent[:, 12:16], dent[:, 12:16], self.beta_q2[:, :], ALU.mult), fI)
            for m in range(4):
                last = m == 3
                p.step("vector", lambda v, m=m, sg=sg: v.scalar_tensor_tensor(
                    wv[:, m * 65:m * 65 + 64],
                    ap(pb4, m * 65, [[260, P], [1, 64]]),
                    dent[:, m:m + 1],
                    ap(cosr, sg * 256 + m * 64, [[4096, P], [1, 64]]),
                    ALU.mult, ALU.subtract), fI)
            wv_tick = fI.n
            for m in range(4):
                p.step("vector", lambda v, m=m, sg=sg: v.tensor_scalar(
                    ap(kr, sg * 256 + m * 64, [[4096, P], [1, 64]]),
                    ap(pb5, m * 65, [[260, P], [1, 64]]),
                    dent[:, 12 + m:13 + m], None, ALU.mult), sDV)
            p.wait("tensor", fI, wv_tick)
            for j in range(4):
                if j > 0:
                    p.wait("tensor", sM2)
                for pr in (j, j + 4):
                    for e in range(2):
                        h = 2 * pr + e
                        m = 2 * (pr // 4) + e
                        pb = pb6 if pr < 4 else pb7
                        pcol = 65 * (pr % 4)
                        p.step("tensor", lambda t, pr=pr, e=e, j=j, m=m, sg=sg,
                               pb=pb, pcol=pcol: t.matmul(
                                   ap(pb, e * 64 * 260 + pcol, [[260, 64], [1, 65]]),
                                   ap(sinr, j * 32 * 4096 + sg * 256 + m * 64,
                                      [[4096, 32], [1, 64]]),
                                   ap(wv, j * 32 * 260 + m * 65, [[260, 32], [1, 65]]),
                                   start=True, stop=True,
                                   tile_position=(j * 32, e * 64)), sM2)
            p.wait("vector", sM2)
            for pb, base in ((pb6, 0), (pb7, 4)):
                for e in range(2):
                    p.step("vector", lambda v, pb=pb, base=base, e=e: v.tensor_tensor(
                        ap(memz2, e * 64 * 1040 + base * 130 + e * 65,
                           [[1040, 64], [130, 4], [1, 65]]),
                        ap(memz2, e * 64 * 1040 + base * 130 + e * 65,
                           [[1040, 64], [130, 4], [1, 65]]),
                        ap(pb, e * 64 * 260, [[260, 64], [65, 4], [1, 65]]),
                        ALU.add), sZ if (pb is pb7 and e == 1) else None)
        p.wait("gpsimd", sZ)
        for j in range(4 if I1_PHASE >= 7 else 0):
            for m in range(4):
                h = 2 * j + (m % 2) + 8 * (m // 2)
                for g in range(4):
                    p.step("gpsimd",
                           lambda gp, j=j, m=m, h=h, g=g: gp.dma_start(
                               ap(sktm, g * 32 * 4096 + h * 64,
                                  [[4096, 32], [1024, 4], [1, 64]]),
                               ap(kr, j * 32 * 4096 + g * 256 + m * 64,
                                  [[4096, 32], [1024, 4], [1, 64]])), sRA, 16)
        p.wait("vector", sRA)
        p.wait("vector", sDV)
        if I1_PHASE >= 7:
            p.step("vector", lambda v: v.tensor_tensor(
                vf[:, :], vf[:, :], sktm[:, :], ALU.add), sDV)
        p.wait("tensor", sDV)
        for c in range(NCH if I1_PHASE >= 5.5 else 0):
            p.wait("tensor", sTE)
            for tt in range(4):
                p.step("tensor", lambda t, c=c, tt=tt: t.transpose(
                    pb1[:, tt * P:(tt + 1) * P],
                    ap(vf, tt * 1024 + c * P, [[4096, P], [1, P]]),
                    self.identF[:, :]), sTP if tt == 3 else None)
            p.wait("vector", sTP)
            p.wait("vector", sOT)
            p.step("vector", lambda v: v.tensor_copy(stg[:, :], pb1[:, :]), sTE)
            p.wait("sync", sTE)
            p.step("sync", lambda sy, c=c: sy.dma_start(
                ap(self.S_outsT, c * P * T + w * WIN, [[T, P], [1, WIN]]),
                stg[:, :]), sOT, 16)
        p.wait("sync", sOT)
        p.wait("sync", sRA)
        p.emit()

    def stage_OUT1(self):
        stk = ExitStack()
        w = self.w
        sMM = self.lsem(stk); sEV = self.lsem(stk)
        sXL = self.lsem(stk); sAL = self.lsem(stk); sCV = self.lsem(stk)
        sSPh = [self.lsem(stk), self.lsem(stk)]
        sWLh = [self.lsem(stk), self.lsem(stk)]
        af = self.lsbuf(stk, "o1_af", [P, NCH * T], BF16)
        lhsO = self.lsbuf(stk, "o1_lhs", [P, 2 * 1024], F32)
        lhsO16 = self.lsbuf(stk, "o1_lhs16", [P, 2 * 1024], BF16)
        x1t = self.lsbuf(stk, "o1_x1t", [P, 2 * WIN], F32)
        stg = self.lsbuf(stk, "o1_stg", [P, 2 * WIN], F32)
        pM = self.lpsum(stk, "o1_pM", [P, 2 * WIN], F32)

        p = self.prog()
        for c in range(NCH):
            p.step("sync", lambda sy, c=c: sy.dma_start(
                ap(af, c * T, [[NCH * T, P], [1, T]]),
                ap(self.S_outsT, c * P * T, [[T, P], [1, T]])), sAL, 16)
        p.wait("sync", sAL)
        for co in range(NCH):
            lo = (co % 2) * 1024
            p.wait("sync", sCV, sCV.n - 1)
            p.step("sync", lambda sy, co=co, lo=lo: sy.dma_start(
                ap(lhsO, lo, [[2048, P], [P, NCH], [1, P]]),
                ap(w["L1_attn_w_out"], co * P, [[D, P], [P * D, NCH], [1, P]])),
                sWLh[co % 2], 16)
            p.wait("vector", sWLh[co % 2])
            p.wait("vector", sMM, sMM.n - NW)
            p.step("vector", lambda v, lo=lo: v.tensor_copy(
                ap(lhsO16, lo, [[2048, P], [1, 1024]]),
                ap(lhsO, lo, [[2048, P], [1, 1024]])), sCV)
            for wi in range(NW):
                o = (wi % 2) * WIN
                p.wait("tensor", sCV)
                p.wait("tensor", sEV, sEV.n - 1)
                for cc in range(NCH):
                    p.step("tensor", lambda t, cc=cc, wi=wi, o=o, lo=lo: t.matmul(
                        ap(pM, o, [[2 * WIN, P], [1, WIN]]),
                        ap(lhsO16, lo + cc * P, [[2048, P], [1, P]]),
                        af[:, cc * T + wi * WIN:cc * T + (wi + 1) * WIN],
                        start=(cc == 0), stop=(cc == NCH - 1)),
                        sMM if cc == NCH - 1 else None)
                p.wait("sync", sEV, sEV.n - 1)
                p.step("sync", lambda sy, co=co, wi=wi, o=o: sy.dma_start(
                    ap(x1t, o, [[2 * WIN, P], [1, WIN]]),
                    ap(self.S_xT1, co * P * T + wi * WIN, [[T, P], [1, WIN]])),
                    sXL, 16)
                p.wait("vector", sMM)
                p.wait("vector", sXL)
                p.wait("vector", sSPh[wi % 2])
                p.step("vector", lambda v, co=co, o=o: v.scalar_tensor_tensor(
                    ap(stg, o, [[2 * WIN, P], [1, WIN]]),
                    ap(pM, o, [[2 * WIN, P], [1, WIN]]),
                    self.vc(self.C_B2Q, co),
                    ap(x1t, o, [[2 * WIN, P], [1, WIN]]), ALU.add, ALU.add), sEV)
                p.wait("sync", sEV)
                p.step("sync", lambda sy, co=co, wi=wi, o=o: sy.dma_start(
                    ap(self.S_x2a, co * P * T + wi * WIN, [[T, P], [1, WIN]]),
                    ap(stg, o, [[2 * WIN, P], [1, WIN]])), sSPh[wi % 2], 16)
        p.wait("sync", sSPh[0])
        p.wait("sync", sSPh[1])
        p.emit()
        self.close_stage(stk)

    def stage_F1(self):
        stk = ExitStack()
        w = self.w
        xqf = self.lsbuf(stk, "f1_xqf", [P, NCH * T], BF16)
        hh = self.lsbuf(stk, "f1_hh", [P, NCH * T], BF16)
        wraw = self.lsbuf(stk, "f1_wraw", [P, 1024], F32)
        lhsQ = self.lsbuf(stk, "f1_lhs", [P, 1024], BF16)
        xpt = self.lsbuf(stk, "f1_xpt", [P, 2 * WIN], F32)
        et = self.lsbuf(stk, "f1_et", [P, 2 * WIN], F32)
        evq = self.lsbuf(stk, "f1_evq", [P, 2 * WIN], F32)
        stg = self.lsbuf(stk, "f1_stg", [P, 2 * WIN], F32)
        pM = self.lpsum(stk, "f1_pM", [P, 2 * WIN], F32)
        pW = self.lpsum(stk, "f1_pW", [P, 1024], F32)
        pWB = self.lpsum(stk, "f1_pWB", [P, 1024], BF16)
        pT = self.lpsum(stk, "f1_pT", [P, WIN], F32)
        sQL = self.lsem(stk); sWL = self.lsem(stk); sWF = self.lsem(stk); sWT = self.lsem(stk)
        sWE = self.lsem(stk); sMM = self.lsem(stk); sAE = self.lsem(stk); sHW = self.lsem(stk)
        sXL = self.lsem(stk); sEV = self.lsem(stk); sTP = self.lsem(stk)
        sFC = self.lsem(stk); sAQ = self.lsem(stk)
        sSPh = [self.lsem(stk), self.lsem(stk)]
        sODh = [self.lsem(stk), self.lsem(stk)]

        p = self.prog()
        for c in range(NCH):
            p.step("sync", lambda sy, c=c: sy.dma_start(
                ap(xqf, c * T, [[NCH * T, P], [1, T]]),
                ap(self.S_xq1, c * P * T, [[T, P], [1, T]])), sQL, 16)
        p.wait("sync", sQL)
        for q in range(F1_QUARTERS):
            src_prev = self.S_x2a if q % 2 == 0 else self.S_x2b
            dst_part = self.S_x2b if q % 2 == 0 else self.S_x2a
            final = q == 3
            for ho in range(NCH if F1_W1 else 0):
                hc = q * NCH + ho
                p.wait("sync", sWE)
                p.wait("sync", sMM)
                p.step("sync", lambda sy, hc=hc: sy.dma_start(
                    wraw[:, :], ap(w["L1_ffn_w1"], hc * P * D, [[D, P], [1, D]])),
                    sWL, 16)
                p.wait("vector", sWL)
                p.step("vector", lambda v: v.tensor_scalar(
                    wraw[:, :], wraw[:, :], 64.0, -128.25, ALU.mult, ALU.max))
                p.step("vector", lambda v: v.tensor_scalar(
                    wraw[:, :], wraw[:, :], 127.25, R23, ALU.min, ALU.add))
                p.step("vector", lambda v: v.tensor_scalar(
                    wraw[:, :], wraw[:, :], R23, None, ALU.subtract), sWF)
                p.wait("tensor", sWF)
                p.wait("tensor", sWE)
                for cc in range(NCH):
                    p.step("tensor", lambda t, cc=cc: t.transpose(
                        pW[:, cc * P:(cc + 1) * P], wraw[:, cc * P:(cc + 1) * P],
                        self.identF[:, :]), sWT if cc == NCH - 1 else None)
                p.wait("vector", sWT)
                p.step("vector", lambda v: v.tensor_copy(lhsQ[:, 0:512], pW[:, 0:512]))
                p.step("vector", lambda v: v.tensor_copy(
                    lhsQ[:, 512:1024], pW[:, 512:1024]), sWE)
                for wi in range(NW):
                    o = (wi % 2) * WIN
                    p.wait("tensor", sWE)
                    p.wait("tensor", sHW, sHW.n - 1)
                    for cc in range(NCH):
                        p.step("tensor", lambda t, cc=cc, wi=wi, o=o: t.matmul(
                            ap(pM, o, [[2 * WIN, P], [1, WIN]]),
                            lhsQ[:, cc * P:(cc + 1) * P],
                            ap(xqf, cc * T + wi * WIN, [[NCH * T, P], [1, WIN]]),
                            start=(cc == 0), stop=(cc == NCH - 1)),
                            sMM if cc == NCH - 1 else None)
                    p.wait("scalar", sMM)
                    p.step("scalar", lambda sc, hc=hc, o=o: sc.activation(
                        ap(et, o, [[2 * WIN, P], [1, WIN]]),
                        ap(pM, o, [[2 * WIN, P], [1, WIN]]), AF.Relu,
                        bias=self.vc(self.C_B1Q, hc), scale=self.cc(1.0 / 64.0)),
                        sAE)
                    p.wait("vector", sAE)
                    p.step("vector", lambda v, o=o: v.tensor_scalar(
                        ap(et, o, [[2 * WIN, P], [1, WIN]]),
                        ap(et, o, [[2 * WIN, P], [1, WIN]]), 127.25, R23,
                        ALU.min, ALU.add))
                    p.step("vector", lambda v, ho=ho, wi=wi, o=o: v.tensor_scalar(
                        ap(hh, ho * T + wi * WIN, [[NCH * T, P], [1, WIN]]),
                        ap(et, o, [[2 * WIN, P], [1, WIN]]), R23, None,
                        ALU.subtract), sHW)
            for co in range(F1_W2 if q == 0 else NCH):
                p.wait("sync", sWE)
                p.wait("sync", sMM)
                p.step("sync", lambda sy, co=co, q=q: sy.dma_start(
                    wraw[:, :], ap(w["L1_ffn_w2"], co * P * 4 * D + q * 1024,
                                   [[4 * D, P], [1, 1024]])), sWL, 16)
                p.wait("vector", sWL)
                p.step("vector", lambda v: v.tensor_scalar(
                    wraw[:, :], wraw[:, :], 64.0, -128.25, ALU.mult, ALU.max))
                p.step("vector", lambda v: v.tensor_scalar(
                    wraw[:, :], wraw[:, :], 127.25, R23, ALU.min, ALU.add))
                p.step("vector", lambda v: v.tensor_scalar(
                    wraw[:, :], wraw[:, :], R23, None, ALU.subtract), sWF)
                p.wait("tensor", sWF)
                p.wait("tensor", sWE)
                for cc in range(NCH):
                    p.step("tensor", lambda t, cc=cc: t.transpose(
                        pW[:, cc * P:(cc + 1) * P], wraw[:, cc * P:(cc + 1) * P],
                        self.identF[:, :]), sWT if cc == NCH - 1 else None)
                p.wait("vector", sWT)
                p.step("vector", lambda v: v.tensor_copy(lhsQ[:, 0:512], pW[:, 0:512]))
                p.step("vector", lambda v: v.tensor_copy(
                    lhsQ[:, 512:1024], pW[:, 512:1024]), sWE)
                for wi in range(F1_WIN):
                    o = 0 if F1_BIS & 8 else (wi % 2) * WIN
                    opm = 0 if F1_BIS & 16 else o
                    p.wait("tensor", sWE)
                    p.wait("tensor", sAQ, sAQ.n - 1)
                    for hc in range(NCH):
                        p.step("tensor", lambda t, hc=hc, wi=wi, opm=opm: t.matmul(
                            ap(pM, opm, [[2 * WIN, P], [1, WIN]]),
                            lhsQ[:, hc * P:(hc + 1) * P],
                            ap(hh, hc * T + wi * WIN, [[NCH * T, P], [1, WIN]]),
                            start=(hc == 0), stop=(hc == NCH - 1)),
                            sMM if hc == NCH - 1 else None)
                    p.wait("sync", sEV, sEV.n - 1)
                    p.step("sync", lambda sy, co=co, wi=wi, o=o, src=src_prev:
                           sy.dma_start(
                               ap(xpt, o, [[2 * WIN, P], [1, WIN]]),
                               ap(src, co * P * T + wi * WIN, [[T, P], [1, WIN]])),
                           sXL, 16)
                    p.wait("scalar", sMM)
                    p.wait("scalar", sEV, sEV.n - 1)
                    p.step("scalar", lambda sc, o=o, opm=opm: sc.activation(
                        ap(evq, o, [[2 * WIN, P], [1, WIN]]),
                        ap(pM, opm, [[2 * WIN, P], [1, WIN]]), AF.Copy,
                        bias=0.0, scale=self.cc(1.0 / 1024.0)), sAQ)
                    p.wait("vector", sAQ)
                    p.wait("vector", sXL)
                    if final:
                        p.wait("vector", sTP)
                    else:
                        p.wait("vector", sSPh[wi % 2])
                    p.step("vector", lambda v, o=o: v.tensor_tensor(
                        ap(stg, o, [[2 * WIN, P], [1, WIN]]),
                        ap(evq, o, [[2 * WIN, P], [1, WIN]]),
                        ap(xpt, o, [[2 * WIN, P], [1, WIN]]), ALU.add), sEV)
                    if not final:
                        p.wait("sync", sEV)
                        p.step("sync", lambda sy, co=co, wi=wi, o=o, dst=dst_part:
                               sy.dma_start(
                                   ap(dst, co * P * T + wi * WIN, [[T, P], [1, WIN]]),
                                   ap(stg, o, [[2 * WIN, P], [1, WIN]])), sSPh[wi % 2], 16)
                    else:
                        p.wait("tensor", sEV)
                        p.wait("tensor", sFC, sFC.n - 1)
                        for tt in range(4):
                            p.step("tensor", lambda t, tt=tt, o=o: t.transpose(
                                pT[:, tt * P:(tt + 1) * P],
                                ap(stg, o + tt * P, [[2 * WIN, P], [1, P]]),
                                self.identF[:, :]), sTP if tt == 3 else None)
                        p.wait("vector", sTP)
                        p.wait("vector", sODh[wi % 2])
                        p.step("vector", lambda v, o=o: v.tensor_copy(
                            ap(et, o, [[2 * WIN, P], [1, WIN]]), pT[:, :]), sFC)
                        p.wait("sync", sFC)
                        p.step("sync", lambda sy, co=co, wi=wi, o=o: sy.dma_start(
                            ap(self.out, wi * WIN * D + co * P,
                               [[D, P], [P * D, 4], [1, P]]),
                            ap(et, o, [[2 * WIN, P], [P, 4], [1, P]])), sODh[wi % 2], 16)
        p.wait("sync", sODh[0])
        p.wait("sync", sODh[1])
        p.wait("sync", sSPh[0])
        p.wait("sync", sSPh[1])
        p.emit()
        self.close_stage(stk)

    def stage_PROBE(self):
        stk = ExitStack()
        w = self.w
        if PR_MODE & 4:
            xqf = self.lsbuf(stk, "pr_xqf", [P, NCH * T], BF16)
        hh = self.lsbuf(stk, "pr_hh", [P, NCH * T], BF16)
        wraw = self.lsbuf(stk, "pr_wraw", [P, 1024], F32)
        lhsQ = self.lsbuf(stk, "pr_lhsQ", [P, 1024], BF16)
        xpt = self.lsbuf(stk, "pr_xpt", [P, 2 * WIN], F32)
        evq = self.lsbuf(stk, "pr_evq", [P, 2 * WIN], F32)
        stg = self.lsbuf(stk, "pr_stg", [P, 2 * WIN], F32)
        if PR_MODE & 8:
            pM = self.lpsum(stk, "pr_pM", [P, 2 * WIN], F32)
            pW = self.lpsum(stk, "pr_pW", [P, 1024], F32)
            pWB = self.lpsum(stk, "pr_pWB", [P, 1024], BF16)
            pT = self.lpsum(stk, "pr_pT", [P, WIN], F32)
        else:
            pW = self.lpsum(stk, "pr_pW", [P, 1024], F32)
            pM = self.lpsum(stk, "pr_pM", [P, 2 * WIN], F32)
        sWT = self.lsem(stk); sWE = self.lsem(stk); sMM = self.lsem(stk)
        sAQ = self.lsem(stk); sEV = self.lsem(stk); sSP = self.lsem(stk)
        sIN = self.lsem(stk); sWL = self.lsem(stk); sWF = self.lsem(stk)
        sXL = self.lsem(stk); sQL = self.lsem(stk)
        p = self.prog()
        if PR_MODE & 4:
            for c in range(NCH):
                p.step("sync", lambda sy, c=c: sy.dma_start(
                    ap(xqf, c * T, [[NCH * T, P], [1, T]]),
                    ap(self.S_xq1, c * P * T, [[T, P], [1, T]])), sQL, 16)
            p.wait("sync", sQL)
        if PR_MODE & 2:
            p.step("sync", lambda sy: sy.dma_start(
                wraw[:, :], ap(w["L1_ffn_w2"], 0, [[4 * D, P], [1, 1024]])),
                sWL, 16)
            p.wait("vector", sWL)
            p.step("vector", lambda v: v.tensor_scalar(
                wraw[:, :], wraw[:, :], 64.0, -128.25, ALU.mult, ALU.max))
            p.step("vector", lambda v: v.tensor_scalar(
                wraw[:, :], wraw[:, :], 127.25, R23, ALU.min, ALU.add))
            p.step("vector", lambda v: v.tensor_scalar(
                wraw[:, :], wraw[:, :], R23, None, ALU.subtract), sWF)
            p.wait("tensor", sWF)
        else:
            p.step("vector", lambda v: v.memset(wraw[:, :], 0.125))
        p.step("vector", lambda v: v.memset(xpt[:, :], 0.25))
        for c in range(NCH):
            p.step("vector", lambda v, c=c: v.memset(
                hh[:, c * T:(c + 1) * T], 0.5), sIN if c == NCH - 1 else None)
        p.wait("tensor", sIN)
        for cc in range(NCH):
            p.step("tensor", lambda t, cc=cc: t.transpose(
                pW[:, cc * P:(cc + 1) * P], wraw[:, cc * P:(cc + 1) * P],
                self.identF[:, :]), sWT if cc == NCH - 1 else None)
        p.wait("vector", sWT)
        p.step("vector", lambda v: v.tensor_copy(lhsQ[:, 0:512], pW[:, 0:512]))
        p.step("vector", lambda v: v.tensor_copy(
            lhsQ[:, 512:1024], pW[:, 512:1024]), sWE)
        for wi in range(NW):
            o = (wi % 2) * WIN
            p.wait("tensor", sWE)
            p.wait("tensor", sAQ, sAQ.n - 1)
            for cc in range(NCH):
                p.step("tensor", lambda t, cc=cc, wi=wi, o=o: t.matmul(
                    ap(pM, o, [[2 * WIN, P], [1, WIN]]),
                    lhsQ[:, cc * P:(cc + 1) * P],
                    ap(hh, cc * T + wi * WIN, [[NCH * T, P], [1, WIN]]),
                    start=(cc == 0), stop=(cc == NCH - 1)),
                    sMM if cc == NCH - 1 else None)
            if PR_MODE & 1:
                p.wait("sync", sEV, sEV.n - 1)
                p.step("sync", lambda sy, wi=wi, o=o: sy.dma_start(
                    ap(xpt, o, [[2 * WIN, P], [1, WIN]]),
                    ap(self.S_x2a, wi * WIN, [[T, P], [1, WIN]])), sXL, 16)
            p.wait("scalar", sMM)
            p.wait("scalar", sEV, sEV.n - 1)
            p.step("scalar", lambda sc, o=o: sc.activation(
                ap(evq, o, [[2 * WIN, P], [1, WIN]]),
                ap(pM, o, [[2 * WIN, P], [1, WIN]]), AF.Copy,
                bias=0.0, scale=self.cc(1.0 / 1024.0)), sAQ)
            p.wait("vector", sAQ)
            if PR_MODE & 1:
                p.wait("vector", sXL)
            p.wait("vector", sSP, sSP.n - 16)
            p.step("vector", lambda v, o=o: v.tensor_tensor(
                ap(stg, o, [[2 * WIN, P], [1, WIN]]),
                ap(evq, o, [[2 * WIN, P], [1, WIN]]),
                ap(xpt, o, [[2 * WIN, P], [1, WIN]]), ALU.add), sEV)
            p.wait("sync", sEV)
            p.step("sync", lambda sy, wi=wi, o=o: sy.dma_start(
                ap(self.S_x2b, wi * WIN, [[T, P], [1, WIN]]),
                ap(stg, o, [[2 * WIN, P], [1, WIN]])), sSP, 16)
        p.wait("sync", sSP)
        p.emit()
        self.close_stage(stk)

    def dump_act(self):
        s = self.sem()
        p = self.prog()
        for c in range(NCH):
            p.step("sync", lambda sy, c=c: sy.dma_start(
                ap(self.S_act, c * P * T, [[T, P], [1, T]]),
                self.act[:, c * T:(c + 1) * T]), s, 16)
        p.wait("sync", s)
        p.emit()


def build_program():
    return K().build()


def kernel(**inputs):
    nc = build_program()
    x = np.ascontiguousarray(np.asarray(inputs["L0_attn_x"], dtype=np.float32))
    base = {}
    for kk, v in inputs.items():
        if kk == "L0_attn_x":
            continue
        a = np.asarray(v)
        if kk == "L1_attn_mem_beta":
            a = a.reshape(HEADS)
        base[kk] = np.ascontiguousarray(a)
    in_maps = []
    for bb in range(8):
        m = dict(base)
        m["x"] = x[bb]
        in_maps.append(m)
    res = run_bass_kernel_spmd(nc, in_maps, core_ids=list(range(8)))
    return np.stack([r["out"] for r in res.results], axis=0)
